# version 22
# baseline (speedup 1.0000x reference)
import math
import numpy as np
import ml_dtypes
import concourse.bass as bass
import concourse.mybir as mybir
from concourse.bass_utils import run_bass_kernel_spmd

F32 = mybir.dt.float32
BF16 = mybir.dt.bfloat16
AF = mybir.ActivationFunctionType
ALU = mybir.AluOpType
AX = mybir.AxisListType

S, D, E, I = 16384, 1024, 128, 2816
NCORE = 8
NU = 16
ROWS = NU * 128
NIC = I // 128
NDC = D // 128

A_ERF = math.sqrt(2.0 / math.pi)
S_IMM = A_ERF / S
MU_A = math.sqrt(1.0 / math.pi)
SQ128A = math.sqrt(128.0) * A_ERF
C32 = float(1.5 * 2 ** 23)
C52 = float(np.float32(1.5 * 2 ** 52))
EPS = 1e-5

KPAY = 128 * ROWS
VPAY = ROWS * D
PAY = KPAY + VPAY
DBG = False


def _ones33():
    o = np.zeros((33, 128), dtype=np.float32)
    o[0] = 1.0
    o[32] = 1.0
    return o


def _bucket_table():
    NUM_BUCKETS, MAX_DISTANCE = 32, 128
    n = np.arange(256)
    max_exact = NUM_BUCKETS // 2
    is_small = n < max_exact
    nf = np.maximum(n, 1).astype(np.float32)
    val_if_large = max_exact + (
        np.log(nf / max_exact) / math.log(MAX_DISTANCE / max_exact) * (NUM_BUCKETS - max_exact)
    ).astype(np.int32)
    val_if_large = np.minimum(val_if_large, NUM_BUCKETS - 1)
    return np.where(is_small, n, val_if_large)


def host_constants():
    buck = _bucket_table()
    onehot = np.zeros((32, 384), dtype=np.float32)
    for d in range(256):
        onehot[buck[d], 128 + d] = SQ128A
    ident = np.eye(128, dtype=np.float32)
    return {"onehot": onehot, "identc": ident, "revc": ident[::-1].copy(),
            "ones1c": _ones33()}


def percore_tables(c):
    zsel = np.zeros((128, 27), dtype=np.float32)
    hmask = np.zeros((128, 9 * 128), dtype=np.float32)
    tri = np.triu(np.ones((128, 128), dtype=np.float32))
    for p in range(9):
        if p < c:
            zsel[:, 3 * p + 0] = 1.0
            hmask[:, p * 128:(p + 1) * 128] = 0.5
        elif p == c:
            zsel[:, 3 * p + 1] = 1.0
            hmask[:, p * 128:(p + 1) * 128] = 0.5
        elif p == c + 1:
            zsel[:, 3 * p + 2] = 1.0
            hmask[:, p * 128:(p + 1) * 128] = 0.5 * tri
        else:
            zsel[:, 3 * p + 0] = 1.0
    return zsel, hmask


def attn_schedule():
    sched = []
    for g2 in range(8):
        for u in range(2 * g2, 16):
            if u == 2 * g2:
                items = [("zone", 0, 1, 4), ("zone", 4, 5, 4)]
            elif u == 2 * g2 + 1:
                items = [("far", 0, None, 4), ("far", 4, None, 3),
                         ("zone", 7, 0, 4), ("zone", 11, 4, 4), ("zone", 15, 8, 1)]
            elif u == 2 * g2 + 2:
                items = [("far", 0, None, 4), ("far", 4, None, 4), ("far", 8, None, 4),
                         ("far", 12, None, 3), ("zone", 15, 0, 1)]
            else:
                items = [("far", 0, None, 4), ("far", 4, None, 4),
                         ("far", 8, None, 4), ("far", 12, None, 4)]
            sched.append((g2, u, items))
    return sched


class Orch:
    ENG = ("sync", "tensor", "vector", "scalar", "gpsimd")

    def __init__(self):
        self.q = {e: [] for e in self.ENG}

    def on(self, eng, fn):
        self.q[eng].append(fn)

    def emit(self, blk):
        for e in self.ENG:
            fns = self.q[e]

            def body(eng, fns=fns):
                for f in fns:
                    f(eng)
            getattr(blk, e)(body)


def build_program(stage="all"):
    nc = bass.Bass("TRN2", target_bir_lowering=False, detect_race_conditions=False)

    DIN = {}
    for name, shape in [
        ("x_loc", [ROWS, D]), ("wqk", [D, E]), ("bqk", [E, 1]), ("g0", [E, 1]),
        ("b0", [E, 1]), ("g1", [E, 1]), ("b1", [E, 1]), ("wv", [D, D]),
        ("bv", [33, 512]), ("rel", [32, 1]), ("clipx", [1, 1]), ("clipd", [1, 1]),
        ("wgate", [I, D]), ("wup", [I, D]), ("wdown", [D, I]),
        ("onehot", [32, 384]), ("identc", [128, 128]), ("revc", [128, 128]),
        ("ones1c", [33, 128]), ("zselc", [128, 27]),
    ]:
        DIN[name] = nc.dram_tensor(name, shape, F32, kind="ExternalInput")
    DIN["hmaskc"] = nc.dram_tensor("hmaskc", [128, 9 * 128], BF16, kind="ExternalInput")
    out_ext = nc.dram_tensor("out_loc", [ROWS, D], F32, kind="ExternalOutput")
    DB = {}
    if DBG:
        for nm, shape, ddt in [("dbg_q", [128, ROWS], BF16), ("dbg_k", [128, 2048], BF16),
                               ("dbg_xint", [128, 2048], BF16), ("dbg_h", [128, ROWS], F32),
                               ("dbg_wq", [1, 131072], BF16), ("dbg_wd", [1, 131072], BF16),
                               ("dbg_accF", [128, 1024], F32), ("dbg_cols", [128, 16], F32),
                               ("dbg_agA", [1, 4096], BF16)]:
            DB[nm] = nc.dram_tensor(nm, shape, ddt, kind="ExternalOutput")

    VQ = 4 * 128 * D
    agAk_in = nc.dram_tensor("agAk_in", [1, KPAY], BF16)
    agAk_out = nc.dram_tensor("agAk_out", [NCORE, KPAY], BF16)
    agAv_in = [nc.dram_tensor(f"agAv{j}_in", [1, VQ], BF16) for j in range(4)]
    agAv_out = [nc.dram_tensor(f"agAv{j}_out", [NCORE, VQ], BF16) for j in range(4)]
    agB_in = nc.dram_tensor("agB_in", [1, 16], F32)
    agB_out = nc.dram_tensor("agB_out", [NCORE, 16], F32)
    agC_in = nc.dram_tensor("agC_in", [1, 16], F32)
    agC_out = nc.dram_tensor("agC_out", [NCORE, 16], F32)
    buf384 = nc.dram_tensor("buf384", [1, 384], F32)
    scr4 = nc.dram_tensor("scr4", [2, 4], F32)
    h_spill = nc.dram_tensor("h_spill", [I, ROWS], F32)
    wq_hbm = nc.dram_tensor("wq_hbm", [2 * NIC * NDC * 128 * 128], BF16)
    wdT_hbm = nc.dram_tensor("wdT_hbm", [NDC * NIC * 128 * 128], BF16)

    o = Orch()
    _guards = []

    def SB(name, shape, dt=F32):
        g = nc.sbuf_tensor(name, shape, dt)
        h = g.__enter__()
        _guards.append(g)
        return h

    def PSUM(name, shape, dt=F32):
        g = nc.psum_tensor(name, shape, dt)
        h = g.__enter__()
        _guards.append(g)
        return h

    class Sem:
        def __init__(self, name):
            g = nc.semaphore(name)
            self.h = g.__enter__()
            _guards.append(g)
            self.n = 0

        def inc(self, k=1):
            self.n += k
            return self.n

    acc = SB("acc", [128, NU * D])
    q_bf = SB("q_bf", [128, ROWS], BF16)
    ident = SB("identt", [128, 128])
    ones1 = SB("ones1t", [33, 128])
    zsel = SB("zselt", [128, 27])
    zbias = SB("zbiast", [128, 9 * 128])
    bvsb = SB("bvsb", [33, 512])
    relsb = SB("relsb", [32, 1])
    cols = SB("colst", [128, 24])
    rws_g = SB("rws_g", [128, NIC])
    rws_u = SB("rws_u", [128, NIC])
    rws_d = SB("rws_d", [128, NDC])
    scol_g = SB("scol_g", [128, NIC])
    scol_u = SB("scol_u", [128, NIC])
    scol_d = SB("scol_d", [128, NDC])
    smalls = SB("smalls", [1, 16])
    mx8 = SB("mx8", [1, 16])
    agst = SB("agst", [1, 32])
    bc_x = SB("bc_x", [128, 2])
    bc_h = SB("bc_h", [128, 2])
    bqk_t = SB("bqk_t", [128, 1])
    farc_t = SB("farc_t", [128, 1])
    cp_t = SB("cp_t", [128, 1])
    ncp_t = SB("ncp_t", [128, 1])
    amx_t = SB("amx_t", [128, 1])
    amh_t = SB("amh_t", [128, 1])
    scg_t = [SB(f"scg{i}", [128, 1]) for i in range(NIC)]
    scd_t = [SB(f"scd{i}", [128, 1]) for i in range(NDC)]
    s_m = SB("s_m", [128, NU])
    s_e2 = SB("s_e2", [128, NU])
    s_var = SB("s_var", [128, NU])
    s_rs = SB("s_rs", [128, NU])
    s_nb = SB("s_nb", [128, NU])
    amx16 = SB("amx16", [128, NU])
    amh88 = SB("amh88", [128, 88])
    tmpA = SB("tmpA", [128, D])
    tmpB = SB("tmpB", [128, D])
    onehotsb = SB("onehott", [32, 384])

    CP, NCP, FLAG, RXC, RHC, BQK, G0c, B0c, G1c, B1c, FARC, AMX, AMH, SXC, SHC = range(15)
    T0, T1, T2, T4, T5, T6, T7 = 16, 17, 18, 20, 21, 22, 23

    ar1 = SB("ar1", [128, NU * D])
    ar1b = ar1.bitcast(BF16)
    ar2 = SB("ar2", [128, NDC * ROWS], BF16)
    ar4 = SB("ar4", [128, 6144])
    ar4b = ar4.bitcast(BF16)

    xnT = ar1
    xnTb = ar2
    ar2f = ar2.bitcast(F32)

    def A2F(off, n):
        return bass.AP(ar2f, off, [[8192, 128], [1, n]])

    def A2B(off, n):
        return bass.AP(ar2, off, [[16384, 128], [1, n]])

    def A4(off_f32, n, dt=F32):
        if dt is F32:
            return bass.AP(ar4, off_f32, [[6144, 128], [1, n]])
        return bass.AP(ar4b, off_f32, [[12288, 128], [1, n]])

    bufsb = bass.AP(ar4, 0, [[6144, 1], [1, 384]])
    buf241 = bass.AP(ar4, 241, [[6144, 1], [1, 1]])
    w0t = A4(512, 128)
    w1t = A4(640, 128)
    rev = A4(768, 128)
    b0t = A4(896, 128)
    b1t = A4(1024, 128)
    fart = A4(1152, 128)
    wvb = A4(0, NDC * D, BF16)
    wqkb = A4(8192, NDC * 128, BF16)
    krot = A4(9216, ROWS, BF16)
    tmpAb = tmpA.bitcast(BF16)
    vrot = [bass.AP(tmpAb, i * 1024, [[2048, 128], [1, D]]) for i in range(2)]
    qkf = None
    wstage = [A2F(i * 1024, D) for i in range(2)]
    hacc = [A2F(2048 + i * 2048, ROWS) for i in range(2)]
    wqf = [A2F(6144 + i * 512, 512) for i in range(2)]
    wdstage = A4(2048, I)
    wqst = [A4(9728 + i * 512, 512, BF16) for i in range(2)]
    wqe = [A4(i * 1024, NDC * 128, BF16) for i in range(4)]
    silu_t = None
    az = [A4(i * 512, 512, BF16) for i in range(3)]
    ez = [A4(1536 + i * 512, 512, BF16) for i in range(3)]
    ztmp = [bass.AP(tmpB, 0, [[D, 128], [1, 512]]),
            bass.AP(tmpB, 512, [[D, 128], [1, 512]]), A4(1536, 512)]
    hmask = A2B(14336, 9 * 128)
    kbuf = [A4(4608 + i * 2048, 2048, BF16) for i in range(2)]
    hq = A2B(0, NIC * 512)
    hrl = [A2F(6144 + i * 512, 512) for i in range(2)]
    foT = bass.AP(tmpA, 0, [[D, 128], [1, 512]])
    qtmp = bass.AP(tmpA, 512, [[D, 128], [1, 512]])
    wdb_s = A4(8704, NIC * 128, BF16)

    def A1B(off, n):
        return bass.AP(ar1b, off, [[32768, 128], [1, n]])
    xintT = A1B(0, NDC * ROWS)
    vbuf = [A1B(16384, NU * 1024), A1B(0, NU * 1024)]

    def _sl(apx, off, n):
        return bass.AP(apx.tensor, apx.offset + off, [[apx.ap[0][0], 128], [1, n]])

    psA = [PSUM(f"psA{i}", [128, 1024]) for i in range(2)]
    psS_all = PSUM("psSall", [128, 1536])
    psG = PSUM("psG", [128, 512])

    class _PsAlias:
        def __init__(self, base, coff):
            self.base = base
            self.coff = coff

        def __getitem__(self, idx):
            ps, cs = idx
            return self.base[ps, cs.start + self.coff:cs.stop + self.coff]

    psS = [_PsAlias(psS_all, 0), _PsAlias(psS_all, 512)]
    psW = [_PsAlias(psS_all, 1024), _PsAlias(psG, 0)]
    psS_allb = psS_all.bitcast(BF16)
    psGb = psG.bitcast(BF16)
    psWb = [_PsAlias(psS_allb, 2048), _PsAlias(psGb, 0)]
    psSb = [_PsAlias(psS_all, 0), _PsAlias(psS_all, 512), _PsAlias(psS_all, 1024)]

    sem = {n: Sem(n) for n in
           ["ld", "lg", "lsc", "lk", "lv0", "lv1", "lw", "lwq", "lw2", "lwq2",
            "ccA", "ccB", "ccC", "tv", "vt", "sv", "pay", "st", "dbs"]}
    dbs = sem["dbs"]
    ld, lg, lsc, lk, lv0, lv1, lw, lwq = (sem[n] for n in
        ["ld", "lg", "lsc", "lk", "lv0", "lv1", "lw", "lwq"])
    lw2, lwq2 = sem["lw2"], sem["lwq2"]
    lwq_sem = [lwq, lwq2]
    ccA, ccB, ccC, tv, vt, sv, pay, st = (sem[n] for n in
        ["ccA", "ccB", "ccC", "tv", "vt", "sv", "pay", "st"])

    def ph_a(sy):
        for t, nm in [(ident, "identc"), (ones1, "ones1c"), (onehotsb, "onehot"),
                      (zsel, "zselc"), (relsb, "rel"), (bvsb, "bv")]:
            sy.dma_start(out=t[:], in_=DIN[nm].ap()).then_inc(ld.h, 16)
        sy.dma_start(out=rev, in_=DIN["revc"].ap()).then_inc(ld.h, 16)
        sy.dma_start(out=bqk_t[:], in_=DIN["bqk"].ap()).then_inc(ld.h, 16)
        for slot, nm in [(G0c, "g0"), (B0c, "b0"), (G1c, "g1"), (B1c, "b1")]:
            sy.dma_start(out=cols[:, slot:slot + 1], in_=DIN[nm].ap()).then_inc(ld.h, 16)
        sy.dma_start(out=smalls[0:1, 8:9], in_=DIN["clipx"].ap()).then_inc(ld.h, 16)
        sy.dma_start(out=smalls[0:1, 9:10], in_=DIN["clipd"].ap()).then_inc(ld.h, 16)
    o.on("sync", ph_a)
    ld.inc(16 * 14)
    LD_CONST = ld.n

    ld_x = [ld.inc(16) for _ in range(NU)]

    def ph_a_x(sy):
        for t in range(NU):
            src = bass.AP(DIN["x_loc"], t * 128 * D, [[D, 128], [1, D]])
            sy.dma_start(out=acc[:, t * D:(t + 1) * D], in_=src).then_inc(ld.h, 16)
    o.on("sync", ph_a_x)

    wvstg = [A4(4096, D), A4(5120, D)]
    lg_wv = [lg.inc(16) for _ in range(NDC)]
    LG_QKSTG = lg.inc(16)
    VT_CAST0 = [0] * NDC

    def ph_a_g(gp):
        for dc in range(NDC):
            if dc >= 2:
                gp.wait_ge(vt.h, VT_CAST0[dc - 2])
            gp.dma_start(out=wvstg[dc % 2],
                         in_=bass.AP(DIN["wv"], dc * 128 * D, [[D, 128], [1, D]])).then_inc(lg.h, 16)
        gp.wait_ge(vt.h, VT_CAST0[NDC - 1] - 0)
        gp.dma_start(out=wvstg[1],
                     in_=bass.AP(DIN["wqk"], 0, [[E, 128], [128 * E, NDC], [1, E]])).then_inc(lg.h, 16)
    o.on("gpsimd", ph_a_g)



    TV_BUF = tv.inc()

    def b_t1(te):
        te.wait_ge(ld.h, LD_CONST)
        te.matmul(psS[0][0:1, 0:384], relsb[:, 0:1], onehotsb[:],
                  start=True, stop=True).then_inc(tv.h, 1)
    o.on("tensor", b_t1)

    VT_CPREP = vt.inc()
    VT_BUFC = vt.inc()

    def b_v1(v):
        v.memset(cols[:, 0:24], 0.0)
        v.memset(smalls[0:1, 0:16], 0.0)
        v.memset(agst[0:1, 0:32], 0.0)
        v.memset(cols[:, FLAG:FLAG + 1], 1.0)
        v.drain()
        v.tensor_scalar(cols[:, T0:T0 + 1], cols[:, FLAG:FLAG + 1],
                        float(np.float32(2.0 ** -30)), None, ALU.add)
        v.drain()
        v.tensor_scalar(cols[:, T1:T1 + 1], cols[:, T0:T0 + 1], 1.0, None, ALU.subtract)
        v.drain()
        v.tensor_scalar(cols[:, FLAG:FLAG + 1], cols[:, T1:T1 + 1], 0.0, None, ALU.is_gt)
        v.drain()
        v.tensor_scalar(cols[:, T0:T0 + 1], cols[:, FLAG:FLAG + 1], C52, None, ALU.mult)
        v.tensor_scalar(cols[:, T1:T1 + 1], cols[:, FLAG:FLAG + 1], -1.0, None, ALU.subtract)
        v.drain()
        v.tensor_scalar(cols[:, T1:T1 + 1], cols[:, T1:T1 + 1], -C32, None, ALU.mult)
        v.drain()
        v.tensor_tensor(cols[:, CP:CP + 1], cols[:, T0:T0 + 1], cols[:, T1:T1 + 1], ALU.add)
        v.drain()
        v.tensor_scalar(cols[:, NCP:NCP + 1], cols[:, CP:CP + 1], -1.0, None, ALU.mult)
        v.tensor_copy(cp_t[:], cols[:, CP:CP + 1])
        v.drain()
        v.tensor_copy(ncp_t[:], cols[:, NCP:NCP + 1]).then_inc(vt.h, 1)
        v.wait_ge(tv.h, TV_BUF)
        v.tensor_scalar(bufsb, psS[0][0:1, 0:384], -MU_A, None, ALU.add).then_inc(vt.h, 1)
    o.on("vector", b_v1)

    LD_B384 = ld.inc(16)
    LD_W01 = ld.inc(32)

    def b_s1(sy):
        sy.wait_ge(vt.h, VT_BUFC)
        sy.dma_start(out=buf384.ap(), in_=bufsb).then_inc(ld.h, 16)
        sy.wait_ge(ld.h, LD_B384)
        sy.dma_start(out=w0t, in_=bass.AP(buf384, 1, [[1, 128], [1, 128]])).then_inc(ld.h, 16)
        sy.dma_start(out=w1t, in_=bass.AP(buf384, 129, [[1, 128], [1, 128]])).then_inc(ld.h, 16)
    o.on("sync", b_s1)

    TV_B01 = tv.inc()

    def b_t2(te):
        te.wait_ge(ld.h, LD_W01)
        te.matmul(psW[0][:, 0:128], rev, w0t, start=True, stop=True)
        te.matmul(psW[0][:, 128:256], rev, w1t, start=True, stop=True)
        te.matmul(psW[0][:, 256:257], ones1[0:1, :], buf241,
                  start=True, stop=True).then_inc(tv.h, 1)
    o.on("tensor", b_t2)

    VT_ZB = vt.inc()

    def b_v2(v):
        v.wait_ge(tv.h, TV_B01)
        v.tensor_copy(b0t, psW[0][:, 0:128])
        v.tensor_copy(b1t, psW[0][:, 128:256])
        v.tensor_copy(cols[:, FARC:FARC + 1], psW[0][:, 256:257])
        v.tensor_copy(farc_t[:], psW[0][:, 256:257])
        v.drain()
        v.tensor_scalar(fart, ident[:], 0.0, cols[:, FARC:FARC + 1], ALU.mult, ALU.add)
        v.drain()
        for p in range(9):
            dstp = zbias[:, p * 128:(p + 1) * 128]
            v.tensor_scalar(dstp, fart, zsel[:, 3 * p:3 * p + 1], None, ALU.mult)
            v.drain()
            v.scalar_tensor_tensor(tmpA[:, 0:128], b1t, zsel[:, 3 * p + 1:3 * p + 2],
                                   dstp, ALU.mult, ALU.add)
            v.drain()
            inst = v.scalar_tensor_tensor(dstp, b0t, zsel[:, 3 * p + 2:3 * p + 3],
                                          tmpA[:, 0:128], ALU.mult, ALU.add)
            v.drain()
        inst.then_inc(vt.h, 1)
    o.on("vector", b_v2)

    VT_CAST0_DECL = None

    def ph_a_cast_v(v):
        v.wait_ge(tv.h, TV_B01)
        for dc in range(NDC):
            v.wait_ge(lg.h, lg_wv[dc])
            v.tensor_copy(_sl(wvb, dc * D, D), wvstg[dc % 2]).then_inc(vt.h, 1)
        v.wait_ge(lg.h, LG_QKSTG)
        v.tensor_copy(wqkb, wvstg[1]).then_inc(vt.h, 1)
    o.on("vector", ph_a_cast_v)
    for dc in range(NDC):
        VT_CAST0[dc] = vt.inc()
    VT_WQKB = vt.inc()

    sv_sq = []
    for _ in range(NU):
        sv_sq.append(sv.inc())
    vt_red = []
    for _ in range(NU):
        vt_red.append(vt.inc())
    VT_VAR = vt.inc()
    SV_SD = sv.inc()
    VT_RS = vt.inc()
    vt_xn, vt_ev = [], []
    for _ in range(NU):
        vt_xn.append(vt.inc())
        vt_ev.append([vt.inc(), vt.inc()])
    VT_AMXR = vt.inc()
    tv_tr = []
    for _ in range(NU):
        tv_tr.append([tv.inc(), tv.inc()])

    def b_ln_s(sc):
        for t in range(NU):
            sc.wait_ge(ld.h, ld_x[t])
            if t == 0:
                sc.wait_ge(vt.h, VT_ZB)
            if t >= 2:
                sc.wait_ge(vt.h, vt_red[t - 2])
            xt = acc[:, t * D:(t + 1) * D]
            sc.activation(tmpA[:, 0:1024] if t % 2 == 0 else tmpB[:, 0:1024],
                          xt, AF.Square, bias=0.0, scale=1.0).then_inc(sv.h, 1)
        sc.wait_ge(vt.h, VT_VAR)
        sc.activation(s_var[:], s_var[:], AF.Sqrt, bias=0.0, scale=1.0).then_inc(sv.h, 1)
    o.on("scalar", b_ln_s)

    def b_ln_v(v):
        for t in range(NU):
            xt = acc[:, t * D:(t + 1) * D]
            v.wait_ge(ld.h, ld_x[t])
            v.tensor_reduce(s_m[:, t:t + 1], xt, AX.X, ALU.add)
            v.wait_ge(sv.h, sv_sq[t])
            v.tensor_reduce(s_e2[:, t:t + 1], tmpA[:, 0:1024] if t % 2 == 0 else tmpB[:, 0:1024],
                            AX.X, ALU.add).then_inc(vt.h, 1)
        v.drain()
        v.tensor_scalar(s_m[:], s_m[:], 1.0 / D, None, ALU.mult)
        v.tensor_scalar(s_e2[:], s_e2[:], 1.0 / D, None, ALU.mult)
        v.drain()
        v.tensor_tensor(s_var[:], s_m[:], s_m[:], ALU.mult)
        v.drain()
        v.tensor_tensor(s_var[:], s_e2[:], s_var[:], ALU.subtract)
        v.drain()
        v.tensor_scalar(s_var[:], s_var[:], EPS, None, ALU.add)
        v.drain()
        v.nop().then_inc(vt.h, 1)
        v.wait_ge(sv.h, SV_SD)
        v.reciprocal(s_rs[:], s_var[:])
        v.drain()
        v.tensor_tensor(s_nb[:], s_m[:], s_rs[:], ALU.mult)
        v.drain()
        v.tensor_scalar(s_nb[:], s_nb[:], -1.0, None, ALU.mult)
        v.drain()
        v.nop().then_inc(vt.h, 1)
        for t in range(NU):
            xt = acc[:, t * D:(t + 1) * D]
            v.tensor_scalar(tmpB[:, 0:1024], xt, s_rs[:, t:t + 1], s_nb[:, t:t + 1],
                            ALU.mult, ALU.add)
            v.tensor_reduce(amx16[:, t:t + 1], tmpB[:, 0:1024], AX.X, ALU.max,
                            apply_absolute_value=True).then_inc(vt.h, 1)
            for grp in range(2):
                v.wait_ge(tv.h, tv_tr[t][grp])
                dst = bass.AP(xnT, (4 * grp) * ROWS + t * 128,
                              [[NDC * ROWS, 128], [ROWS, 4], [1, 128]])
                v.tensor_copy(dst, psW[grp][:, 0:512]).then_inc(vt.h, 1)
        v.drain()
        v.tensor_reduce(amx_t[:], amx16[:], AX.X, ALU.max).then_inc(vt.h, 1)
    o.on("vector", b_ln_v)

    def b_ln_t(te):
        for t in range(NU):
            te.wait_ge(vt.h, vt_xn[t])
            for grp in range(2):
                if t == 0 and grp == 0:
                    te.wait_ge(vt.h, VT_ZB)
                if t > 0:
                    te.wait_ge(vt.h, vt_ev[t - 1][grp])
                for dc in range(4):
                    dcg = grp * 4 + dc
                    inst = te.transpose(psW[grp][:, dc * 128:(dc + 1) * 128],
                                        tmpB[:, dcg * 128:(dcg + 1) * 128], ident[:])
                inst.then_inc(tv.h, 1)
    o.on("tensor", b_ln_t)

    TV_AMX = tv.inc()

    def b_amax_t(te):
        te.wait_ge(vt.h, VT_AMXR)
        te.transpose(psS[1][0:1, 0:128], amx_t[:], ident[:]).then_inc(tv.h, 1)
    o.on("tensor", b_amax_t)

    VT_AMX = vt.inc()

    def b_amax_v(v):
        v.wait_ge(tv.h, TV_AMX)
        v.tensor_copy(tmpA[0:1, 0:128], psS[1][0:1, 0:128])
        v.drain()
        v.tensor_reduce(agst[0:1, 0:1], tmpA[0:1, 0:128], AX.X, ALU.max,
                        apply_absolute_value=True).then_inc(vt.h, 1)
    o.on("vector", b_amax_v)

    LG_AGB = lg.inc(16)
    ccB.inc()

    def b_amax_g(gp):
        gp.wait_ge(vt.h, VT_AMX)
        gp.dma_start(out=bass.AP(agB_in, 0, [[16, 1], [1, 16]]),
                     in_=agst[0:1, 0:16]).then_inc(lg.h, 16)
        gp.wait_ge(lg.h, LG_AGB)
        gp.collective_compute("AllGather", ALU.bypass, replica_groups=[list(range(NCORE))],
                              ins=[agB_in.ap().opt()], outs=[agB_out.ap().opt()]).then_inc(ccB.h, 1)
    o.on("gpsimd", b_amax_g)

    vt_cast = [vt.inc() for _ in range(NDC)]

    def c_cast_v(v):
        v.wait_ge(vt.h, vt_ev[NU - 1][1])
        for dc in range(NDC):
            v.tensor_copy(xnTb[:, dc * ROWS:(dc + 1) * ROWS],
                          xnT[:, dc * ROWS:(dc + 1) * ROWS]).then_inc(vt.h, 1)
    o.on("vector", c_cast_v)

    tv_qk = [tv.inc() for _ in range(4)]
    sv_qk = [sv.inc() for _ in range(4)]
    vt_qk = [vt.inc() for _ in range(4)]

    def c_qk_t(te):
        te.wait_ge(vt.h, VT_WQKB)
        te.wait_ge(vt.h, vt_cast[NDC - 1])
        te.wait_ge(vt.h, VT_AMX)
        for sc4 in range(4):
            if sc4 >= 2:
                te.wait_ge(sv.h, sv_qk[sc4 - 2])
            ps = psS[sc4 % 2]
            for dc in range(NDC):
                inst = te.matmul(ps[:, 0:512], _sl(wqkb, dc * 128, 128),
                                 xnTb[:, dc * ROWS + sc4 * 512: dc * ROWS + sc4 * 512 + 512],
                                 start=(dc == 0), stop=(dc == NDC - 1))
            inst.then_inc(tv.h, 1)
    o.on("tensor", c_qk_t)

    def c_qk_s(sc):
        for sc4 in range(4):
            sc.wait_ge(tv.h, tv_qk[sc4])
            if sc4 >= 1:
                sc.wait_ge(vt.h, vt_qk[sc4 - 1])
            sc.activation(tmpB[:, 0:512], psS[sc4 % 2][:, 0:512], AF.Silu,
                          bias=bqk_t[:], scale=1.0).then_inc(sv.h, 1)
    o.on("scalar", c_qk_s)

    def c_qk_v(v):
        for sc4 in range(4):
            v.wait_ge(sv.h, sv_qk[sc4])
            v.tensor_scalar(q_bf[:, sc4 * 512:(sc4 + 1) * 512], tmpB[:, 0:512],
                            cols[:, G0c:G0c + 1], cols[:, B0c:B0c + 1], ALU.mult, ALU.add)
            v.tensor_scalar(_sl(krot, sc4 * 512, 512), tmpB[:, 0:512],
                            cols[:, G1c:G1c + 1], cols[:, B1c:B1c + 1],
                            ALU.mult, ALU.add).then_inc(vt.h, 1)
    o.on("vector", c_qk_v)

    PAY_K = pay.inc(16)

    def c_k_pay(sy):
        sy.wait_ge(vt.h, vt_qk[3])
        sy.dma_start(out=bass.AP(agAk_in, 0, [[ROWS, 128], [1, ROWS]]),
                     in_=krot).then_inc(pay.h, 16)
    o.on("sync", c_k_pay)

    tv_v = [tv.inc() for _ in range(NU)]
    sv_v = [sv.inc() for _ in range(NU)]
    pay_v = [pay.inc(16) for _ in range(NU)]
    PAY_ALL = pay.n

    def c_v_t(te):
        te.wait_ge(vt.h, VT_WQKB)
        for t in range(NU):
            if t >= 2:
                te.wait_ge(sv.h, sv_v[t - 2])
            ps = psA[t % 2]
            for dc in range(NDC):
                for half in range(2):
                    te.matmul(ps[:, half * 512:half * 512 + 512],
                              xnTb[:, dc * ROWS + t * 128: dc * ROWS + t * 128 + 128],
                              _sl(wvb, dc * D + half * 512, 512),
                              start=(dc == 0), stop=False)
            for half in range(2):
                inst = te.matmul(ps[:, half * 512:half * 512 + 512],
                                 ones1[half * 32:half * 32 + 1, :],
                                 bvsb[half * 32:half * 32 + 1, 0:512], start=False, stop=True)
            inst.then_inc(tv.h, 1)
    o.on("tensor", c_v_t)

    def c_v_s(sc):
        for t in range(NU):
            sc.wait_ge(tv.h, tv_v[t])
            if t >= 2:
                sc.wait_ge(pay.h, pay_v[t - 2])
            sc.activation(vrot[t % 2], psA[t % 2][:, 0:1024], AF.Silu,
                          bias=0.0, scale=1.0).then_inc(sv.h, 1)
    o.on("scalar", c_v_s)

    def c_v_pay(sy):
        for t in range(NU):
            sy.wait_ge(sv.h, sv_v[t])
            sy.dma_start(out=bass.AP(agAv_in[t // 4], (t % 4) * (128 * D),
                                     [[D, 128], [1, D]]),
                         in_=vrot[t % 2]).then_inc(pay.h, 16)
    o.on("sync", c_v_pay)

    ccA.inc(5)

    def c_agA(gp):
        gp.wait_ge(pay.h, PAY_K)
        gp.collective_compute("AllGather", ALU.bypass, replica_groups=[list(range(NCORE))],
                              ins=[agAk_in.ap().opt()],
                              outs=[agAk_out.ap().opt()]).then_inc(ccA.h, 1)
        for j in range(4):
            gp.wait_ge(pay.h, pay_v[4 * j + 3])
            gp.collective_compute("AllGather", ALU.bypass,
                                  replica_groups=[list(range(NCORE))],
                                  ins=[agAv_in[j].ap().opt()],
                                  outs=[agAv_out[j].ap().opt()]).then_inc(ccA.h, 1)

    LG_MAXX = lg.inc(16)
    LG_BCX = lg.inc(32)

    def d_mx_g(gp):
        gp.wait_ge(ccB.h, 1)
        gp.dma_start(out=mx8[0:1, 0:8],
                     in_=bass.AP(agB_out, 0, [[1, 1], [16, 8]]),
                     allow_slow_non_contiguous=True).then_inc(lg.h, 16)
        gp.wait_ge(vt.h, VT_RX)
        gp.dma_start(out=bass.AP(scr4, 0, [[4, 1], [1, 2]]),
                     in_=smalls[0:1, 2:4]).then_inc(lg.h, 16)
        gp.wait_ge(lg.h, LG_BCX - 16)
        gp.dma_start(out=bc_x[:], in_=bass.AP(scr4, 0, [[0, 128], [1, 2]])).then_inc(lg.h, 16)
    o.on("gpsimd", d_mx_g)

    LG_HM = lg.inc(16)

    def f_hm_g(gp):
        gp.wait_ge(tv.h, tv_v[NU - 1])
        gp.dma_start(out=hmask, in_=DIN["hmaskc"].ap()).then_inc(lg.h, 16)
    o.on("gpsimd", f_hm_g)
    o.on("gpsimd", c_agA)

    SV_SIG = sv.inc()

    def d_sig_s(sc):
        sc.wait_ge(ld.h, LD_CONST)
        sc.activation(smalls[0:1, 10:11], smalls[0:1, 8:9], AF.Sigmoid, bias=0.0, scale=1.0)
        sc.activation(smalls[0:1, 11:12], smalls[0:1, 9:10], AF.Sigmoid,
                      bias=0.0, scale=1.0).then_inc(sv.h, 1)
    o.on("scalar", d_sig_s)

    VT_RX = vt.inc()

    def d_rx_v(v):
        v.wait_ge(lg.h, LG_MAXX)
        v.wait_ge(sv.h, SV_SIG)
        v.tensor_reduce(smalls[0:1, 1:2], mx8[0:1, 0:8], AX.X, ALU.max)
        v.drain()
        v.tensor_tensor(smalls[0:1, 2:3], smalls[0:1, 10:11], smalls[0:1, 1:2], ALU.mult)
        v.drain()
        v.tensor_scalar(smalls[0:1, 2:3], smalls[0:1, 2:3], 1.0 / 127.0, None, ALU.mult)
        v.drain()
        v.reciprocal(smalls[0:1, 3:4], smalls[0:1, 2:3])
        v.drain()
        v.nop().then_inc(vt.h, 1)
    o.on("vector", d_rx_v)

    vt_xq = [vt.inc() for _ in range(NDC)]

    def d_xq_v(v):
        v.wait_ge(lg.h, LG_BCX)
        for dc in range(NDC):
            for hf in range(2):
                src_ = xnT[:, dc * ROWS + hf * 1024: dc * ROWS + (hf + 1) * 1024]
                dst = _sl(xintT, dc * ROWS + hf * 1024, 1024)
                v.tensor_scalar(tmpB[:, 0:1024], src_, bc_x[:, 1:2], 127.0,
                                ALU.mult, ALU.min)
                v.tensor_scalar(tmpB[:, 0:1024], tmpB[:, 0:1024], -128.0, cols[:, CP:CP + 1],
                                ALU.max, ALU.add)
                inst = v.tensor_scalar(dst, tmpB[:, 0:1024], cols[:, CP:CP + 1], None,
                                       ALU.subtract)
            inst.then_inc(vt.h, 1)
    o.on("vector", d_xq_v)

    tmpBb = tmpB.bitcast(BF16)
    wqb = [bass.AP(tmpBb, i * 1024, [[2048, 128], [1, 1024]]) for i in range(2)]
    identb = A4(11264, 128, BF16)
    seqw = [(wi, it) for it in range(NIC) for wi in range(2)]
    lw_t = {}
    lw_sem = {}
    for _i, _k in enumerate(seqw):
        _s = lw if _i % 2 == 0 else lw2
        lw_t[_k] = _s.inc(16)
        lw_sem[_k] = _s
    vt_wdiag = {}
    for k in seqw:
        vt_wdiag[k] = vt.inc()
    tv_wq = {}
    for k in seqw:
        tv_wq[(k, 0)] = tv.inc()
        tv_wq[(k, 1)] = tv.inc()
    sv_wq = {}
    for k in seqw:
        sv_wq[(k, 0)] = sv.inc()
        sv_wq[(k, 1)] = sv.inc()
    lwq_st = {}
    for k in seqw:
        lwq_st[(k, 0)] = lwq.inc(16)
        lwq_st[(k, 1)] = lwq2.inc(16)

    def d_w_sync(sy):
        for idx, (wi, it) in enumerate(seqw):
            if idx == 0:
                sy.wait_ge(tv.h, tv_v[NU - 1])
            if idx >= 2:
                sy.wait_ge(tv.h, tv_wq[(seqw[idx - 2], 1)])
            win = DIN["wgate"] if wi == 0 else DIN["wup"]
            sy.dma_start(out=wstage[idx % 2],
                         in_=bass.AP(win, it * 128 * D,
                                     [[D, 128], [1, D]])).then_inc(lw_sem[(wi, it)].h, 16)
            if idx >= 1:
                pk = seqw[idx - 1]
                for grp in range(2):
                    sy.wait_ge(sv.h, sv_wq[(pk, grp)])
                    dst = bass.AP(wq_hbm, (pk[0] * NIC + pk[1]) * (NDC * 128 * 128) + grp * 4 * 16384,
                                  [[128, 128], [16384, 4], [1, 128]])
                    sy.dma_start(out=dst, in_=wqst[grp]).then_inc(lwq_sem[grp].h, 16)
        pk = seqw[-1]
        for grp in range(2):
            sy.wait_ge(sv.h, sv_wq[(pk, grp)])
            dst = bass.AP(wq_hbm, (pk[0] * NIC + pk[1]) * (NDC * 128 * 128) + grp * 4 * 16384,
                          [[128, 128], [16384, 4], [1, 128]])
            sy.dma_start(out=dst, in_=wqst[grp]).then_inc(lwq_sem[grp].h, 16)
    o.on("sync", d_w_sync)

    def d_w_v(v):
        for idx, k in enumerate(seqw):
            wi, it = k
            v.wait_ge(lw_sem[k].h, lw_t[k])
            if idx == 0:
                v.wait_ge(pay.h, PAY_ALL)
            rws = rws_g if wi == 0 else rws_u
            v.tensor_reduce(cols[:, T0:T0 + 1], wstage[idx % 2], AX.X, ALU.max,
                            apply_absolute_value=True)
            v.drain()
            v.tensor_scalar(rws[:, it:it + 1], cols[:, T0:T0 + 1], 1.0 / 127.0, None, ALU.mult)
            v.reciprocal(cols[:, T1:T1 + 1], cols[:, T0:T0 + 1])
            v.drain()
            v.tensor_scalar(cols[:, T1:T1 + 1], cols[:, T1:T1 + 1], 127.0, None, ALU.mult)
            v.drain()
            v.tensor_scalar(tmpB[:, 0:1024], wstage[idx % 2], cols[:, T1:T1 + 1],
                            127.0, ALU.mult, ALU.min)
            v.tensor_scalar(tmpB[:, 0:1024], tmpB[:, 0:1024], -128.0, cp_t[:],
                            ALU.max, ALU.add)
            v.tensor_scalar(wstage[idx % 2], tmpB[:, 0:1024], ncp_t[:], None,
                            ALU.add).then_inc(vt.h, 1)
    o.on("vector", d_w_v)

    def d_w_t(te):
        for idx, k in enumerate(seqw):
            te.wait_ge(vt.h, vt_wdiag[k])
            if idx == 0:
                te.wait_ge(vt.h, vt_ev[NU - 1][1])
            for grp in range(2):
                if idx >= 1:
                    te.wait_ge(sv.h, sv_wq[(seqw[idx - 1], grp)])
                for b in range(4):
                    hc = grp * 4 + b
                    inst = te.transpose(psW[grp][:, b * 128:(b + 1) * 128],
                                        _sl(wstage[idx % 2], hc * 128, 128),
                                        ident[:])
                inst.then_inc(tv.h, 1)
    o.on("tensor", d_w_t)

    def d_w_s(sc):
        for idx, k in enumerate(seqw):
            for grp in range(2):
                sc.wait_ge(tv.h, tv_wq[(k, grp)])
                if idx == 0 and grp == 0:
                    sc.wait_ge(pay.h, PAY_K)
                if idx >= 1:
                    sc.wait_ge(lwq_sem[grp].h, lwq_st[(seqw[idx - 1], grp)])
                sc.activation(wqst[grp], psW[grp][:, 0:512], AF.Copy, bias=0.0,
                              scale=1.0).then_inc(sv.h, 1)
    o.on("scalar", d_w_s)

    seqd = [(hd, gb) for hd in range(NDC) for gb in range(6)]
    lwd_t = {hd: lw.inc(16) for hd in range(NDC)}
    vt_dclip = {}
    for hd in range(NDC):
        for gb in range(6):
            vt_dclip[(hd, gb)] = vt.inc()
    tv_dq = {k: tv.inc() for k in seqd}
    sv_dq = {k: sv.inc() for k in seqd}
    lwd_st = {k: lwq_sem[k[1] % 2].inc(16) for k in seqd}
    LWQ_ALL0 = lwq.n
    LWQ_ALL1 = lwq2.n
    TV_DQL = tv_dq[(NDC - 1, 5)]
    SV_DCL = sv_dq[(NDC - 1, 5)]

    def d_wd_sync(sy):
        sy.dma_start(out=wdstage,
                     in_=bass.AP(DIN["wdown"], 0, [[I, 128], [1, I]])).then_inc(lw.h, 16)
        for hd in range(NDC):
            for gb in range(6):
                nb = 4 if gb < 5 else 2
                sy.wait_ge(sv.h, sv_dq[(hd, gb)])
                dst = bass.AP(wdT_hbm, hd * (NIC * 128 * 128) + gb * 4 * 16384,
                              [[128, 128], [16384, nb], [1, 128]])
                sy.dma_start(out=dst,
                             in_=_sl(wqst[gb % 2], 0, nb * 128)).then_inc(lwq_sem[gb % 2].h, 16)
            if hd < NDC - 1:
                sy.wait_ge(tv.h, tv_dq[(hd, 5)])
                sy.dma_start(out=wdstage,
                             in_=bass.AP(DIN["wdown"], (hd + 1) * 128 * I,
                                         [[I, 128], [1, I]])).then_inc(lw.h, 16)
    o.on("sync", d_wd_sync)

    def d_wd_v(v):
        for hd in range(NDC):
            v.wait_ge(lw.h, lwd_t[hd])
            v.tensor_reduce(cols[:, T0:T0 + 1], wdstage, AX.X, ALU.max,
                            apply_absolute_value=True)
            v.drain()
            v.tensor_scalar(rws_d[:, hd:hd + 1], cols[:, T0:T0 + 1], 1.0 / 127.0, None, ALU.mult)
            v.reciprocal(cols[:, T1:T1 + 1], cols[:, T0:T0 + 1])
            v.drain()
            v.tensor_scalar(cols[:, T1:T1 + 1], cols[:, T1:T1 + 1], 127.0, None, ALU.mult)
            v.drain()
            for gb in range(6):
                nb = 4 if gb < 5 else 2
                src_ = _sl(wdstage, gb * 512, nb * 128)
                v.tensor_scalar(tmpB[:, 0:nb * 128], src_, cols[:, T1:T1 + 1], 127.0,
                                ALU.mult, ALU.min)
                v.tensor_scalar(tmpB[:, 0:nb * 128], tmpB[:, 0:nb * 128], -128.0,
                                cp_t[:], ALU.max, ALU.add)
                v.tensor_scalar(src_, tmpB[:, 0:nb * 128], ncp_t[:], None,
                                ALU.add).then_inc(vt.h, 1)
    o.on("vector", d_wd_v)

    def d_wd_t(te):
        for hd in range(NDC):
            for gb in range(6):
                nb = 4 if gb < 5 else 2
                kk = hd * 6 + gb
                te.wait_ge(vt.h, vt_dclip[(hd, gb)])
                if kk < 2:
                    te.wait_ge(sv.h, sv_wq[(seqw[-1], kk % 2)])
                else:
                    te.wait_ge(sv.h, sv_dq[seqd[kk - 2]])
                for b in range(nb):
                    it = gb * 4 + b
                    inst = te.transpose(psW[gb % 2][:, b * 128:(b + 1) * 128],
                                        _sl(wdstage, it * 128, 128),
                                        ident[:])
                inst.then_inc(tv.h, 1)
    o.on("tensor", d_wd_t)

    def d_wd_s(sc):
        for hd in range(NDC):
            for gb in range(6):
                nb = 4 if gb < 5 else 2
                kk = hd * 6 + gb
                sc.wait_ge(tv.h, tv_dq[(hd, gb)])
                if kk < 2:
                    sc.wait_ge(lwq_sem[kk % 2].h, lwq_st[(seqw[-1], kk % 2)])
                else:
                    sc.wait_ge(lwq_sem[gb % 2].h, lwd_st[seqd[kk - 2]])
                sc.activation(_sl(wqst[gb % 2], 0, nb * 128), psW[gb % 2][:, 0:nb * 128],
                              AF.Copy, bias=0.0, scale=1.0).then_inc(sv.h, 1)
    o.on("scalar", d_wd_s)

    lwe_t = {}
    lv_ch = {}
    for it in range(NIC):
        for wi in range(2):
            lwe_t[(it, wi)] = lv0.inc(16)
        if it == 3:
            lv_ch[0] = lv0.inc(64)
    tv_gp = {}
    for it in range(NIC):
        for pr in range(2):
            tv_gp[(it, pr, 0)] = tv.inc()
            tv_gp[(it, pr, 1)] = tv.inc()
    sv_gu = {}
    for it in range(NIC):
        for sc4 in range(4):
            sv_gu[(it, sc4)] = sv.inc()
    VT_SCOL = vt.inc()
    vt_gu = {}
    for it in range(NIC):
        for sc4 in range(4):
            vt_gu[(it, sc4)] = vt.inc()
    lh_sp = {it: lsc.inc(16) for it in range(NIC)}

    def e_wq_ld(gp):
        for it in range(NIC):
            for wi in range(2):
                gp.wait_ge(lwq.h, lwq_st[((1, it), 0)])
                gp.wait_ge(lwq2.h, lwq_st[((1, it), 1)])
                if it >= 2 and wi == 0:
                    gp.wait_ge(vt.h, vt_gu[(it - 2, 3)])
                gp.dma_start(out=wqe[2 * (it % 2) + wi],
                             in_=bass.AP(wq_hbm, (wi * NIC + it) * (NDC * 128 * 128),
                                         [[NDC * 128, 128], [1, NDC * 128]])).then_inc(lv0.h, 16)
            if it == 3:
                gp.wait_ge(ccA.h, 2)
                gp.wait_ge(vt.h, vt_xq[NDC - 1])
                gp.wait_ge(tv.h, TV_DQL)
                for ubit in range(2):
                    gp.dma_start(
                        out=bass.AP(kbuf[0].tensor, kbuf[0].offset + ubit * 1024,
                                    [[12288, 128], [128, 8], [1, 128]]),
                        in_=bass.AP(agAk_out, ubit * 128,
                                    [[ROWS, 128], [KPAY, 8], [1, 128]]),
                    ).then_inc(lv0.h, 16)
                for ubit in range(2):
                    gp.dma_start(
                        out=bass.AP(vbuf[0].tensor, vbuf[0].offset + ubit * 8192,
                                    [[32768, 128], [1024, 8], [1, 1024]]),
                        in_=bass.AP(agAv_out[0], ubit * (128 * D),
                                    [[D, 128], [VQ, 8], [1, D]]),
                    ).then_inc(lv0.h, 16)
    o.on("gpsimd", e_wq_ld)

    _eset = [
        [psS[0][:, 0:512], psS[1][:, 0:512], psA[0][:, 0:512], psA[0][:, 512:1024]],
        [psW[0][:, 0:512], psW[1][:, 0:512], psA[1][:, 0:512], psA[1][:, 512:1024]],
    ]

    def e_gu_t(te):
        te.wait_ge(vt.h, vt_xq[NDC - 1])
        te.wait_ge(sv.h, sv_v[NU - 1])
        for it in range(NIC):
            te.wait_ge(lv0.h, lwe_t[(it, 1)])
            for pr in range(2):
                pi = it * 2 + pr
                bs = _eset[pi % 2]
                if pi == 1:
                    te.wait_ge(sv.h, SV_DCL)
                if pi >= 2:
                    pit, ppr = divmod(pi - 2, 2)
                    te.wait_ge(vt.h, vt_gu[(pit, 2 * ppr + 1)])
                for wi in range(2):
                    for dc in range(NDC):
                        te.matmul(bs[2 * wi + 0], _sl(wqe[2 * (it % 2) + wi], dc * 128, 128),
                                  _sl(xintT, dc * ROWS + (2 * pr) * 512, 512),
                                  start=(dc == 0), stop=(dc == NDC - 1))
                        inst = te.matmul(bs[2 * wi + 1],
                                         _sl(wqe[2 * (it % 2) + wi], dc * 128, 128),
                                         _sl(xintT, dc * ROWS + (2 * pr + 1) * 512, 512),
                                         start=(dc == 0), stop=(dc == NDC - 1))
                    inst.then_inc(tv.h, 1)
    o.on("tensor", e_gu_t)

    def e_scol_v(v):
        v.wait_ge(lg.h, LG_BCX)
        v.drain()
        v.tensor_scalar(scol_g[:], rws_g[:], bc_x[:, 0:1], None, ALU.mult)
        v.tensor_scalar(scol_u[:], rws_u[:], bc_x[:, 0:1], None, ALU.mult)
        v.drain()
        for it in range(NIC):
            v.tensor_copy(scg_t[it][:], scol_g[:, it:it + 1])
        v.drain()
        v.nop().then_inc(vt.h, 1)
    o.on("vector", e_scol_v)

    def e_gu_s(sc):
        sc.wait_ge(vt.h, VT_SCOL)
        for it in range(NIC):
            for sc4 in range(4):
                pr, j = divmod(sc4, 2)
                sc.wait_ge(tv.h, tv_gp[(it, pr, 0)])
                kk = it * 4 + sc4
                if kk >= 1:
                    pit, psc = divmod(kk - 1, 4)
                    sc.wait_ge(vt.h, vt_gu[(pit, psc)])
                sc.activation(tmpB[:, 512:1024], _eset[(it * 2 + pr) % 2][j], AF.Silu,
                              bias=0.0, scale=scg_t[it][:]).then_inc(sv.h, 1)
    o.on("scalar", e_gu_s)

    def e_gu_v(v):
        for it in range(NIC):
            for sc4 in range(4):
                pr, j = divmod(sc4, 2)
                v.wait_ge(sv.h, sv_gu[(it, sc4)])
                v.wait_ge(tv.h, tv_gp[(it, pr, 1)])
                if it >= 2 and sc4 == 0:
                    v.wait_ge(lsc.h, lh_sp[it - 2])
                dst = _sl(hacc[it % 2], sc4 * 512, 512)
                v.scalar_tensor_tensor(dst, _eset[(it * 2 + pr) % 2][2 + j],
                                       scol_u[:, it:it + 1], tmpB[:, 512:1024],
                                       ALU.mult, ALU.mult)
                v.tensor_reduce(amh88[:, it * 4 + sc4:it * 4 + sc4 + 1], dst, AX.X, ALU.max,
                                apply_absolute_value=True).then_inc(vt.h, 1)
    o.on("vector", e_gu_v)

    def e_h_sp(sy):
        for it in range(NIC):
            sy.wait_ge(vt.h, vt_gu[(it, 3)])
            dst = bass.AP(h_spill, it * 128 * ROWS, [[ROWS, 128], [1, ROWS]])
            sy.dma_start(out=dst, in_=hacc[it % 2]).then_inc(lsc.h, 16)
    o.on("sync", e_h_sp)

    VT_AMHR = vt.inc()

    def e_amh_v(v):
        v.drain()
        v.tensor_reduce(amh_t[:], amh88[:, 0:NIC * 4], AX.X, ALU.max).then_inc(vt.h, 1)
    o.on("vector", e_amh_v)

    TV_AMH = tv.inc()

    def e_hm_t(te):
        te.wait_ge(vt.h, VT_AMHR)
        te.transpose(psS[1][0:1, 0:128], amh_t[:], ident[:]).then_inc(tv.h, 1)
    o.on("tensor", e_hm_t)

    VT_AMH = vt.inc()

    def e_hm_v(v):
        v.wait_ge(tv.h, TV_AMH)
        v.tensor_copy(tmpA[0:1, 0:128], psS[1][0:1, 0:128])
        v.drain()
        v.tensor_reduce(agst[0:1, 16:17], tmpA[0:1, 0:128], AX.X, ALU.max,
                        apply_absolute_value=True).then_inc(vt.h, 1)
    o.on("vector", e_hm_v)

    LG_AGC = lg.inc(16)
    ccC.inc()

    def e_hm_g(gp):
        gp.wait_ge(vt.h, VT_AMH)
        gp.dma_start(out=bass.AP(agC_in, 0, [[16, 1], [1, 16]]),
                     in_=agst[0:1, 16:32]).then_inc(lg.h, 16)
        gp.wait_ge(lg.h, LG_AGC)
        gp.collective_compute("AllGather", ALU.bypass, replica_groups=[list(range(NCORE))],
                              ins=[agC_in.ap().opt()], outs=[agC_out.ap().opt()]).then_inc(ccC.h, 1)
    o.on("gpsimd", e_hm_g)

    sched = attn_schedule()

    groups = []
    pair_last_gi = {}
    _gi = 0
    for (g2, u, items) in sched:
        for it_ in items:
            kind, jt0, p0, n = it_
            groups.append(dict(kind=kind, jt0=jt0, p0=p0, n=n, g2=g2, u=u))
            _gi += 1
        pair_last_gi[(g2, u)] = _gi - 1
    NG = len(groups)

    NPAIR = len(sched)
    QBASE = [2 + 17 * sq for sq in range(4)]
    slot_te = [[] for _ in range(NPAIR)]
    slot_sc = [[] for _ in range(NPAIR)]
    slot_v = [[] for _ in range(NPAIR)]
    slot_sy = [[] for _ in range(NPAIR)]
    slot_v[1].append(("rh",))
    slot_v[2].append(("scd",))
    for sq in range(4):
        b = QBASE[sq]
        for it in range(NIC):
            slot_v[max(b + it // 3 - 1, 1)].append(("q", sq, it))
            slot_sy[max(b + it // 3 - 2, 1)].append(("hrl", sq, it))
        for hd in range(NDC):
            kk = sq * NDC + hd
            slot_te[b + 8 + hd].append(("dn", kk))
            slot_sc[b + 8 + hd].append(("scale", kk))
            slot_te[b + 9 + hd].append(("tr", kk))
            slot_v[b + 8 + hd].append(("add", kk))
            slot_sy[b + 7 + hd].append(("wd", kk))

    for i2, (g2, u, items) in enumerate(sched):
        for k, it_ in enumerate(items):
            groups[sum(len(s[2]) for s in sched[:i2]) + k]["i2"] = i2
            groups[sum(len(s[2]) for s in sched[:i2]) + k]["k"] = k
    T_SEQ = []
    S_SEQ = []
    V_SEQ = []
    pend_v = []
    _gi = 0
    backlog = []
    for i2, (g2, u, items) in enumerate(sched):
        for k, it_ in enumerate(items):
            S_SEQ.append(("erf", _gi))
            if it_[0] == "zone":
                V_SEQ.append(("ztmp", _gi))
            V_SEQ.append(("az", _gi))
            if k == 0:
                V_SEQ.extend(pend_v)
                pend_v = []
            T_SEQ.append(("sims", _gi))
            backlog.append(_gi)
            if len(backlog) > 3:
                T_SEQ.append(("pv", backlog.pop(0)))
            if k == 1:
                T_SEQ.extend(slot_te[i2])
            _gi += 1
        for op in slot_v[i2]:
            if op[0] == "add":
                V_SEQ.append(op)
        S_SEQ.extend(slot_sc[i2])
        pend_v.append(("acc", i2))
        pend_v.extend([op for op in slot_v[i2] if op[0] != "add"])
    while backlog:
        T_SEQ.append(("pv", backlog.pop(0)))
    V_SEQ.extend(pend_v)
    pair_of = {i2: (g2, u) for i2, (g2, u, _it) in enumerate(sched)}

    SV_F0 = sv.n
    VT_F0 = vt.n
    tv_sim = {}
    tv_av = {}
    tv_dn = {}
    tv_tr2 = {}
    sv_erf = {}
    sv_scale = {}
    vt_zb = {}
    vt_att = {}
    vt_acc = {}
    vt_add = {}
    vt_hq = {}
    VT_RH = VT_SCD = None
    for op in T_SEQ:
        if op[0] == "sims":
            tv_sim[op[1]] = tv.inc()
        elif op[0] == "pv":
            tv_av[op[1]] = tv.inc()
        elif op[0] == "dn":
            tv_dn[op[1]] = tv.inc()
        else:
            tv_tr2[op[1]] = tv.inc()
    for op in S_SEQ:
        if op[0] == "erf":
            sv_erf[op[1]] = sv.inc()
        else:
            sv_scale[op[1]] = sv.inc()
    for op in V_SEQ:
        if op[0] == "ztmp":
            vt_zb[op[1]] = vt.inc()
        elif op[0] == "az":
            vt_att[op[1]] = vt.inc()
        elif op[0] == "acc":
            vt_acc[pair_of[op[1]]] = vt.inc()
        elif op[0] == "rh":
            VT_RH = vt.inc()
        elif op[0] == "scd":
            VT_SCD = vt.inc()
        elif op[0] == "q":
            vt_hq[(op[1], op[2])] = vt.inc()
        else:
            vt_add[op[1]] = vt.inc()
    LK_MX8 = lk.inc(16)
    LK_SCR4 = lk.inc(16)
    LK_BCH = lk.inc(16)
    lhr = {}
    lk_wd = {}
    for i2 in range(NPAIR):
        for op in slot_sy[i2]:
            if op[0] == "hrl":
                lhr[(op[1], op[2])] = (lw if op[2] % 2 == 0 else lw2).inc(16)
            else:
                lk_wd[op[1]] = lk.inc(16)

    ps_free = {}
    for gidx in range(NG):
        gj = gidx - 3
        if gj < 0:
            ps_free[gidx] = ("sv", SV_F0)
        elif groups[gj]["kind"] == "far":
            ps_free[gidx] = ("sv", sv_erf[gj])
        else:
            ps_free[gidx] = ("vt", vt_zb[gj])
    pair_list = [(g2, u) for (g2, u, _) in sched]
    pa_free = {}
    for i2, pk in enumerate(pair_list):
        same = [p for p in pair_list[:i2] if p[1] % 2 == pk[1] % 2]
        pa_free[pk] = vt_acc[same[-1]] if same else VT_F0

    for g2 in range(1, 8):
        semv = lv1 if (g2 % 2) else lv0
        lv_ch[g2] = semv.inc(64)

    def f_v_g(gp):
        for g2 in range(1, 8):
            semv = lv1 if (g2 % 2) else lv0
            if g2 == 1:
                gp.wait_ge(tv.h, tv_gp[(NIC - 1, 1, 1)])
            if g2 >= 2:
                gp.wait_ge(vt.h, vt_acc[(g2 - 2, 15)])
            gp.wait_ge(ccA.h, 2 + g2 // 2)
            for ubit in range(2):
                gp.dma_start(
                    out=bass.AP(kbuf[g2 % 2].tensor, kbuf[g2 % 2].offset + ubit * 1024,
                                [[12288, 128], [128, 8], [1, 128]]),
                    in_=bass.AP(agAk_out, (2 * g2 + ubit) * 128,
                                [[ROWS, 128], [KPAY, 8], [1, 128]]),
                ).then_inc(semv.h, 16)
            for ubit in range(2):
                t = 2 * g2 + ubit
                gp.dma_start(
                    out=bass.AP(vbuf[g2 % 2].tensor, vbuf[g2 % 2].offset + ubit * 8192,
                                [[32768, 128], [1024, 8], [1, 1024]]),
                    in_=bass.AP(agAv_out[t // 4], (t % 4) * (128 * D),
                                [[D, 128], [VQ, 8], [1, D]]),
                ).then_inc(semv.h, 16)
    o.on("gpsimd", f_v_g)

    def fg_t(te):
        for op in T_SEQ:
            if op[0] == "sims":
                gidx = op[1]
                g = groups[gidx]
                g2, u, jt0, n = g["g2"], g["u"], g["jt0"], g["n"]
                if g["k"] == 0:
                    semv = lv1 if (g2 % 2) else lv0
                    te.wait_ge(semv.h, lv_ch[g2])
                w, val = ps_free[gidx]
                te.wait_ge((sv if w == "sv" else vt).h, val)
                for qq in range(n):
                    jt = jt0 + qq
                    inst = te.matmul(psSb[gidx % 3][:, qq * 128:(qq + 1) * 128],
                                     _sl(kbuf[g2 % 2], jt * 128, 128),
                                     q_bf[:, u * 128:(u + 1) * 128], start=True, stop=True)
                inst.then_inc(tv.h, 1)
            elif op[0] == "pv":
                gidx = op[1]
                g = groups[gidx]
                g2, u, jt0, n = g["g2"], g["u"], g["jt0"], g["n"]
                if g["k"] == 0:
                    te.wait_ge(vt.h, pa_free[(g2, u)])
                te.wait_ge(vt.h, vt_att[gidx])
                first_mm = (g["k"] == 0)
                for qq in range(n):
                    jt = jt0 + qq
                    for half in range(2):
                        inst = te.matmul(psA[u % 2][:, half * 512:half * 512 + 512],
                                         _sl(az[gidx % 3], qq * 128, 128),
                                         _sl(vbuf[g2 % 2], jt * 1024 + half * 512, 512),
                                         start=first_mm, stop=False, skip_group_check=True)
                    first_mm = False
                inst.then_inc(tv.h, 1)
            elif op[0] == "dn":
                kk = op[1]
                sq, hd = divmod(kk, NDC)
                te.wait_ge(lk.h, lk_wd[kk])
                te.wait_ge(vt.h, vt_hq[(sq, NIC - 1)])
                if kk == 0:
                    te.wait_ge(sv.h, SV_F0)
                else:
                    te.wait_ge(vt.h, vt_add[kk - 1])
                for it in range(NIC):
                    inst = te.matmul(psG[:, 0:512], _sl(wdb_s, it * 128, 128),
                                     _sl(hq, it * 512, 512),
                                     start=(it == 0), stop=(it == NIC - 1),
                                     skip_group_check=True)
                inst.then_inc(tv.h, 1)
            else:
                kk = op[1]
                te.wait_ge(sv.h, sv_scale[kk])
                for b in range(4):
                    inst = te.transpose(psG[:, b * 128:(b + 1) * 128],
                                        bass.AP(tmpA, b * 128, [[D, 128], [1, 128]]),
                                        ident[:])
                inst.then_inc(tv.h, 1)
    o.on("tensor", fg_t)

    def fg_s(sc):
        for op in S_SEQ:
            if op[0] == "erf":
                gidx = op[1]
                g = groups[gidx]
                n = g["n"]
                if gidx >= 3:
                    sc.wait_ge(vt.h, vt_att[gidx - 3])
                if g["kind"] == "zone":
                    sc.wait_ge(vt.h, vt_zb[gidx])
                    sc.activation(_sl(ez[gidx % 3], 0, n * 128), _sl(ztmp[gidx % 3], 0, n * 128),
                                  AF.Erf, bias=0.0, scale=1.0).then_inc(sv.h, 1)
                else:
                    sc.wait_ge(tv.h, tv_sim[gidx])
                    sc.activation(_sl(ez[gidx % 3], 0, n * 128), psSb[gidx % 3][:, 0:n * 128],
                                  AF.Erf, bias=farc_t[:],
                                  scale=S_IMM).then_inc(sv.h, 1)
            else:
                kk = op[1]
                hd = kk % NDC
                sc.wait_ge(tv.h, tv_dn[kk])
                sc.activation(foT, psG[:, 0:512], AF.Copy, bias=0.0,
                              scale=scd_t[hd][:]).then_inc(sv.h, 1)
    o.on("scalar", fg_s)

    def fg_v(v):
        v.wait_ge(lg.h, LG_HM)
        v.wait_ge(tv.h, tv_gp[(NIC - 1, 1, 1)])
        v.wait_ge(sv.h, SV_F0)
        for op in V_SEQ:
            if op[0] == "ztmp":
                gidx = op[1]
                g = groups[gidx]
                n, p0 = g["n"], g["p0"]
                v.wait_ge(tv.h, tv_sim[gidx])
                if gidx >= 3:
                    v.wait_ge(sv.h, sv_erf[gidx - 3])
                v.scalar_tensor_tensor(_sl(ztmp[gidx % 3], 0, n * 128),
                                       psSb[gidx % 3][:, 0:n * 128], S_IMM,
                                       zbias[:, p0 * 128:(p0 + n) * 128],
                                       ALU.mult, ALU.add).then_inc(vt.h, 1)
            elif op[0] == "az":
                gidx = op[1]
                g = groups[gidx]
                n, p0 = g["n"], g["p0"]
                v.wait_ge(sv.h, sv_erf[gidx])
                if gidx >= 3:
                    v.wait_ge(tv.h, tv_av[gidx - 3])
                if g["kind"] == "zone":
                    v.scalar_tensor_tensor(_sl(az[gidx % 3], 0, n * 128),
                                           _sl(ez[gidx % 3], 0, n * 128), 1.0,
                                           _sl(hmask, p0 * 128, n * 128),
                                           ALU.add, ALU.mult).then_inc(vt.h, 1)
                else:
                    v.tensor_scalar(_sl(az[gidx % 3], 0, n * 128), _sl(ez[gidx % 3], 0, n * 128),
                                    0.5, 0.5, ALU.mult, ALU.add).then_inc(vt.h, 1)
            elif op[0] == "acc":
                g2, u = pair_of[op[1]]
                v.wait_ge(tv.h, tv_av[pair_last_gi[(g2, u)]])
                v.tensor_tensor(acc[:, u * D:(u + 1) * D], acc[:, u * D:(u + 1) * D],
                                psA[u % 2][:, 0:1024], ALU.add).then_inc(vt.h, 1)
            elif op[0] == "rh":
                v.wait_ge(lk.h, LK_MX8)
                v.tensor_reduce(smalls[0:1, 5:6], mx8[0:1, 8:16], AX.X, ALU.max)
                v.drain()
                v.tensor_tensor(smalls[0:1, 6:7], smalls[0:1, 11:12],
                                smalls[0:1, 5:6], ALU.mult)
                v.drain()
                v.tensor_scalar(smalls[0:1, 6:7], smalls[0:1, 6:7],
                                1.0 / 127.0, None, ALU.mult)
                v.drain()
                v.reciprocal(smalls[0:1, 7:8], smalls[0:1, 6:7])
                v.drain()
                v.nop().then_inc(vt.h, 1)
            elif op[0] == "scd":
                v.wait_ge(lk.h, LK_BCH)
                v.drain()
                v.tensor_scalar(scol_d[:], rws_d[:], bc_h[:, 0:1], None, ALU.mult)
                v.drain()
                for hd in range(NDC):
                    v.tensor_copy(scd_t[hd][:], scol_d[:, hd:hd + 1])
                v.drain()
                v.nop().then_inc(vt.h, 1)
            elif op[0] == "q":
                sq, it = op[1], op[2]
                if sq == 0 and it == 0:
                    v.wait_ge(tv.h, TV_DQL)
                    v.wait_ge(lsc.h, lh_sp[NIC - 1])
                if sq > 0 and it == 0:
                    v.wait_ge(tv.h, tv_dn[sq * NDC - 1])
                v.wait_ge((lw if it % 2 == 0 else lw2).h, lhr[(sq, it)])
                v.tensor_scalar(qtmp, _sl(hrl[it % 2], 0, 512), bc_h[:, 1:2],
                                127.0, ALU.mult, ALU.min)
                v.tensor_scalar(qtmp, qtmp, -128.0,
                                cols[:, CP:CP + 1], ALU.max, ALU.add)
                v.tensor_scalar(_sl(hq, it * 512, 512), qtmp,
                                cols[:, CP:CP + 1], None, ALU.subtract).then_inc(vt.h, 1)
            else:
                kk = op[1]
                sq, hd = divmod(kk, NDC)
                v.wait_ge(tv.h, tv_tr2[kk])
                base = sq * 4 * D + hd * 128
                dst = bass.AP(acc, base, [[NU * D, 128], [D, 4], [1, 128]])
                v.tensor_tensor(dst, dst, psG[:, 0:512], ALU.add).then_inc(vt.h, 1)
    o.on("vector", fg_v)

    hrl_order = [(op[1], op[2]) for i2 in range(NPAIR)
                 for op in slot_sy[i2] if op[0] == "hrl"]

    def g_sync(sy):
        sy.wait_ge(lwq.h, LWQ_ALL0)
        sy.wait_ge(lwq2.h, LWQ_ALL1)
        sy.wait_ge(ccC.h, 1)
        sy.dma_start(out=mx8[0:1, 8:16],
                     in_=bass.AP(agC_out, 0, [[1, 1], [16, 8]]),
                     allow_slow_non_contiguous=True).then_inc(lk.h, 16)
        sy.wait_ge(vt.h, VT_RH)
        sy.dma_start(out=bass.AP(scr4, 4, [[4, 1], [1, 2]]),
                     in_=smalls[0:1, 6:8]).then_inc(lk.h, 16)
        sy.wait_ge(lk.h, LK_SCR4)
        sy.dma_start(out=bc_h[:], in_=bass.AP(scr4, 4, [[0, 128], [1, 2]])).then_inc(lk.h, 16)
        nhrl = 0
        for i2 in range(NPAIR):
            for op in slot_sy[i2]:
                if op[0] == "hrl":
                    sq, it = op[1], op[2]
                    if nhrl >= 2:
                        sy.wait_ge(vt.h, vt_hq[hrl_order[nhrl - 2]])
                    sy.wait_ge(lsc.h, lh_sp[it])
                    src_ = bass.AP(h_spill, it * 128 * ROWS + sq * 512,
                                   [[ROWS, 128], [1, 512]])
                    sy.dma_start(out=_sl(hrl[it % 2], 0, 512),
                                 in_=src_).then_inc((lw if it % 2 == 0 else lw2).h, 16)
                    nhrl += 1
                else:
                    kk = op[1]
                    sq, hd = divmod(kk, NDC)
                    if kk > 0:
                        sy.wait_ge(tv.h, tv_dn[kk - 1])
                    sy.dma_start(out=wdb_s,
                                 in_=bass.AP(wdT_hbm, hd * (NIC * 128 * 128),
                                             [[NIC * 128, 128], [1, NIC * 128]])
                                 ).then_inc(lk.h, 16)
    o.on("sync", g_sync)

    ST_ALL = st.inc(64 + (32 if DBG else 0))
    DBS_ALL = dbs.inc(112) if DBG else 0

    def g_out(sy):
        for sq in range(4):
            tgt = vt_add[sq * NDC + NDC - 1]
            for u in range(sq * 4, sq * 4 + 4):
                tgt = max(tgt, vt_acc[(u // 2, u)])
            sy.wait_ge(vt.h, tgt)
            src_ = bass.AP(acc, sq * 4 * D, [[NU * D, 128], [D, 4], [1, D]])
            dst = bass.AP(out_ext, sq * 4 * 128 * D, [[D, 128], [128 * D, 4], [1, D]])
            sy.dma_start(out=dst, in_=src_).then_inc(st.h, 16)
        sy.wait_ge(st.h, ST_ALL)
        if DBG:
            sy.wait_ge(dbs.h, DBS_ALL)
    o.on("sync", g_out)

    blk = nc.Block()
    with blk:
        o.emit(blk)
    return nc


_NC_CACHE = {}


def _get_nc(stage="all"):
    if stage not in _NC_CACHE:
        _NC_CACHE[stage] = build_program(stage)
    return _NC_CACHE[stage]


def _bv33(bv):
    b = np.zeros((33, 512), dtype=np.float32)
    v = np.asarray(bv, np.float32).reshape(1024)
    b[0] = v[:512]
    b[32] = v[512:]
    return b


def make_in_maps(inputs):
    x = np.ascontiguousarray(np.asarray(inputs["L0_attn_x"], np.float32).reshape(S, D))
    consts = host_constants()
    gm = np.asarray(inputs["L0_attn_gamma"], np.float32)
    bt = np.asarray(inputs["L0_attn_beta"], np.float32)
    shared = {
        "wqk": np.ascontiguousarray(np.asarray(inputs["L0_attn_Wqk"], np.float32)),
        "bqk": np.ascontiguousarray(np.asarray(inputs["L0_attn_bqk"], np.float32).reshape(E, 1)),
        "g0": np.ascontiguousarray(gm[0].reshape(E, 1)),
        "b0": np.ascontiguousarray(bt[0].reshape(E, 1)),
        "g1": np.ascontiguousarray(gm[1].reshape(E, 1)),
        "b1": np.ascontiguousarray(bt[1].reshape(E, 1)),
        "wv": np.ascontiguousarray(np.asarray(inputs["L0_attn_Wv"], np.float32)),
        "bv": _bv33(inputs["L0_attn_bv"]),
        "rel": np.ascontiguousarray(np.asarray(inputs["L0_attn_rel_emb"], np.float32).reshape(32, 1)),
        "clipx": np.ascontiguousarray(np.asarray(inputs["L0_ffn_clip_x"], np.float32).reshape(1, 1)),
        "clipd": np.ascontiguousarray(np.asarray(inputs["L0_ffn_clip_down"], np.float32).reshape(1, 1)),
        "wgate": np.ascontiguousarray(np.asarray(inputs["L0_ffn_w_gate"], np.float32)),
        "wup": np.ascontiguousarray(np.asarray(inputs["L0_ffn_w_up"], np.float32)),
        "wdown": np.ascontiguousarray(np.asarray(inputs["L0_ffn_w_down"], np.float32)),
    }
    shared.update(consts)
    in_maps = []
    for c in range(NCORE):
        tiles = [x[t * 128:(t + 1) * 128] for t in range(128) if t % 8 == c]
        zs, hm = percore_tables(c)
        m = dict(shared)
        m["x_loc"] = np.ascontiguousarray(np.concatenate(tiles, axis=0))
        m["zselc"] = zs
        m["hmaskc"] = hm.astype(ml_dtypes.bfloat16)
        in_maps.append(m)
    return in_maps


def kernel(**inputs):
    nc = _get_nc("all")
    in_maps = make_in_maps(inputs)
    res = run_bass_kernel_spmd(nc, in_maps, core_ids=list(range(NCORE)))
    out = np.empty((S, D), dtype=np.float32)
    for c in range(NCORE):
        ol = np.asarray(res.results[c]["out_loc"])
        for u in range(NU):
            out[(8 * u + c) * 128:(8 * u + c + 1) * 128] = ol[u * 128:(u + 1) * 128]
    return out.reshape(1, S, D)


if __name__ == "__main__":
    nc = _get_nc("all")
    try:
        print("built ok; instructions:", nc.n_instructions())
    except Exception:
        print("built ok")


# version 23
# speedup vs baseline: 1.0210x; 1.0210x over previous
import math
import numpy as np
import ml_dtypes
import concourse.bass as bass
import concourse.mybir as mybir
from concourse.bass_utils import run_bass_kernel_spmd

F32 = mybir.dt.float32
BF16 = mybir.dt.bfloat16
AF = mybir.ActivationFunctionType
ALU = mybir.AluOpType
AX = mybir.AxisListType

S, D, E, I = 16384, 1024, 128, 2816
NCORE = 8
NU = 16
ROWS = NU * 128
NIC = I // 128
NDC = D // 128

A_ERF = math.sqrt(2.0 / math.pi)
S_IMM = A_ERF / S
MU_A = math.sqrt(1.0 / math.pi)
SQ128A = math.sqrt(128.0) * A_ERF
C32 = float(1.5 * 2 ** 23)
C52 = float(np.float32(1.5 * 2 ** 52))
EPS = 1e-5

KPAY = 128 * ROWS
VPAY = ROWS * D
PAY = KPAY + VPAY
DBG = False


def _ones33():
    o = np.zeros((33, 128), dtype=np.float32)
    o[0] = 1.0
    o[32] = 1.0
    return o


def _bucket_table():
    NUM_BUCKETS, MAX_DISTANCE = 32, 128
    n = np.arange(256)
    max_exact = NUM_BUCKETS // 2
    is_small = n < max_exact
    nf = np.maximum(n, 1).astype(np.float32)
    val_if_large = max_exact + (
        np.log(nf / max_exact) / math.log(MAX_DISTANCE / max_exact) * (NUM_BUCKETS - max_exact)
    ).astype(np.int32)
    val_if_large = np.minimum(val_if_large, NUM_BUCKETS - 1)
    return np.where(is_small, n, val_if_large)


def host_constants():
    buck = _bucket_table()
    onehot = np.zeros((32, 384), dtype=np.float32)
    for d in range(256):
        onehot[buck[d], 128 + d] = SQ128A
    ident = np.eye(128, dtype=np.float32)
    return {"onehot": onehot, "identc": ident, "revc": ident[::-1].copy(),
            "ones1c": _ones33()}


def percore_tables(c):
    zsel = np.zeros((128, 27), dtype=np.float32)
    hmask = np.zeros((128, 9 * 128), dtype=np.float32)
    tri = np.triu(np.ones((128, 128), dtype=np.float32))
    for p in range(9):
        if p < c:
            zsel[:, 3 * p + 0] = 1.0
            hmask[:, p * 128:(p + 1) * 128] = 0.5
        elif p == c:
            zsel[:, 3 * p + 1] = 1.0
            hmask[:, p * 128:(p + 1) * 128] = 0.5
        elif p == c + 1:
            zsel[:, 3 * p + 2] = 1.0
            hmask[:, p * 128:(p + 1) * 128] = 0.5 * tri
        else:
            zsel[:, 3 * p + 0] = 1.0
    return zsel, hmask


def attn_schedule():
    sched = []
    for g2 in range(8):
        for u in range(2 * g2, 16):
            if u == 2 * g2:
                items = [("zone", 0, 1, 4), ("zone", 4, 5, 4)]
            elif u == 2 * g2 + 1:
                items = [("far", 0, None, 4), ("far", 4, None, 3),
                         ("zone", 7, 0, 4), ("zone", 11, 4, 4), ("zone", 15, 8, 1)]
            elif u == 2 * g2 + 2:
                items = [("far", 0, None, 4), ("far", 4, None, 4), ("far", 8, None, 4),
                         ("far", 12, None, 3), ("zone", 15, 0, 1)]
            else:
                items = [("far", 0, None, 4), ("far", 4, None, 4),
                         ("far", 8, None, 4), ("far", 12, None, 4)]
            sched.append((g2, u, items))
    return sched


class Orch:
    ENG = ("sync", "tensor", "vector", "scalar", "gpsimd")

    def __init__(self):
        self.q = {e: [] for e in self.ENG}

    def on(self, eng, fn):
        self.q[eng].append(fn)

    def emit(self, blk):
        for e in self.ENG:
            fns = self.q[e]

            def body(eng, fns=fns):
                for f in fns:
                    f(eng)
            getattr(blk, e)(body)


def build_program(stage="all"):
    nc = bass.Bass("TRN2", target_bir_lowering=False, detect_race_conditions=False)

    DIN = {}
    for name, shape in [
        ("x_loc", [ROWS, D]), ("wqk", [D, E]), ("bqk", [E, 1]), ("g0", [E, 1]),
        ("b0", [E, 1]), ("g1", [E, 1]), ("b1", [E, 1]), ("wv", [D, D]),
        ("bv", [33, 512]), ("rel", [32, 1]), ("clipx", [1, 1]), ("clipd", [1, 1]),
        ("wgate", [I, D]), ("wup", [I, D]), ("wdown", [D, I]),
        ("onehot", [32, 384]), ("identc", [128, 128]), ("revc", [128, 128]),
        ("ones1c", [33, 128]), ("zselc", [128, 27]),
    ]:
        DIN[name] = nc.dram_tensor(name, shape, F32, kind="ExternalInput")
    DIN["hmaskc"] = nc.dram_tensor("hmaskc", [128, 9 * 128], BF16, kind="ExternalInput")
    out_ext = nc.dram_tensor("out_loc", [ROWS, D], F32, kind="ExternalOutput")
    DB = {}
    if DBG:
        for nm, shape, ddt in [("dbg_q", [128, ROWS], BF16), ("dbg_k", [128, 2048], BF16),
                               ("dbg_xint", [128, 2048], BF16), ("dbg_h", [128, ROWS], F32),
                               ("dbg_wq", [1, 131072], BF16), ("dbg_wd", [1, 131072], BF16),
                               ("dbg_accF", [128, 1024], F32), ("dbg_cols", [128, 16], F32),
                               ("dbg_agA", [1, 4096], BF16)]:
            DB[nm] = nc.dram_tensor(nm, shape, ddt, kind="ExternalOutput")

    VQ = 4 * 128 * D
    agAk_in = nc.dram_tensor("agAk_in", [1, KPAY], BF16)
    agAk_out = nc.dram_tensor("agAk_out", [NCORE, KPAY], BF16)
    agAv_in = [nc.dram_tensor(f"agAv{j}_in", [1, VQ], BF16) for j in range(4)]
    agAv_out = [nc.dram_tensor(f"agAv{j}_out", [NCORE, VQ], BF16) for j in range(4)]
    agB_in = nc.dram_tensor("agB_in", [1, 16], F32)
    agB_out = nc.dram_tensor("agB_out", [NCORE, 16], F32)
    agC_in = nc.dram_tensor("agC_in", [1, 16], F32)
    agC_out = nc.dram_tensor("agC_out", [NCORE, 16], F32)
    buf384 = nc.dram_tensor("buf384", [1, 384], F32)
    scr4 = nc.dram_tensor("scr4", [2, 4], F32)
    h_spill = nc.dram_tensor("h_spill", [I, ROWS], F32)
    wq_hbm = nc.dram_tensor("wq_hbm", [2 * NIC * NDC * 128 * 128], BF16)
    wdT_hbm = nc.dram_tensor("wdT_hbm", [NDC * NIC * 128 * 128], BF16)

    o = Orch()
    _guards = []

    def SB(name, shape, dt=F32):
        g = nc.sbuf_tensor(name, shape, dt)
        h = g.__enter__()
        _guards.append(g)
        return h

    def PSUM(name, shape, dt=F32):
        g = nc.psum_tensor(name, shape, dt)
        h = g.__enter__()
        _guards.append(g)
        return h

    class Sem:
        def __init__(self, name):
            g = nc.semaphore(name)
            self.h = g.__enter__()
            _guards.append(g)
            self.n = 0

        def inc(self, k=1):
            self.n += k
            return self.n

    acc = SB("acc", [128, NU * D])
    q_bf = SB("q_bf", [128, ROWS], BF16)
    ident = SB("identt", [128, 128])
    ones1 = SB("ones1t", [33, 128])
    zsel = SB("zselt", [128, 27])
    zbias = SB("zbiast", [128, 9 * 128])
    bvsb = SB("bvsb", [33, 512])
    relsb = SB("relsb", [32, 1])
    cols = SB("colst", [128, 24])
    rws_g = SB("rws_g", [128, NIC])
    rws_u = SB("rws_u", [128, NIC])
    rws_d = SB("rws_d", [128, NDC])
    scol_g = SB("scol_g", [128, NIC])
    scol_u = SB("scol_u", [128, NIC])
    scol_d = SB("scol_d", [128, NDC])
    smalls = SB("smalls", [1, 16])
    mx8 = SB("mx8", [1, 16])
    agst = SB("agst", [1, 32])
    bc_x = SB("bc_x", [128, 2])
    bc_h = SB("bc_h", [128, 2])
    bqk_t = SB("bqk_t", [128, 1])
    farc_t = SB("farc_t", [128, 1])
    cp_t = SB("cp_t", [128, 1])
    ncp_t = SB("ncp_t", [128, 1])
    amx_t = SB("amx_t", [128, 1])
    amh_t = SB("amh_t", [128, 1])
    scg_t = [SB(f"scg{i}", [128, 1]) for i in range(NIC)]
    scd_t = [SB(f"scd{i}", [128, 1]) for i in range(NDC)]
    s_m = SB("s_m", [128, NU])
    s_e2 = SB("s_e2", [128, NU])
    s_var = SB("s_var", [128, NU])
    s_rs = SB("s_rs", [128, NU])
    s_nb = SB("s_nb", [128, NU])
    amx16 = SB("amx16", [128, NU])
    amh88 = SB("amh88", [128, 88])
    tmpA = SB("tmpA", [128, D])
    tmpB = SB("tmpB", [128, D])
    onehotsb = SB("onehott", [32, 384])

    CP, NCP, FLAG, RXC, RHC, BQK, G0c, B0c, G1c, B1c, FARC, AMX, AMH, SXC, SHC = range(15)
    T0, T1, T2, T4, T5, T6, T7 = 16, 17, 18, 20, 21, 22, 23

    ar1 = SB("ar1", [128, NU * D])
    ar1b = ar1.bitcast(BF16)
    ar2 = SB("ar2", [128, NDC * ROWS], BF16)
    ar4 = SB("ar4", [128, 6144])
    ar4b = ar4.bitcast(BF16)

    xnT = ar1
    xnTb = ar2
    ar2f = ar2.bitcast(F32)

    def A2F(off, n):
        return bass.AP(ar2f, off, [[8192, 128], [1, n]])

    def A2B(off, n):
        return bass.AP(ar2, off, [[16384, 128], [1, n]])

    def A4(off_f32, n, dt=F32):
        if dt is F32:
            return bass.AP(ar4, off_f32, [[6144, 128], [1, n]])
        return bass.AP(ar4b, off_f32, [[12288, 128], [1, n]])

    bufsb = bass.AP(ar4, 0, [[6144, 1], [1, 384]])
    buf241 = bass.AP(ar4, 241, [[6144, 1], [1, 1]])
    w0t = A4(512, 128)
    w1t = A4(640, 128)
    rev = A4(768, 128)
    b0t = A4(896, 128)
    b1t = A4(1024, 128)
    fart = A4(1152, 128)
    wvb = A4(0, NDC * D, BF16)
    wqkb = A4(8192, NDC * 128, BF16)
    krot = A4(9216, ROWS, BF16)
    tmpAb = tmpA.bitcast(BF16)
    vrot = [bass.AP(tmpAb, i * 1024, [[2048, 128], [1, D]]) for i in range(2)]
    qkf = None
    wstage = [A2F(i * 1024, D) for i in range(2)]
    hacc = [A2F(2048 + i * 2048, ROWS) for i in range(2)]
    wqf = [A2F(6144 + i * 512, 512) for i in range(2)]
    wdstage = [A4(2048, I), A2F(2048, I)]
    wqst = [A4(9728 + i * 512, 512, BF16) for i in range(2)]
    wqe = [A4(i * 1024, NDC * 128, BF16) for i in range(4)]
    silu_t = None
    az = [A4(i * 512, 512, BF16) for i in range(3)]
    ez = [A4(1536 + i * 512, 512, BF16) for i in range(3)]
    ztmp = [bass.AP(tmpB, 0, [[D, 128], [1, 512]]),
            bass.AP(tmpB, 512, [[D, 128], [1, 512]]), A4(1536, 512)]
    hmask = A2B(14336, 9 * 128)
    kbuf = [A4(4608 + i * 2048, 2048, BF16) for i in range(2)]
    hq = A2B(0, NIC * 512)
    hrl = [A2F(6144 + i * 512, 512) for i in range(2)]
    foT = bass.AP(tmpA, 0, [[D, 128], [1, 512]])
    qtmp = bass.AP(tmpA, 512, [[D, 128], [1, 512]])
    wdb_s = A4(8704, NIC * 128, BF16)

    def A1B(off, n):
        return bass.AP(ar1b, off, [[32768, 128], [1, n]])
    xintT = A1B(0, NDC * ROWS)
    vbuf = [A1B(16384, NU * 1024), A1B(0, NU * 1024)]

    def _sl(apx, off, n):
        return bass.AP(apx.tensor, apx.offset + off, [[apx.ap[0][0], 128], [1, n]])

    psA = [PSUM(f"psA{i}", [128, 1024]) for i in range(2)]
    psS_all = PSUM("psSall", [128, 1536])
    psG = PSUM("psG", [128, 512])

    class _PsAlias:
        def __init__(self, base, coff):
            self.base = base
            self.coff = coff

        def __getitem__(self, idx):
            ps, cs = idx
            return self.base[ps, cs.start + self.coff:cs.stop + self.coff]

    psS = [_PsAlias(psS_all, 0), _PsAlias(psS_all, 512)]
    psW = [_PsAlias(psS_all, 1024), _PsAlias(psG, 0)]
    psS_allb = psS_all.bitcast(BF16)
    psGb = psG.bitcast(BF16)
    psWb = [_PsAlias(psS_allb, 2048), _PsAlias(psGb, 0)]
    psSb = [_PsAlias(psS_all, 0), _PsAlias(psS_all, 512), _PsAlias(psS_all, 1024)]

    sem = {n: Sem(n) for n in
           ["ld", "lg", "lsc", "lk", "lv0", "lv1", "lw", "lwq", "lw2", "lwq2",
            "ccA", "ccB", "ccC", "tv", "vt", "sv", "pay", "st", "dbs"]}
    dbs = sem["dbs"]
    ld, lg, lsc, lk, lv0, lv1, lw, lwq = (sem[n] for n in
        ["ld", "lg", "lsc", "lk", "lv0", "lv1", "lw", "lwq"])
    lw2, lwq2 = sem["lw2"], sem["lwq2"]
    lwq_sem = [lwq, lwq2]
    ccA, ccB, ccC, tv, vt, sv, pay, st = (sem[n] for n in
        ["ccA", "ccB", "ccC", "tv", "vt", "sv", "pay", "st"])

    def ph_a(sy):
        for t, nm in [(ident, "identc"), (ones1, "ones1c"), (onehotsb, "onehot"),
                      (zsel, "zselc"), (relsb, "rel"), (bvsb, "bv")]:
            sy.dma_start(out=t[:], in_=DIN[nm].ap()).then_inc(ld.h, 16)
        sy.dma_start(out=rev, in_=DIN["revc"].ap()).then_inc(ld.h, 16)
        sy.dma_start(out=bqk_t[:], in_=DIN["bqk"].ap()).then_inc(ld.h, 16)
        for slot, nm in [(G0c, "g0"), (B0c, "b0"), (G1c, "g1"), (B1c, "b1")]:
            sy.dma_start(out=cols[:, slot:slot + 1], in_=DIN[nm].ap()).then_inc(ld.h, 16)
        sy.dma_start(out=smalls[0:1, 8:9], in_=DIN["clipx"].ap()).then_inc(ld.h, 16)
        sy.dma_start(out=smalls[0:1, 9:10], in_=DIN["clipd"].ap()).then_inc(ld.h, 16)
    o.on("sync", ph_a)
    ld.inc(16 * 14)
    LD_CONST = ld.n

    ld_x = [ld.inc(16) for _ in range(NU)]

    def ph_a_x(sy):
        for t in range(NU):
            src = bass.AP(DIN["x_loc"], t * 128 * D, [[D, 128], [1, D]])
            sy.dma_start(out=acc[:, t * D:(t + 1) * D], in_=src).then_inc(ld.h, 16)
    o.on("sync", ph_a_x)

    wvstg = [A4(4096, D), A4(5120, D)]
    lg_wv = [lg.inc(16) for _ in range(NDC)]
    LG_QKSTG = lg.inc(16)
    VT_CAST0 = [0] * NDC

    def ph_a_g(gp):
        for dc in range(NDC):
            if dc >= 2:
                gp.wait_ge(vt.h, VT_CAST0[dc - 2])
            gp.dma_start(out=wvstg[dc % 2],
                         in_=bass.AP(DIN["wv"], dc * 128 * D, [[D, 128], [1, D]])).then_inc(lg.h, 16)
        gp.wait_ge(vt.h, VT_CAST0[NDC - 1] - 0)
        gp.dma_start(out=wvstg[1],
                     in_=bass.AP(DIN["wqk"], 0, [[E, 128], [128 * E, NDC], [1, E]])).then_inc(lg.h, 16)
    o.on("gpsimd", ph_a_g)



    TV_BUF = tv.inc()

    def b_t1(te):
        te.wait_ge(ld.h, LD_CONST)
        te.matmul(psS[0][0:1, 0:384], relsb[:, 0:1], onehotsb[:],
                  start=True, stop=True).then_inc(tv.h, 1)
    o.on("tensor", b_t1)

    VT_CPREP = vt.inc()
    VT_BUFC = vt.inc()

    def b_v1(v):
        v.memset(cols[:, 0:24], 0.0)
        v.memset(smalls[0:1, 0:16], 0.0)
        v.memset(agst[0:1, 0:32], 0.0)
        v.memset(cols[:, FLAG:FLAG + 1], 1.0)
        v.drain()
        v.tensor_scalar(cols[:, T0:T0 + 1], cols[:, FLAG:FLAG + 1],
                        float(np.float32(2.0 ** -30)), None, ALU.add)
        v.drain()
        v.tensor_scalar(cols[:, T1:T1 + 1], cols[:, T0:T0 + 1], 1.0, None, ALU.subtract)
        v.drain()
        v.tensor_scalar(cols[:, FLAG:FLAG + 1], cols[:, T1:T1 + 1], 0.0, None, ALU.is_gt)
        v.drain()
        v.tensor_scalar(cols[:, T0:T0 + 1], cols[:, FLAG:FLAG + 1], C52, None, ALU.mult)
        v.tensor_scalar(cols[:, T1:T1 + 1], cols[:, FLAG:FLAG + 1], -1.0, None, ALU.subtract)
        v.drain()
        v.tensor_scalar(cols[:, T1:T1 + 1], cols[:, T1:T1 + 1], -C32, None, ALU.mult)
        v.drain()
        v.tensor_tensor(cols[:, CP:CP + 1], cols[:, T0:T0 + 1], cols[:, T1:T1 + 1], ALU.add)
        v.drain()
        v.tensor_scalar(cols[:, NCP:NCP + 1], cols[:, CP:CP + 1], -1.0, None, ALU.mult)
        v.tensor_copy(cp_t[:], cols[:, CP:CP + 1])
        v.drain()
        v.tensor_copy(ncp_t[:], cols[:, NCP:NCP + 1]).then_inc(vt.h, 1)
        v.wait_ge(tv.h, TV_BUF)
        v.tensor_scalar(bufsb, psS[0][0:1, 0:384], -MU_A, None, ALU.add).then_inc(vt.h, 1)
    o.on("vector", b_v1)

    LD_B384 = ld.inc(16)
    LD_W01 = ld.inc(32)

    def b_s1(sy):
        sy.wait_ge(vt.h, VT_BUFC)
        sy.dma_start(out=buf384.ap(), in_=bufsb).then_inc(ld.h, 16)
        sy.wait_ge(ld.h, LD_B384)
        sy.dma_start(out=w0t, in_=bass.AP(buf384, 1, [[1, 128], [1, 128]])).then_inc(ld.h, 16)
        sy.dma_start(out=w1t, in_=bass.AP(buf384, 129, [[1, 128], [1, 128]])).then_inc(ld.h, 16)
    o.on("sync", b_s1)

    TV_B01 = tv.inc()

    def b_t2(te):
        te.wait_ge(ld.h, LD_W01)
        te.matmul(psW[0][:, 0:128], rev, w0t, start=True, stop=True)
        te.matmul(psW[0][:, 128:256], rev, w1t, start=True, stop=True)
        te.matmul(psW[0][:, 256:257], ones1[0:1, :], buf241,
                  start=True, stop=True).then_inc(tv.h, 1)
    o.on("tensor", b_t2)

    VT_ZB = vt.inc()

    def b_v2(v):
        v.wait_ge(tv.h, TV_B01)
        v.tensor_copy(b0t, psW[0][:, 0:128])
        v.tensor_copy(b1t, psW[0][:, 128:256])
        v.tensor_copy(cols[:, FARC:FARC + 1], psW[0][:, 256:257])
        v.tensor_copy(farc_t[:], psW[0][:, 256:257])
        v.drain()
        v.tensor_scalar(fart, ident[:], 0.0, cols[:, FARC:FARC + 1], ALU.mult, ALU.add)
        v.drain()
        for p in range(9):
            dstp = zbias[:, p * 128:(p + 1) * 128]
            v.tensor_scalar(dstp, fart, zsel[:, 3 * p:3 * p + 1], None, ALU.mult)
            v.drain()
            v.scalar_tensor_tensor(tmpA[:, 0:128], b1t, zsel[:, 3 * p + 1:3 * p + 2],
                                   dstp, ALU.mult, ALU.add)
            v.drain()
            inst = v.scalar_tensor_tensor(dstp, b0t, zsel[:, 3 * p + 2:3 * p + 3],
                                          tmpA[:, 0:128], ALU.mult, ALU.add)
            v.drain()
        inst.then_inc(vt.h, 1)
    o.on("vector", b_v2)

    VT_CAST0_DECL = None

    def ph_a_cast_v(v):
        v.wait_ge(tv.h, TV_B01)
        for dc in range(NDC):
            v.wait_ge(lg.h, lg_wv[dc])
            v.tensor_copy(_sl(wvb, dc * D, D), wvstg[dc % 2]).then_inc(vt.h, 1)
        v.wait_ge(lg.h, LG_QKSTG)
        v.tensor_copy(wqkb, wvstg[1]).then_inc(vt.h, 1)
    o.on("vector", ph_a_cast_v)
    for dc in range(NDC):
        VT_CAST0[dc] = vt.inc()
    VT_WQKB = vt.inc()

    sv_sq = []
    for _ in range(NU):
        sv_sq.append(sv.inc())
    vt_red = []
    for _ in range(NU):
        vt_red.append(vt.inc())
    VT_VAR = vt.inc()
    SV_SD = sv.inc()
    VT_RS = vt.inc()
    vt_xn, vt_ev = [], []
    for _ in range(NU):
        vt_xn.append(vt.inc())
        vt_ev.append([vt.inc(), vt.inc()])
    VT_AMXR = vt.inc()
    tv_tr = []
    for _ in range(NU):
        tv_tr.append([tv.inc(), tv.inc()])

    def b_ln_s(sc):
        for t in range(NU):
            sc.wait_ge(ld.h, ld_x[t])
            if t == 0:
                sc.wait_ge(vt.h, VT_ZB)
            if t >= 2:
                sc.wait_ge(vt.h, vt_red[t - 2])
            xt = acc[:, t * D:(t + 1) * D]
            sc.activation(tmpA[:, 0:1024] if t % 2 == 0 else tmpB[:, 0:1024],
                          xt, AF.Square, bias=0.0, scale=1.0).then_inc(sv.h, 1)
        sc.wait_ge(vt.h, VT_VAR)
        sc.activation(s_var[:], s_var[:], AF.Sqrt, bias=0.0, scale=1.0).then_inc(sv.h, 1)
    o.on("scalar", b_ln_s)

    def b_ln_v(v):
        for t in range(NU):
            xt = acc[:, t * D:(t + 1) * D]
            v.wait_ge(ld.h, ld_x[t])
            v.tensor_reduce(s_m[:, t:t + 1], xt, AX.X, ALU.add)
            v.wait_ge(sv.h, sv_sq[t])
            v.tensor_reduce(s_e2[:, t:t + 1], tmpA[:, 0:1024] if t % 2 == 0 else tmpB[:, 0:1024],
                            AX.X, ALU.add).then_inc(vt.h, 1)
        v.drain()
        v.tensor_scalar(s_m[:], s_m[:], 1.0 / D, None, ALU.mult)
        v.tensor_scalar(s_e2[:], s_e2[:], 1.0 / D, None, ALU.mult)
        v.drain()
        v.tensor_tensor(s_var[:], s_m[:], s_m[:], ALU.mult)
        v.drain()
        v.tensor_tensor(s_var[:], s_e2[:], s_var[:], ALU.subtract)
        v.drain()
        v.tensor_scalar(s_var[:], s_var[:], EPS, None, ALU.add)
        v.drain()
        v.nop().then_inc(vt.h, 1)
        v.wait_ge(sv.h, SV_SD)
        v.reciprocal(s_rs[:], s_var[:])
        v.drain()
        v.tensor_tensor(s_nb[:], s_m[:], s_rs[:], ALU.mult)
        v.drain()
        v.tensor_scalar(s_nb[:], s_nb[:], -1.0, None, ALU.mult)
        v.drain()
        v.nop().then_inc(vt.h, 1)
        for t in range(NU):
            xt = acc[:, t * D:(t + 1) * D]
            v.tensor_scalar(tmpB[:, 0:1024], xt, s_rs[:, t:t + 1], s_nb[:, t:t + 1],
                            ALU.mult, ALU.add)
            v.tensor_reduce(amx16[:, t:t + 1], tmpB[:, 0:1024], AX.X, ALU.max,
                            apply_absolute_value=True).then_inc(vt.h, 1)
            for grp in range(2):
                v.wait_ge(tv.h, tv_tr[t][grp])
                dst = bass.AP(xnT, (4 * grp) * ROWS + t * 128,
                              [[NDC * ROWS, 128], [ROWS, 4], [1, 128]])
                v.tensor_copy(dst, psW[grp][:, 0:512]).then_inc(vt.h, 1)
        v.drain()
        v.tensor_reduce(amx_t[:], amx16[:], AX.X, ALU.max).then_inc(vt.h, 1)
    o.on("vector", b_ln_v)

    def b_ln_t(te):
        for t in range(NU):
            te.wait_ge(vt.h, vt_xn[t])
            for grp in range(2):
                if t == 0 and grp == 0:
                    te.wait_ge(vt.h, VT_ZB)
                if t > 0:
                    te.wait_ge(vt.h, vt_ev[t - 1][grp])
                for dc in range(4):
                    dcg = grp * 4 + dc
                    inst = te.transpose(psW[grp][:, dc * 128:(dc + 1) * 128],
                                        tmpB[:, dcg * 128:(dcg + 1) * 128], ident[:])
                inst.then_inc(tv.h, 1)
    o.on("tensor", b_ln_t)

    TV_AMX = tv.inc()

    def b_amax_t(te):
        te.wait_ge(vt.h, VT_AMXR)
        te.transpose(psS[1][0:1, 0:128], amx_t[:], ident[:]).then_inc(tv.h, 1)
    o.on("tensor", b_amax_t)

    VT_AMX = vt.inc()

    def b_amax_v(v):
        v.wait_ge(tv.h, TV_AMX)
        v.tensor_copy(tmpA[0:1, 0:128], psS[1][0:1, 0:128])
        v.drain()
        v.tensor_reduce(agst[0:1, 0:1], tmpA[0:1, 0:128], AX.X, ALU.max,
                        apply_absolute_value=True).then_inc(vt.h, 1)
    o.on("vector", b_amax_v)

    LG_AGB = lg.inc(16)
    ccB.inc()

    def b_amax_g(gp):
        gp.wait_ge(vt.h, VT_AMX)
        gp.dma_start(out=bass.AP(agB_in, 0, [[16, 1], [1, 16]]),
                     in_=agst[0:1, 0:16]).then_inc(lg.h, 16)
        gp.wait_ge(lg.h, LG_AGB)
        gp.collective_compute("AllGather", ALU.bypass, replica_groups=[list(range(NCORE))],
                              ins=[agB_in.ap().opt()], outs=[agB_out.ap().opt()]).then_inc(ccB.h, 1)
    o.on("gpsimd", b_amax_g)

    vt_cast = [vt.inc() for _ in range(NDC)]

    def c_cast_v(v):
        v.wait_ge(vt.h, vt_ev[NU - 1][1])
        for dc in range(NDC):
            v.tensor_copy(xnTb[:, dc * ROWS:(dc + 1) * ROWS],
                          xnT[:, dc * ROWS:(dc + 1) * ROWS]).then_inc(vt.h, 1)
    o.on("vector", c_cast_v)

    tv_qk = [tv.inc() for _ in range(4)]
    sv_qk = [sv.inc() for _ in range(4)]
    vt_qk = [vt.inc() for _ in range(4)]

    def c_qk_t(te):
        te.wait_ge(vt.h, VT_WQKB)
        te.wait_ge(vt.h, vt_cast[NDC - 1])
        te.wait_ge(vt.h, VT_AMX)
        for sc4 in range(4):
            if sc4 >= 2:
                te.wait_ge(sv.h, sv_qk[sc4 - 2])
            ps = psS[sc4 % 2]
            for dc in range(NDC):
                inst = te.matmul(ps[:, 0:512], _sl(wqkb, dc * 128, 128),
                                 xnTb[:, dc * ROWS + sc4 * 512: dc * ROWS + sc4 * 512 + 512],
                                 start=(dc == 0), stop=(dc == NDC - 1))
            inst.then_inc(tv.h, 1)
    o.on("tensor", c_qk_t)

    def c_qk_s(sc):
        for sc4 in range(4):
            sc.wait_ge(tv.h, tv_qk[sc4])
            if sc4 >= 1:
                sc.wait_ge(vt.h, vt_qk[sc4 - 1])
            sc.activation(tmpB[:, 0:512], psS[sc4 % 2][:, 0:512], AF.Silu,
                          bias=bqk_t[:], scale=1.0).then_inc(sv.h, 1)
    o.on("scalar", c_qk_s)

    def c_qk_v(v):
        for sc4 in range(4):
            v.wait_ge(sv.h, sv_qk[sc4])
            v.tensor_scalar(q_bf[:, sc4 * 512:(sc4 + 1) * 512], tmpB[:, 0:512],
                            cols[:, G0c:G0c + 1], cols[:, B0c:B0c + 1], ALU.mult, ALU.add)
            v.tensor_scalar(_sl(krot, sc4 * 512, 512), tmpB[:, 0:512],
                            cols[:, G1c:G1c + 1], cols[:, B1c:B1c + 1],
                            ALU.mult, ALU.add).then_inc(vt.h, 1)
    o.on("vector", c_qk_v)

    PAY_K = pay.inc(16)

    def c_k_pay(sy):
        sy.wait_ge(vt.h, vt_qk[3])
        sy.dma_start(out=bass.AP(agAk_in, 0, [[ROWS, 128], [1, ROWS]]),
                     in_=krot).then_inc(pay.h, 16)
    o.on("sync", c_k_pay)

    tv_v = [tv.inc() for _ in range(NU)]
    sv_v = [sv.inc() for _ in range(NU)]
    pay_v = [pay.inc(16) for _ in range(NU)]
    PAY_ALL = pay.n

    def c_v_t(te):
        te.wait_ge(vt.h, VT_WQKB)
        for t in range(NU):
            if t >= 2:
                te.wait_ge(sv.h, sv_v[t - 2])
            ps = psA[t % 2]
            for dc in range(NDC):
                for half in range(2):
                    te.matmul(ps[:, half * 512:half * 512 + 512],
                              xnTb[:, dc * ROWS + t * 128: dc * ROWS + t * 128 + 128],
                              _sl(wvb, dc * D + half * 512, 512),
                              start=(dc == 0), stop=False)
            for half in range(2):
                inst = te.matmul(ps[:, half * 512:half * 512 + 512],
                                 ones1[half * 32:half * 32 + 1, :],
                                 bvsb[half * 32:half * 32 + 1, 0:512], start=False, stop=True)
            inst.then_inc(tv.h, 1)
    o.on("tensor", c_v_t)

    def c_v_s(sc):
        for t in range(NU):
            sc.wait_ge(tv.h, tv_v[t])
            if t >= 2:
                sc.wait_ge(pay.h, pay_v[t - 2])
            sc.activation(vrot[t % 2], psA[t % 2][:, 0:1024], AF.Silu,
                          bias=0.0, scale=1.0).then_inc(sv.h, 1)
    o.on("scalar", c_v_s)

    def c_v_pay(sy):
        for t in range(NU):
            sy.wait_ge(sv.h, sv_v[t])
            sy.dma_start(out=bass.AP(agAv_in[t // 4], (t % 4) * (128 * D),
                                     [[D, 128], [1, D]]),
                         in_=vrot[t % 2]).then_inc(pay.h, 16)
    o.on("sync", c_v_pay)

    ccA.inc(5)

    def c_agA(gp):
        gp.wait_ge(pay.h, PAY_K)
        gp.collective_compute("AllGather", ALU.bypass, replica_groups=[list(range(NCORE))],
                              ins=[agAk_in.ap().opt()],
                              outs=[agAk_out.ap().opt()]).then_inc(ccA.h, 1)
        for j in range(4):
            gp.wait_ge(pay.h, pay_v[4 * j + 3])
            gp.collective_compute("AllGather", ALU.bypass,
                                  replica_groups=[list(range(NCORE))],
                                  ins=[agAv_in[j].ap().opt()],
                                  outs=[agAv_out[j].ap().opt()]).then_inc(ccA.h, 1)

    LG_MAXX = lg.inc(16)
    LG_BCX = lg.inc(32)

    def d_mx_g(gp):
        gp.wait_ge(ccB.h, 1)
        gp.dma_start(out=mx8[0:1, 0:8],
                     in_=bass.AP(agB_out, 0, [[1, 1], [16, 8]]),
                     allow_slow_non_contiguous=True).then_inc(lg.h, 16)
        gp.wait_ge(vt.h, VT_RX)
        gp.dma_start(out=bass.AP(scr4, 0, [[4, 1], [1, 2]]),
                     in_=smalls[0:1, 2:4]).then_inc(lg.h, 16)
        gp.wait_ge(lg.h, LG_BCX - 16)
        gp.dma_start(out=bc_x[:], in_=bass.AP(scr4, 0, [[0, 128], [1, 2]])).then_inc(lg.h, 16)
    o.on("gpsimd", d_mx_g)

    LG_HM = lg.inc(16)

    def f_hm_g(gp):
        gp.wait_ge(tv.h, tv_v[NU - 1])
        gp.dma_start(out=hmask, in_=DIN["hmaskc"].ap()).then_inc(lg.h, 16)
    o.on("gpsimd", f_hm_g)
    o.on("gpsimd", c_agA)

    SV_SIG = sv.inc()

    def d_sig_s(sc):
        sc.wait_ge(ld.h, LD_CONST)
        sc.activation(smalls[0:1, 10:11], smalls[0:1, 8:9], AF.Sigmoid, bias=0.0, scale=1.0)
        sc.activation(smalls[0:1, 11:12], smalls[0:1, 9:10], AF.Sigmoid,
                      bias=0.0, scale=1.0).then_inc(sv.h, 1)
    o.on("scalar", d_sig_s)

    VT_RX = vt.inc()

    def d_rx_v(v):
        v.wait_ge(lg.h, LG_MAXX)
        v.wait_ge(sv.h, SV_SIG)
        v.tensor_reduce(smalls[0:1, 1:2], mx8[0:1, 0:8], AX.X, ALU.max)
        v.drain()
        v.tensor_tensor(smalls[0:1, 2:3], smalls[0:1, 10:11], smalls[0:1, 1:2], ALU.mult)
        v.drain()
        v.tensor_scalar(smalls[0:1, 2:3], smalls[0:1, 2:3], 1.0 / 127.0, None, ALU.mult)
        v.drain()
        v.reciprocal(smalls[0:1, 3:4], smalls[0:1, 2:3])
        v.drain()
        v.nop().then_inc(vt.h, 1)
    o.on("vector", d_rx_v)

    vt_xq = [vt.inc() for _ in range(NDC)]

    def d_xq_v(v):
        v.wait_ge(lg.h, LG_BCX)
        for dc in range(NDC):
            for hf in range(2):
                src_ = xnT[:, dc * ROWS + hf * 1024: dc * ROWS + (hf + 1) * 1024]
                dst = _sl(xintT, dc * ROWS + hf * 1024, 1024)
                v.tensor_scalar(tmpB[:, 0:1024], src_, bc_x[:, 1:2], 127.0,
                                ALU.mult, ALU.min)
                v.tensor_scalar(tmpB[:, 0:1024], tmpB[:, 0:1024], -128.0, cols[:, CP:CP + 1],
                                ALU.max, ALU.add)
                inst = v.tensor_scalar(dst, tmpB[:, 0:1024], cols[:, CP:CP + 1], None,
                                       ALU.subtract)
            inst.then_inc(vt.h, 1)
    o.on("vector", d_xq_v)

    tmpBb = tmpB.bitcast(BF16)
    wqb = [bass.AP(tmpBb, i * 1024, [[2048, 128], [1, 1024]]) for i in range(2)]
    identb = A4(11264, 128, BF16)
    seqw = [(wi, it) for it in range(NIC) for wi in range(2)]
    lw_t = {}
    lw_sem = {}
    for _i, _k in enumerate(seqw):
        _s = lw if _i % 2 == 0 else lw2
        lw_t[_k] = _s.inc(16)
        lw_sem[_k] = _s
    vt_wdiag = {}
    for k in seqw:
        vt_wdiag[k] = vt.inc()
    tv_wq = {}
    for k in seqw:
        tv_wq[(k, 0)] = tv.inc()
        tv_wq[(k, 1)] = tv.inc()
    sv_wq = {}
    for k in seqw:
        sv_wq[(k, 0)] = sv.inc()
        sv_wq[(k, 1)] = sv.inc()
    lwq_st = {}
    for k in seqw:
        lwq_st[(k, 0)] = lwq.inc(16)
        lwq_st[(k, 1)] = lwq2.inc(16)

    def d_w_sync(sy):
        for idx, (wi, it) in enumerate(seqw):
            if idx == 0:
                sy.wait_ge(tv.h, tv_v[NU - 1])
            if idx >= 2:
                sy.wait_ge(tv.h, tv_wq[(seqw[idx - 2], 1)])
            win = DIN["wgate"] if wi == 0 else DIN["wup"]
            sy.dma_start(out=wstage[idx % 2],
                         in_=bass.AP(win, it * 128 * D,
                                     [[D, 128], [1, D]])).then_inc(lw_sem[(wi, it)].h, 16)
            if idx >= 1:
                pk = seqw[idx - 1]
                for grp in range(2):
                    sy.wait_ge(sv.h, sv_wq[(pk, grp)])
                    dst = bass.AP(wq_hbm, (pk[0] * NIC + pk[1]) * (NDC * 128 * 128) + grp * 4 * 16384,
                                  [[128, 128], [16384, 4], [1, 128]])
                    sy.dma_start(out=dst, in_=wqst[grp]).then_inc(lwq_sem[grp].h, 16)
        pk = seqw[-1]
        for grp in range(2):
            sy.wait_ge(sv.h, sv_wq[(pk, grp)])
            dst = bass.AP(wq_hbm, (pk[0] * NIC + pk[1]) * (NDC * 128 * 128) + grp * 4 * 16384,
                          [[128, 128], [16384, 4], [1, 128]])
            sy.dma_start(out=dst, in_=wqst[grp]).then_inc(lwq_sem[grp].h, 16)
    o.on("sync", d_w_sync)

    def d_w_v(v):
        for idx, k in enumerate(seqw):
            wi, it = k
            v.wait_ge(lw_sem[k].h, lw_t[k])
            if idx == 0:
                v.wait_ge(pay.h, PAY_ALL)
            rws = rws_g if wi == 0 else rws_u
            v.tensor_reduce(cols[:, T0:T0 + 1], wstage[idx % 2], AX.X, ALU.max,
                            apply_absolute_value=True)
            v.drain()
            v.tensor_scalar(rws[:, it:it + 1], cols[:, T0:T0 + 1], 1.0 / 127.0, None, ALU.mult)
            v.reciprocal(cols[:, T1:T1 + 1], cols[:, T0:T0 + 1])
            v.drain()
            v.tensor_scalar(cols[:, T1:T1 + 1], cols[:, T1:T1 + 1], 127.0, None, ALU.mult)
            v.drain()
            v.tensor_scalar(tmpB[:, 0:1024], wstage[idx % 2], cols[:, T1:T1 + 1],
                            127.0, ALU.mult, ALU.min)
            v.tensor_scalar(tmpB[:, 0:1024], tmpB[:, 0:1024], -128.0, cp_t[:],
                            ALU.max, ALU.add)
            v.tensor_scalar(wstage[idx % 2], tmpB[:, 0:1024], ncp_t[:], None,
                            ALU.add).then_inc(vt.h, 1)
    o.on("vector", d_w_v)

    def d_w_t(te):
        for idx, k in enumerate(seqw):
            te.wait_ge(vt.h, vt_wdiag[k])
            if idx == 0:
                te.wait_ge(vt.h, vt_ev[NU - 1][1])
            for grp in range(2):
                if idx >= 1:
                    te.wait_ge(sv.h, sv_wq[(seqw[idx - 1], grp)])
                for b in range(4):
                    hc = grp * 4 + b
                    inst = te.transpose(psW[grp][:, b * 128:(b + 1) * 128],
                                        _sl(wstage[idx % 2], hc * 128, 128),
                                        ident[:])
                inst.then_inc(tv.h, 1)
    o.on("tensor", d_w_t)

    def d_w_s(sc):
        for idx, k in enumerate(seqw):
            for grp in range(2):
                sc.wait_ge(tv.h, tv_wq[(k, grp)])
                if idx == 0 and grp == 0:
                    sc.wait_ge(pay.h, PAY_K)
                if idx >= 1:
                    sc.wait_ge(lwq_sem[grp].h, lwq_st[(seqw[idx - 1], grp)])
                sc.activation(wqst[grp], psW[grp][:, 0:512], AF.Copy, bias=0.0,
                              scale=1.0).then_inc(sv.h, 1)
    o.on("scalar", d_w_s)

    seqd = [(hd, gb) for hd in range(NDC) for gb in range(6)]
    lw_sem2 = [lw, lw2]
    lwd_t = {hd: lw_sem2[hd % 2].inc(16) for hd in range(NDC)}
    vt_dclip = {}
    for hd in range(NDC):
        for gb in range(6):
            vt_dclip[(hd, gb)] = vt.inc()
    tv_dq = {k: tv.inc() for k in seqd}
    sv_dq = {k: sv.inc() for k in seqd}
    lwd_st = {k: lwq_sem[k[1] % 2].inc(16) for k in seqd}
    LWQ_ALL0 = lwq.n
    LWQ_ALL1 = lwq2.n
    TV_DQL = tv_dq[(NDC - 1, 5)]
    SV_DCL = sv_dq[(NDC - 1, 5)]

    def d_wd_sync(sy):
        for h0 in range(2):
            sy.dma_start(out=wdstage[h0],
                         in_=bass.AP(DIN["wdown"], h0 * 128 * I,
                                     [[I, 128], [1, I]])).then_inc(lw_sem2[h0].h, 16)
        for hd in range(NDC):
            for gb in range(6):
                nb = 4 if gb < 5 else 2
                sy.wait_ge(sv.h, sv_dq[(hd, gb)])
                dst = bass.AP(wdT_hbm, hd * (NIC * 128 * 128) + gb * 4 * 16384,
                              [[128, 128], [16384, nb], [1, 128]])
                sy.dma_start(out=dst,
                             in_=_sl(wqst[gb % 2], 0, nb * 128)).then_inc(lwq_sem[gb % 2].h, 16)
            if hd < NDC - 2:
                sy.wait_ge(tv.h, tv_dq[(hd, 5)])
                sy.dma_start(out=wdstage[hd % 2],
                             in_=bass.AP(DIN["wdown"], (hd + 2) * 128 * I,
                                         [[I, 128], [1, I]])).then_inc(lw_sem2[hd % 2].h, 16)
    o.on("sync", d_wd_sync)

    def d_wd_v(v):
        for hd in range(NDC):
            v.wait_ge(lw_sem2[hd % 2].h, lwd_t[hd])
            v.tensor_reduce(cols[:, T0:T0 + 1], wdstage[hd % 2], AX.X, ALU.max,
                            apply_absolute_value=True)
            v.drain()
            v.tensor_scalar(rws_d[:, hd:hd + 1], cols[:, T0:T0 + 1], 1.0 / 127.0, None, ALU.mult)
            v.reciprocal(cols[:, T1:T1 + 1], cols[:, T0:T0 + 1])
            v.drain()
            v.tensor_scalar(cols[:, T1:T1 + 1], cols[:, T1:T1 + 1], 127.0, None, ALU.mult)
            v.drain()
            for gb in range(6):
                nb = 4 if gb < 5 else 2
                src_ = _sl(wdstage[hd % 2], gb * 512, nb * 128)
                v.tensor_scalar(tmpB[:, 0:nb * 128], src_, cols[:, T1:T1 + 1], 127.0,
                                ALU.mult, ALU.min)
                v.tensor_scalar(tmpB[:, 0:nb * 128], tmpB[:, 0:nb * 128], -128.0,
                                cp_t[:], ALU.max, ALU.add)
                v.tensor_scalar(src_, tmpB[:, 0:nb * 128], ncp_t[:], None,
                                ALU.add).then_inc(vt.h, 1)
    o.on("vector", d_wd_v)

    def d_wd_t(te):
        for hd in range(NDC):
            for gb in range(6):
                nb = 4 if gb < 5 else 2
                kk = hd * 6 + gb
                te.wait_ge(vt.h, vt_dclip[(hd, gb)])
                if kk < 2:
                    te.wait_ge(sv.h, sv_wq[(seqw[-1], kk % 2)])
                else:
                    te.wait_ge(sv.h, sv_dq[seqd[kk - 2]])
                for b in range(nb):
                    it = gb * 4 + b
                    inst = te.transpose(psW[gb % 2][:, b * 128:(b + 1) * 128],
                                        _sl(wdstage[hd % 2], it * 128, 128),
                                        ident[:])
                inst.then_inc(tv.h, 1)
    o.on("tensor", d_wd_t)

    def d_wd_s(sc):
        for hd in range(NDC):
            for gb in range(6):
                nb = 4 if gb < 5 else 2
                kk = hd * 6 + gb
                sc.wait_ge(tv.h, tv_dq[(hd, gb)])
                if kk < 2:
                    sc.wait_ge(lwq_sem[kk % 2].h, lwq_st[(seqw[-1], kk % 2)])
                else:
                    sc.wait_ge(lwq_sem[gb % 2].h, lwd_st[seqd[kk - 2]])
                sc.activation(_sl(wqst[gb % 2], 0, nb * 128), psW[gb % 2][:, 0:nb * 128],
                              AF.Copy, bias=0.0, scale=1.0).then_inc(sv.h, 1)
    o.on("scalar", d_wd_s)

    lwe_t = {}
    lv_ch = {}
    for it in range(NIC):
        for wi in range(2):
            lwe_t[(it, wi)] = lv0.inc(16)
        if it == 3:
            lv_ch[0] = lv0.inc(64)
    tv_gp = {}
    for it in range(NIC):
        for pr in range(2):
            tv_gp[(it, pr, 0)] = tv.inc()
            tv_gp[(it, pr, 1)] = tv.inc()
    sv_gu = {}
    for it in range(NIC):
        for sc4 in range(4):
            sv_gu[(it, sc4)] = sv.inc()
    VT_SCOL = vt.inc()
    vt_gu = {}
    for it in range(NIC):
        for sc4 in range(4):
            vt_gu[(it, sc4)] = vt.inc()
    lh_sp = {it: lsc.inc(16) for it in range(NIC)}

    def e_wq_ld(gp):
        for it in range(NIC):
            for wi in range(2):
                gp.wait_ge(lwq.h, lwq_st[((1, it), 0)])
                gp.wait_ge(lwq2.h, lwq_st[((1, it), 1)])
                if it >= 2 and wi == 0:
                    gp.wait_ge(vt.h, vt_gu[(it - 2, 3)])
                gp.dma_start(out=wqe[2 * (it % 2) + wi],
                             in_=bass.AP(wq_hbm, (wi * NIC + it) * (NDC * 128 * 128),
                                         [[NDC * 128, 128], [1, NDC * 128]])).then_inc(lv0.h, 16)
            if it == 3:
                gp.wait_ge(ccA.h, 2)
                gp.wait_ge(vt.h, vt_xq[NDC - 1])
                gp.wait_ge(tv.h, TV_DQL)
                for ubit in range(2):
                    gp.dma_start(
                        out=bass.AP(kbuf[0].tensor, kbuf[0].offset + ubit * 1024,
                                    [[12288, 128], [128, 8], [1, 128]]),
                        in_=bass.AP(agAk_out, ubit * 128,
                                    [[ROWS, 128], [KPAY, 8], [1, 128]]),
                    ).then_inc(lv0.h, 16)
                for ubit in range(2):
                    gp.dma_start(
                        out=bass.AP(vbuf[0].tensor, vbuf[0].offset + ubit * 8192,
                                    [[32768, 128], [1024, 8], [1, 1024]]),
                        in_=bass.AP(agAv_out[0], ubit * (128 * D),
                                    [[D, 128], [VQ, 8], [1, D]]),
                    ).then_inc(lv0.h, 16)
    o.on("gpsimd", e_wq_ld)

    _eset = [
        [psS[0][:, 0:512], psS[1][:, 0:512], psA[0][:, 0:512], psA[0][:, 512:1024]],
        [psW[0][:, 0:512], psW[1][:, 0:512], psA[1][:, 0:512], psA[1][:, 512:1024]],
    ]

    def e_gu_t(te):
        te.wait_ge(vt.h, vt_xq[NDC - 1])
        te.wait_ge(sv.h, sv_v[NU - 1])
        for it in range(NIC):
            te.wait_ge(lv0.h, lwe_t[(it, 1)])
            for pr in range(2):
                pi = it * 2 + pr
                bs = _eset[pi % 2]
                if pi == 1:
                    te.wait_ge(sv.h, SV_DCL)
                if pi >= 2:
                    pit, ppr = divmod(pi - 2, 2)
                    te.wait_ge(vt.h, vt_gu[(pit, 2 * ppr + 1)])
                for wi in range(2):
                    for dc in range(NDC):
                        te.matmul(bs[2 * wi + 0], _sl(wqe[2 * (it % 2) + wi], dc * 128, 128),
                                  _sl(xintT, dc * ROWS + (2 * pr) * 512, 512),
                                  start=(dc == 0), stop=(dc == NDC - 1))
                        inst = te.matmul(bs[2 * wi + 1],
                                         _sl(wqe[2 * (it % 2) + wi], dc * 128, 128),
                                         _sl(xintT, dc * ROWS + (2 * pr + 1) * 512, 512),
                                         start=(dc == 0), stop=(dc == NDC - 1))
                    inst.then_inc(tv.h, 1)
    o.on("tensor", e_gu_t)

    def e_scol_v(v):
        v.wait_ge(lg.h, LG_BCX)
        v.drain()
        v.tensor_scalar(scol_g[:], rws_g[:], bc_x[:, 0:1], None, ALU.mult)
        v.tensor_scalar(scol_u[:], rws_u[:], bc_x[:, 0:1], None, ALU.mult)
        v.drain()
        for it in range(NIC):
            v.tensor_copy(scg_t[it][:], scol_g[:, it:it + 1])
        v.drain()
        v.nop().then_inc(vt.h, 1)
    o.on("vector", e_scol_v)

    def e_gu_s(sc):
        sc.wait_ge(vt.h, VT_SCOL)
        for it in range(NIC):
            for sc4 in range(4):
                pr, j = divmod(sc4, 2)
                sc.wait_ge(tv.h, tv_gp[(it, pr, 0)])
                kk = it * 4 + sc4
                if kk >= 1:
                    pit, psc = divmod(kk - 1, 4)
                    sc.wait_ge(vt.h, vt_gu[(pit, psc)])
                sc.activation(tmpB[:, 512:1024], _eset[(it * 2 + pr) % 2][j], AF.Silu,
                              bias=0.0, scale=scg_t[it][:]).then_inc(sv.h, 1)
    o.on("scalar", e_gu_s)

    def e_gu_v(v):
        v.wait_ge(tv.h, TV_DQL)
        for it in range(NIC):
            for sc4 in range(4):
                pr, j = divmod(sc4, 2)
                v.wait_ge(sv.h, sv_gu[(it, sc4)])
                v.wait_ge(tv.h, tv_gp[(it, pr, 1)])
                if it >= 2 and sc4 == 0:
                    v.wait_ge(lsc.h, lh_sp[it - 2])
                dst = _sl(hacc[it % 2], sc4 * 512, 512)
                v.scalar_tensor_tensor(dst, _eset[(it * 2 + pr) % 2][2 + j],
                                       scol_u[:, it:it + 1], tmpB[:, 512:1024],
                                       ALU.mult, ALU.mult)
                v.tensor_reduce(amh88[:, it * 4 + sc4:it * 4 + sc4 + 1], dst, AX.X, ALU.max,
                                apply_absolute_value=True).then_inc(vt.h, 1)
    o.on("vector", e_gu_v)

    def e_h_sp(sy):
        for it in range(NIC):
            sy.wait_ge(vt.h, vt_gu[(it, 3)])
            dst = bass.AP(h_spill, it * 128 * ROWS, [[ROWS, 128], [1, ROWS]])
            sy.dma_start(out=dst, in_=hacc[it % 2]).then_inc(lsc.h, 16)
    o.on("sync", e_h_sp)

    VT_AMHR = vt.inc()

    def e_amh_v(v):
        v.drain()
        v.tensor_reduce(amh_t[:], amh88[:, 0:NIC * 4], AX.X, ALU.max).then_inc(vt.h, 1)
    o.on("vector", e_amh_v)

    TV_AMH = tv.inc()

    def e_hm_t(te):
        te.wait_ge(vt.h, VT_AMHR)
        te.transpose(psS[1][0:1, 0:128], amh_t[:], ident[:]).then_inc(tv.h, 1)
    o.on("tensor", e_hm_t)

    VT_AMH = vt.inc()

    def e_hm_v(v):
        v.wait_ge(tv.h, TV_AMH)
        v.tensor_copy(tmpA[0:1, 0:128], psS[1][0:1, 0:128])
        v.drain()
        v.tensor_reduce(agst[0:1, 16:17], tmpA[0:1, 0:128], AX.X, ALU.max,
                        apply_absolute_value=True).then_inc(vt.h, 1)
    o.on("vector", e_hm_v)

    LG_AGC = lg.inc(16)
    ccC.inc()

    def e_hm_g(gp):
        gp.wait_ge(vt.h, VT_AMH)
        gp.dma_start(out=bass.AP(agC_in, 0, [[16, 1], [1, 16]]),
                     in_=agst[0:1, 16:32]).then_inc(lg.h, 16)
        gp.wait_ge(lg.h, LG_AGC)
        gp.collective_compute("AllGather", ALU.bypass, replica_groups=[list(range(NCORE))],
                              ins=[agC_in.ap().opt()], outs=[agC_out.ap().opt()]).then_inc(ccC.h, 1)
    o.on("gpsimd", e_hm_g)

    sched = attn_schedule()

    groups = []
    pair_last_gi = {}
    _gi = 0
    for (g2, u, items) in sched:
        for it_ in items:
            kind, jt0, p0, n = it_
            groups.append(dict(kind=kind, jt0=jt0, p0=p0, n=n, g2=g2, u=u))
            _gi += 1
        pair_last_gi[(g2, u)] = _gi - 1
    NG = len(groups)

    NPAIR = len(sched)
    QBASE = [2 + 17 * sq for sq in range(4)]
    slot_te = [[] for _ in range(NPAIR)]
    slot_sc = [[] for _ in range(NPAIR)]
    slot_v = [[] for _ in range(NPAIR)]
    slot_sy = [[] for _ in range(NPAIR)]
    slot_v[1].append(("rh",))
    slot_v[2].append(("scd",))
    for sq in range(4):
        b = QBASE[sq]
        for it in range(NIC):
            slot_v[max(b + it // 3 - 1, 1)].append(("q", sq, it))
            slot_sy[max(b + it // 3 - 2, 1)].append(("hrl", sq, it))
        for hd in range(NDC):
            kk = sq * NDC + hd
            slot_te[b + 8 + hd].append(("dn", kk))
            slot_sc[b + 8 + hd].append(("scale", kk))
            slot_te[b + 9 + hd].append(("tr", kk))
            slot_v[b + 8 + hd].append(("add", kk))
            slot_sy[b + 7 + hd].append(("wd", kk))

    for i2, (g2, u, items) in enumerate(sched):
        for k, it_ in enumerate(items):
            groups[sum(len(s[2]) for s in sched[:i2]) + k]["i2"] = i2
            groups[sum(len(s[2]) for s in sched[:i2]) + k]["k"] = k
    T_SEQ = []
    S_SEQ = []
    V_SEQ = []
    pend_v = []
    _gi = 0
    backlog = []
    for i2, (g2, u, items) in enumerate(sched):
        for k, it_ in enumerate(items):
            S_SEQ.append(("erf", _gi))
            if it_[0] == "zone":
                V_SEQ.append(("ztmp", _gi))
            V_SEQ.append(("az", _gi))
            if k == 0:
                V_SEQ.extend(pend_v)
                pend_v = []
            T_SEQ.append(("sims", _gi))
            backlog.append(_gi)
            if len(backlog) > 2:
                T_SEQ.append(("pv", backlog.pop(0)))
            if k == 1:
                T_SEQ.extend(slot_te[i2])
            _gi += 1
        for op in slot_v[i2]:
            if op[0] == "add":
                V_SEQ.append(op)
        S_SEQ.extend(slot_sc[i2])
        pend_v.append(("acc", i2))
        pend_v.extend([op for op in slot_v[i2] if op[0] != "add"])
    while backlog:
        T_SEQ.append(("pv", backlog.pop(0)))
    V_SEQ.extend(pend_v)
    pair_of = {i2: (g2, u) for i2, (g2, u, _it) in enumerate(sched)}

    SV_F0 = sv.n
    VT_F0 = vt.n
    tv_sim = {}
    tv_av = {}
    tv_dn = {}
    tv_tr2 = {}
    sv_erf = {}
    sv_scale = {}
    vt_zb = {}
    vt_att = {}
    vt_acc = {}
    vt_add = {}
    vt_hq = {}
    VT_RH = VT_SCD = None
    for op in T_SEQ:
        if op[0] == "sims":
            tv_sim[op[1]] = tv.inc()
        elif op[0] == "pv":
            tv_av[op[1]] = tv.inc()
        elif op[0] == "dn":
            tv_dn[op[1]] = tv.inc()
        else:
            tv_tr2[op[1]] = tv.inc()
    for op in S_SEQ:
        if op[0] == "erf":
            sv_erf[op[1]] = sv.inc()
        else:
            sv_scale[op[1]] = sv.inc()
    for op in V_SEQ:
        if op[0] == "ztmp":
            vt_zb[op[1]] = vt.inc()
        elif op[0] == "az":
            vt_att[op[1]] = vt.inc()
        elif op[0] == "acc":
            vt_acc[pair_of[op[1]]] = vt.inc()
        elif op[0] == "rh":
            VT_RH = vt.inc()
        elif op[0] == "scd":
            VT_SCD = vt.inc()
        elif op[0] == "q":
            vt_hq[(op[1], op[2])] = vt.inc()
        else:
            vt_add[op[1]] = vt.inc()
    LK_MX8 = lk.inc(16)
    LK_SCR4 = lk.inc(16)
    LK_BCH = lk.inc(16)
    lhr = {}
    lk_wd = {}
    for i2 in range(NPAIR):
        for op in slot_sy[i2]:
            if op[0] == "hrl":
                lhr[(op[1], op[2])] = (lw if op[2] % 2 == 0 else lw2).inc(16)
            else:
                lk_wd[op[1]] = lk.inc(16)

    ps_free = {}
    for gidx in range(NG):
        gj = gidx - 3
        if gj < 0:
            ps_free[gidx] = ("sv", SV_F0)
        elif groups[gj]["kind"] == "far":
            ps_free[gidx] = ("sv", sv_erf[gj])
        else:
            ps_free[gidx] = ("vt", vt_zb[gj])
    pair_list = [(g2, u) for (g2, u, _) in sched]
    pa_free = {}
    for i2, pk in enumerate(pair_list):
        same = [p for p in pair_list[:i2] if p[1] % 2 == pk[1] % 2]
        pa_free[pk] = vt_acc[same[-1]] if same else VT_F0

    for g2 in range(1, 8):
        semv = lv1 if (g2 % 2) else lv0
        lv_ch[g2] = semv.inc(64)

    def f_v_g(gp):
        for g2 in range(1, 8):
            semv = lv1 if (g2 % 2) else lv0
            if g2 == 1:
                gp.wait_ge(tv.h, tv_gp[(NIC - 1, 1, 1)])
            if g2 >= 2:
                gp.wait_ge(vt.h, vt_acc[(g2 - 2, 15)])
            gp.wait_ge(ccA.h, 2 + g2 // 2)
            for ubit in range(2):
                gp.dma_start(
                    out=bass.AP(kbuf[g2 % 2].tensor, kbuf[g2 % 2].offset + ubit * 1024,
                                [[12288, 128], [128, 8], [1, 128]]),
                    in_=bass.AP(agAk_out, (2 * g2 + ubit) * 128,
                                [[ROWS, 128], [KPAY, 8], [1, 128]]),
                ).then_inc(semv.h, 16)
            for ubit in range(2):
                t = 2 * g2 + ubit
                gp.dma_start(
                    out=bass.AP(vbuf[g2 % 2].tensor, vbuf[g2 % 2].offset + ubit * 8192,
                                [[32768, 128], [1024, 8], [1, 1024]]),
                    in_=bass.AP(agAv_out[t // 4], (t % 4) * (128 * D),
                                [[D, 128], [VQ, 8], [1, D]]),
                ).then_inc(semv.h, 16)
    o.on("gpsimd", f_v_g)

    def fg_t(te):
        for op in T_SEQ:
            if op[0] == "sims":
                gidx = op[1]
                g = groups[gidx]
                g2, u, jt0, n = g["g2"], g["u"], g["jt0"], g["n"]
                if g["k"] == 0:
                    semv = lv1 if (g2 % 2) else lv0
                    te.wait_ge(semv.h, lv_ch[g2])
                w, val = ps_free[gidx]
                te.wait_ge((sv if w == "sv" else vt).h, val)
                for qq in range(n):
                    jt = jt0 + qq
                    inst = te.matmul(psSb[gidx % 3][:, qq * 128:(qq + 1) * 128],
                                     _sl(kbuf[g2 % 2], jt * 128, 128),
                                     q_bf[:, u * 128:(u + 1) * 128], start=True, stop=True)
                inst.then_inc(tv.h, 1)
            elif op[0] == "pv":
                gidx = op[1]
                g = groups[gidx]
                g2, u, jt0, n = g["g2"], g["u"], g["jt0"], g["n"]
                if g["k"] == 0:
                    te.wait_ge(vt.h, pa_free[(g2, u)])
                te.wait_ge(vt.h, vt_att[gidx])
                first_mm = (g["k"] == 0)
                for qq in range(n):
                    jt = jt0 + qq
                    for half in range(2):
                        inst = te.matmul(psA[u % 2][:, half * 512:half * 512 + 512],
                                         _sl(az[gidx % 3], qq * 128, 128),
                                         _sl(vbuf[g2 % 2], jt * 1024 + half * 512, 512),
                                         start=first_mm, stop=False, skip_group_check=True)
                    first_mm = False
                inst.then_inc(tv.h, 1)
            elif op[0] == "dn":
                kk = op[1]
                sq, hd = divmod(kk, NDC)
                te.wait_ge(lk.h, lk_wd[kk])
                te.wait_ge(vt.h, vt_hq[(sq, NIC - 1)])
                if kk == 0:
                    te.wait_ge(sv.h, SV_F0)
                else:
                    te.wait_ge(vt.h, vt_add[kk - 1])
                for it in range(NIC):
                    inst = te.matmul(psG[:, 0:512], _sl(wdb_s, it * 128, 128),
                                     _sl(hq, it * 512, 512),
                                     start=(it == 0), stop=(it == NIC - 1),
                                     skip_group_check=True)
                inst.then_inc(tv.h, 1)
            else:
                kk = op[1]
                te.wait_ge(sv.h, sv_scale[kk])
                for b in range(4):
                    inst = te.transpose(psG[:, b * 128:(b + 1) * 128],
                                        bass.AP(tmpA, b * 128, [[D, 128], [1, 128]]),
                                        ident[:])
                inst.then_inc(tv.h, 1)
    o.on("tensor", fg_t)

    def fg_s(sc):
        for op in S_SEQ:
            if op[0] == "erf":
                gidx = op[1]
                g = groups[gidx]
                n = g["n"]
                if gidx >= 3:
                    sc.wait_ge(vt.h, vt_att[gidx - 3])
                if g["kind"] == "zone":
                    sc.wait_ge(vt.h, vt_zb[gidx])
                    sc.activation(_sl(ez[gidx % 3], 0, n * 128), _sl(ztmp[gidx % 3], 0, n * 128),
                                  AF.Erf, bias=0.0, scale=1.0).then_inc(sv.h, 1)
                else:
                    sc.wait_ge(tv.h, tv_sim[gidx])
                    sc.activation(_sl(ez[gidx % 3], 0, n * 128), psSb[gidx % 3][:, 0:n * 128],
                                  AF.Erf, bias=farc_t[:],
                                  scale=S_IMM).then_inc(sv.h, 1)
            else:
                kk = op[1]
                hd = kk % NDC
                sc.wait_ge(tv.h, tv_dn[kk])
                sc.activation(foT, psG[:, 0:512], AF.Copy, bias=0.0,
                              scale=scd_t[hd][:]).then_inc(sv.h, 1)
    o.on("scalar", fg_s)

    def fg_v(v):
        v.wait_ge(lg.h, LG_HM)
        v.wait_ge(tv.h, tv_gp[(NIC - 1, 1, 1)])
        v.wait_ge(sv.h, SV_F0)
        for op in V_SEQ:
            if op[0] == "ztmp":
                gidx = op[1]
                g = groups[gidx]
                n, p0 = g["n"], g["p0"]
                v.wait_ge(tv.h, tv_sim[gidx])
                if gidx >= 3:
                    v.wait_ge(sv.h, sv_erf[gidx - 3])
                v.scalar_tensor_tensor(_sl(ztmp[gidx % 3], 0, n * 128),
                                       psSb[gidx % 3][:, 0:n * 128], S_IMM,
                                       zbias[:, p0 * 128:(p0 + n) * 128],
                                       ALU.mult, ALU.add).then_inc(vt.h, 1)
            elif op[0] == "az":
                gidx = op[1]
                g = groups[gidx]
                n, p0 = g["n"], g["p0"]
                v.wait_ge(sv.h, sv_erf[gidx])
                if gidx >= 3:
                    v.wait_ge(tv.h, tv_av[gidx - 3])
                if g["kind"] == "zone":
                    v.scalar_tensor_tensor(_sl(az[gidx % 3], 0, n * 128),
                                           _sl(ez[gidx % 3], 0, n * 128), 1.0,
                                           _sl(hmask, p0 * 128, n * 128),
                                           ALU.add, ALU.mult).then_inc(vt.h, 1)
                else:
                    v.tensor_scalar(_sl(az[gidx % 3], 0, n * 128), _sl(ez[gidx % 3], 0, n * 128),
                                    0.5, 0.5, ALU.mult, ALU.add).then_inc(vt.h, 1)
            elif op[0] == "acc":
                g2, u = pair_of[op[1]]
                v.wait_ge(tv.h, tv_av[pair_last_gi[(g2, u)]])
                v.tensor_tensor(acc[:, u * D:(u + 1) * D], acc[:, u * D:(u + 1) * D],
                                psA[u % 2][:, 0:1024], ALU.add).then_inc(vt.h, 1)
            elif op[0] == "rh":
                v.wait_ge(lk.h, LK_MX8)
                v.tensor_reduce(smalls[0:1, 5:6], mx8[0:1, 8:16], AX.X, ALU.max)
                v.drain()
                v.tensor_tensor(smalls[0:1, 6:7], smalls[0:1, 11:12],
                                smalls[0:1, 5:6], ALU.mult)
                v.drain()
                v.tensor_scalar(smalls[0:1, 6:7], smalls[0:1, 6:7],
                                1.0 / 127.0, None, ALU.mult)
                v.drain()
                v.reciprocal(smalls[0:1, 7:8], smalls[0:1, 6:7])
                v.drain()
                v.nop().then_inc(vt.h, 1)
            elif op[0] == "scd":
                v.wait_ge(lk.h, LK_BCH)
                v.drain()
                v.tensor_scalar(scol_d[:], rws_d[:], bc_h[:, 0:1], None, ALU.mult)
                v.drain()
                for hd in range(NDC):
                    v.tensor_copy(scd_t[hd][:], scol_d[:, hd:hd + 1])
                v.drain()
                v.nop().then_inc(vt.h, 1)
            elif op[0] == "q":
                sq, it = op[1], op[2]
                if sq == 0 and it == 0:
                    v.wait_ge(tv.h, TV_DQL)
                    v.wait_ge(lsc.h, lh_sp[NIC - 1])
                if sq > 0 and it == 0:
                    v.wait_ge(tv.h, tv_dn[sq * NDC - 1])
                v.wait_ge((lw if it % 2 == 0 else lw2).h, lhr[(sq, it)])
                v.tensor_scalar(qtmp, _sl(hrl[it % 2], 0, 512), bc_h[:, 1:2],
                                127.0, ALU.mult, ALU.min)
                v.tensor_scalar(qtmp, qtmp, -128.0,
                                cols[:, CP:CP + 1], ALU.max, ALU.add)
                v.tensor_scalar(_sl(hq, it * 512, 512), qtmp,
                                cols[:, CP:CP + 1], None, ALU.subtract).then_inc(vt.h, 1)
            else:
                kk = op[1]
                sq, hd = divmod(kk, NDC)
                v.wait_ge(tv.h, tv_tr2[kk])
                base = sq * 4 * D + hd * 128
                dst = bass.AP(acc, base, [[NU * D, 128], [D, 4], [1, 128]])
                v.tensor_tensor(dst, dst, psG[:, 0:512], ALU.add).then_inc(vt.h, 1)
    o.on("vector", fg_v)

    hrl_order = [(op[1], op[2]) for i2 in range(NPAIR)
                 for op in slot_sy[i2] if op[0] == "hrl"]

    def g_sync(sy):
        sy.wait_ge(lwq.h, LWQ_ALL0)
        sy.wait_ge(lwq2.h, LWQ_ALL1)
        sy.wait_ge(ccC.h, 1)
        sy.dma_start(out=mx8[0:1, 8:16],
                     in_=bass.AP(agC_out, 0, [[1, 1], [16, 8]]),
                     allow_slow_non_contiguous=True).then_inc(lk.h, 16)
        sy.wait_ge(vt.h, VT_RH)
        sy.dma_start(out=bass.AP(scr4, 4, [[4, 1], [1, 2]]),
                     in_=smalls[0:1, 6:8]).then_inc(lk.h, 16)
        sy.wait_ge(lk.h, LK_SCR4)
        sy.dma_start(out=bc_h[:], in_=bass.AP(scr4, 4, [[0, 128], [1, 2]])).then_inc(lk.h, 16)
        nhrl = 0
        for i2 in range(NPAIR):
            for op in slot_sy[i2]:
                if op[0] == "hrl":
                    sq, it = op[1], op[2]
                    if nhrl >= 2:
                        sy.wait_ge(vt.h, vt_hq[hrl_order[nhrl - 2]])
                    sy.wait_ge(lsc.h, lh_sp[it])
                    src_ = bass.AP(h_spill, it * 128 * ROWS + sq * 512,
                                   [[ROWS, 128], [1, 512]])
                    sy.dma_start(out=_sl(hrl[it % 2], 0, 512),
                                 in_=src_).then_inc((lw if it % 2 == 0 else lw2).h, 16)
                    nhrl += 1
                else:
                    kk = op[1]
                    sq, hd = divmod(kk, NDC)
                    if kk > 0:
                        sy.wait_ge(tv.h, tv_dn[kk - 1])
                    sy.dma_start(out=wdb_s,
                                 in_=bass.AP(wdT_hbm, hd * (NIC * 128 * 128),
                                             [[NIC * 128, 128], [1, NIC * 128]])
                                 ).then_inc(lk.h, 16)
    o.on("sync", g_sync)

    ST_ALL = st.inc(64 + (32 if DBG else 0))
    DBS_ALL = dbs.inc(112) if DBG else 0

    def g_out(sy):
        for sq in range(4):
            tgt = vt_add[sq * NDC + NDC - 1]
            for u in range(sq * 4, sq * 4 + 4):
                tgt = max(tgt, vt_acc[(u // 2, u)])
            sy.wait_ge(vt.h, tgt)
            src_ = bass.AP(acc, sq * 4 * D, [[NU * D, 128], [D, 4], [1, D]])
            dst = bass.AP(out_ext, sq * 4 * 128 * D, [[D, 128], [128 * D, 4], [1, D]])
            sy.dma_start(out=dst, in_=src_).then_inc(st.h, 16)
        sy.wait_ge(st.h, ST_ALL)
        if DBG:
            sy.wait_ge(dbs.h, DBS_ALL)
    o.on("sync", g_out)

    blk = nc.Block()
    with blk:
        o.emit(blk)
    return nc


_NC_CACHE = {}


def _get_nc(stage="all"):
    if stage not in _NC_CACHE:
        _NC_CACHE[stage] = build_program(stage)
    return _NC_CACHE[stage]


def _bv33(bv):
    b = np.zeros((33, 512), dtype=np.float32)
    v = np.asarray(bv, np.float32).reshape(1024)
    b[0] = v[:512]
    b[32] = v[512:]
    return b


def make_in_maps(inputs):
    x = np.ascontiguousarray(np.asarray(inputs["L0_attn_x"], np.float32).reshape(S, D))
    consts = host_constants()
    gm = np.asarray(inputs["L0_attn_gamma"], np.float32)
    bt = np.asarray(inputs["L0_attn_beta"], np.float32)
    shared = {
        "wqk": np.ascontiguousarray(np.asarray(inputs["L0_attn_Wqk"], np.float32)),
        "bqk": np.ascontiguousarray(np.asarray(inputs["L0_attn_bqk"], np.float32).reshape(E, 1)),
        "g0": np.ascontiguousarray(gm[0].reshape(E, 1)),
        "b0": np.ascontiguousarray(bt[0].reshape(E, 1)),
        "g1": np.ascontiguousarray(gm[1].reshape(E, 1)),
        "b1": np.ascontiguousarray(bt[1].reshape(E, 1)),
        "wv": np.ascontiguousarray(np.asarray(inputs["L0_attn_Wv"], np.float32)),
        "bv": _bv33(inputs["L0_attn_bv"]),
        "rel": np.ascontiguousarray(np.asarray(inputs["L0_attn_rel_emb"], np.float32).reshape(32, 1)),
        "clipx": np.ascontiguousarray(np.asarray(inputs["L0_ffn_clip_x"], np.float32).reshape(1, 1)),
        "clipd": np.ascontiguousarray(np.asarray(inputs["L0_ffn_clip_down"], np.float32).reshape(1, 1)),
        "wgate": np.ascontiguousarray(np.asarray(inputs["L0_ffn_w_gate"], np.float32)),
        "wup": np.ascontiguousarray(np.asarray(inputs["L0_ffn_w_up"], np.float32)),
        "wdown": np.ascontiguousarray(np.asarray(inputs["L0_ffn_w_down"], np.float32)),
    }
    shared.update(consts)
    in_maps = []
    for c in range(NCORE):
        tiles = [x[t * 128:(t + 1) * 128] for t in range(128) if t % 8 == c]
        zs, hm = percore_tables(c)
        m = dict(shared)
        m["x_loc"] = np.ascontiguousarray(np.concatenate(tiles, axis=0))
        m["zselc"] = zs
        m["hmaskc"] = hm.astype(ml_dtypes.bfloat16)
        in_maps.append(m)
    return in_maps


def kernel(**inputs):
    nc = _get_nc("all")
    in_maps = make_in_maps(inputs)
    res = run_bass_kernel_spmd(nc, in_maps, core_ids=list(range(NCORE)))
    out = np.empty((S, D), dtype=np.float32)
    for c in range(NCORE):
        ol = np.asarray(res.results[c]["out_loc"])
        for u in range(NU):
            out[(8 * u + c) * 128:(8 * u + c + 1) * 128] = ol[u * 128:(u + 1) * 128]
    return out.reshape(1, S, D)


if __name__ == "__main__":
    nc = _get_nc("all")
    try:
        print("built ok; instructions:", nc.n_instructions())
    except Exception:
        print("built ok")


# version 24
# speedup vs baseline: 1.0266x; 1.0054x over previous
import math
import numpy as np
import ml_dtypes
import concourse.bass as bass
import concourse.mybir as mybir
from concourse.bass_utils import run_bass_kernel_spmd

F32 = mybir.dt.float32
BF16 = mybir.dt.bfloat16
AF = mybir.ActivationFunctionType
ALU = mybir.AluOpType
AX = mybir.AxisListType

S, D, E, I = 16384, 1024, 128, 2816
NCORE = 8
NU = 16
ROWS = NU * 128
NIC = I // 128
NDC = D // 128

A_ERF = math.sqrt(2.0 / math.pi)
S_IMM = A_ERF / S
MU_A = math.sqrt(1.0 / math.pi)
SQ128A = math.sqrt(128.0) * A_ERF
C32 = float(1.5 * 2 ** 23)
C52 = float(np.float32(1.5 * 2 ** 52))
EPS = 1e-5

KPAY = 128 * ROWS
VPAY = ROWS * D
PAY = KPAY + VPAY
DBG = False


def _ones33():
    o = np.zeros((33, 128), dtype=np.float32)
    o[0] = 1.0
    o[32] = 1.0
    return o


def _bucket_table():
    NUM_BUCKETS, MAX_DISTANCE = 32, 128
    n = np.arange(256)
    max_exact = NUM_BUCKETS // 2
    is_small = n < max_exact
    nf = np.maximum(n, 1).astype(np.float32)
    val_if_large = max_exact + (
        np.log(nf / max_exact) / math.log(MAX_DISTANCE / max_exact) * (NUM_BUCKETS - max_exact)
    ).astype(np.int32)
    val_if_large = np.minimum(val_if_large, NUM_BUCKETS - 1)
    return np.where(is_small, n, val_if_large)


def host_constants():
    buck = _bucket_table()
    onehot = np.zeros((32, 384), dtype=np.float32)
    for d in range(256):
        onehot[buck[d], 128 + d] = SQ128A
    ident = np.eye(128, dtype=np.float32)
    return {"onehot": onehot, "identc": ident, "revc": ident[::-1].copy(),
            "ones1c": _ones33()}


def percore_tables(c):
    zsel = np.zeros((128, 27), dtype=np.float32)
    hmask = np.zeros((128, 9 * 128), dtype=np.float32)
    tri = np.triu(np.ones((128, 128), dtype=np.float32))
    for p in range(9):
        if p < c:
            zsel[:, 3 * p + 0] = 1.0
            hmask[:, p * 128:(p + 1) * 128] = 0.5
        elif p == c:
            zsel[:, 3 * p + 1] = 1.0
            hmask[:, p * 128:(p + 1) * 128] = 0.5
        elif p == c + 1:
            zsel[:, 3 * p + 2] = 1.0
            hmask[:, p * 128:(p + 1) * 128] = 0.5 * tri
        else:
            zsel[:, 3 * p + 0] = 1.0
    return zsel, hmask


def attn_schedule():
    sched = []
    for g2 in range(8):
        for u in range(2 * g2, 16):
            if u == 2 * g2:
                items = [("zone", 0, 1, 4), ("zone", 4, 5, 4)]
            elif u == 2 * g2 + 1:
                items = [("far", 0, None, 4), ("far", 4, None, 3),
                         ("zone", 7, 0, 4), ("zone", 11, 4, 4), ("zone", 15, 8, 1)]
            elif u == 2 * g2 + 2:
                items = [("far", 0, None, 4), ("far", 4, None, 4), ("far", 8, None, 4),
                         ("far", 12, None, 3), ("zone", 15, 0, 1)]
            else:
                items = [("far", 0, None, 4), ("far", 4, None, 4),
                         ("far", 8, None, 4), ("far", 12, None, 4)]
            sched.append((g2, u, items))
    return sched


class Orch:
    ENG = ("sync", "tensor", "vector", "scalar", "gpsimd")

    def __init__(self):
        self.q = {e: [] for e in self.ENG}

    def on(self, eng, fn):
        self.q[eng].append(fn)

    def emit(self, blk):
        for e in self.ENG:
            fns = self.q[e]

            def body(eng, fns=fns):
                for f in fns:
                    f(eng)
            getattr(blk, e)(body)


def build_program(stage="all"):
    nc = bass.Bass("TRN2", target_bir_lowering=False, detect_race_conditions=False)

    DIN = {}
    for name, shape in [
        ("x_loc", [ROWS, D]), ("wqk", [D, E]), ("bqk", [E, 1]), ("g0", [E, 1]),
        ("b0", [E, 1]), ("g1", [E, 1]), ("b1", [E, 1]), ("wv", [D, D]),
        ("bv", [33, 512]), ("rel", [32, 1]), ("clipx", [1, 1]), ("clipd", [1, 1]),
        ("wgate", [I, D]), ("wup", [I, D]), ("wdown", [D, I]),
        ("onehot", [32, 384]), ("identc", [128, 128]), ("revc", [128, 128]),
        ("ones1c", [33, 128]), ("zselc", [128, 27]),
    ]:
        DIN[name] = nc.dram_tensor(name, shape, F32, kind="ExternalInput")
    DIN["hmaskc"] = nc.dram_tensor("hmaskc", [128, 9 * 128], BF16, kind="ExternalInput")
    out_ext = nc.dram_tensor("out_loc", [ROWS, D], F32, kind="ExternalOutput")
    DB = {}
    if DBG:
        for nm, shape, ddt in [("dbg_q", [128, ROWS], BF16), ("dbg_k", [128, 2048], BF16),
                               ("dbg_xint", [128, 2048], BF16), ("dbg_h", [128, ROWS], F32),
                               ("dbg_wq", [1, 131072], BF16), ("dbg_wd", [1, 131072], BF16),
                               ("dbg_accF", [128, 1024], F32), ("dbg_cols", [128, 16], F32),
                               ("dbg_agA", [1, 4096], BF16)]:
            DB[nm] = nc.dram_tensor(nm, shape, ddt, kind="ExternalOutput")

    VQ = 4 * 128 * D
    agAk_in = nc.dram_tensor("agAk_in", [1, KPAY], BF16)
    agAk_out = nc.dram_tensor("agAk_out", [NCORE, KPAY], BF16)
    agAv_in = [nc.dram_tensor(f"agAv{j}_in", [1, VQ], BF16) for j in range(4)]
    agAv_out = [nc.dram_tensor(f"agAv{j}_out", [NCORE, VQ], BF16) for j in range(4)]
    agB_in = nc.dram_tensor("agB_in", [1, 16], F32)
    agB_out = nc.dram_tensor("agB_out", [NCORE, 16], F32)
    agC_in = nc.dram_tensor("agC_in", [1, 16], F32)
    agC_out = nc.dram_tensor("agC_out", [NCORE, 16], F32)
    buf384 = nc.dram_tensor("buf384", [1, 384], F32)
    scr4 = nc.dram_tensor("scr4", [2, 4], F32)
    h_spill = nc.dram_tensor("h_spill", [I, ROWS], F32)
    wq_hbm = nc.dram_tensor("wq_hbm", [2 * NIC * NDC * 128 * 128], BF16)
    wdT_hbm = nc.dram_tensor("wdT_hbm", [NDC * NIC * 128 * 128], BF16)

    o = Orch()
    _guards = []

    def SB(name, shape, dt=F32):
        g = nc.sbuf_tensor(name, shape, dt)
        h = g.__enter__()
        _guards.append(g)
        return h

    def PSUM(name, shape, dt=F32):
        g = nc.psum_tensor(name, shape, dt)
        h = g.__enter__()
        _guards.append(g)
        return h

    class Sem:
        def __init__(self, name):
            g = nc.semaphore(name)
            self.h = g.__enter__()
            _guards.append(g)
            self.n = 0

        def inc(self, k=1):
            self.n += k
            return self.n

    acc = SB("acc", [128, NU * D])
    q_bf = SB("q_bf", [128, ROWS], BF16)
    ident = SB("identt", [128, 128])
    ones1 = SB("ones1t", [33, 128])
    zsel = SB("zselt", [128, 27])
    zbias = SB("zbiast", [128, 9 * 128])
    bvsb = SB("bvsb", [33, 512])
    relsb = SB("relsb", [32, 1])
    cols = SB("colst", [128, 24])
    rws_g = SB("rws_g", [128, NIC])
    rws_u = SB("rws_u", [128, NIC])
    rws_d = SB("rws_d", [128, NDC])
    scol_g = SB("scol_g", [128, NIC])
    scol_u = SB("scol_u", [128, NIC])
    scol_d = SB("scol_d", [128, NDC])
    smalls = SB("smalls", [1, 16])
    mx8 = SB("mx8", [1, 16])
    agst = SB("agst", [1, 32])
    bc_x = SB("bc_x", [128, 2])
    bc_h = SB("bc_h", [128, 2])
    bqk_t = SB("bqk_t", [128, 1])
    farc_t = SB("farc_t", [128, 1])
    cp_t = SB("cp_t", [128, 1])
    ncp_t = SB("ncp_t", [128, 1])
    amx_t = SB("amx_t", [128, 1])
    amh_t = SB("amh_t", [128, 1])
    scg_t = [SB(f"scg{i}", [128, 1]) for i in range(NIC)]
    scd_t = [SB(f"scd{i}", [128, 1]) for i in range(NDC)]
    s_m = SB("s_m", [128, NU])
    s_e2 = SB("s_e2", [128, NU])
    s_var = SB("s_var", [128, NU])
    s_rs = SB("s_rs", [128, NU])
    s_nb = SB("s_nb", [128, NU])
    amx16 = SB("amx16", [128, NU])
    amh88 = SB("amh88", [128, 88])
    tmpA = SB("tmpA", [128, D])
    tmpB = SB("tmpB", [128, D])
    onehotsb = SB("onehott", [32, 384])

    CP, NCP, FLAG, RXC, RHC, BQK, G0c, B0c, G1c, B1c, FARC, AMX, AMH, SXC, SHC = range(15)
    T0, T1, T2, T4, T5, T6, T7 = 16, 17, 18, 20, 21, 22, 23

    ar1 = SB("ar1", [128, NU * D])
    ar1b = ar1.bitcast(BF16)
    ar2 = SB("ar2", [128, NDC * ROWS], BF16)
    ar4 = SB("ar4", [128, 6144])
    ar4b = ar4.bitcast(BF16)

    xnT = ar1
    xnTb = ar2
    ar2f = ar2.bitcast(F32)

    def A2F(off, n):
        return bass.AP(ar2f, off, [[8192, 128], [1, n]])

    def A2B(off, n):
        return bass.AP(ar2, off, [[16384, 128], [1, n]])

    def A4(off_f32, n, dt=F32):
        if dt is F32:
            return bass.AP(ar4, off_f32, [[6144, 128], [1, n]])
        return bass.AP(ar4b, off_f32, [[12288, 128], [1, n]])

    bufsb = bass.AP(ar4, 0, [[6144, 1], [1, 384]])
    buf241 = bass.AP(ar4, 241, [[6144, 1], [1, 1]])
    w0t = A4(512, 128)
    w1t = A4(640, 128)
    rev = A4(768, 128)
    b0t = A4(896, 128)
    b1t = A4(1024, 128)
    fart = A4(1152, 128)
    wvb = A4(0, NDC * D, BF16)
    wqkb = A4(8192, NDC * 128, BF16)
    krot = A4(9216, ROWS, BF16)
    tmpAb = tmpA.bitcast(BF16)
    vrot = [bass.AP(tmpAb, i * 1024, [[2048, 128], [1, D]]) for i in range(2)]
    qkf = None
    wstage = [A2F(0, D), A2F(1024, D), A2F(4864, D)]
    hacc = [A2F(2048 + i * 2048, ROWS) for i in range(2)]
    wqf = [A2F(6144 + i * 512, 512) for i in range(2)]
    wdstage = [A4(2048, I), A2F(2048, I)]
    wqst = [A4(9728 + i * 512, 512, BF16) for i in range(2)]
    wqe = [A4(i * 1024, NDC * 128, BF16) for i in range(4)]
    silu_t = None
    az = [A4(i * 512, 512, BF16) for i in range(3)]
    ez = [A4(1536 + i * 512, 512, BF16) for i in range(3)]
    ztmp = [bass.AP(tmpB, 0, [[D, 128], [1, 512]]),
            bass.AP(tmpB, 512, [[D, 128], [1, 512]]), A4(1536, 512)]
    hmask = A2B(14336, 9 * 128)
    kbuf = [A4(4608 + i * 2048, 2048, BF16) for i in range(2)]
    hq = A2B(0, NIC * 512)
    hrl = [A2F(6144 + i * 512, 512) for i in range(2)]
    foT = bass.AP(tmpA, 0, [[D, 128], [1, 512]])
    qtmp = bass.AP(tmpA, 512, [[D, 128], [1, 512]])
    wdb_s = A4(8704, NIC * 128, BF16)

    def A1B(off, n):
        return bass.AP(ar1b, off, [[32768, 128], [1, n]])
    xintT = A1B(0, NDC * ROWS)
    vbuf = [A1B(16384, NU * 1024), A1B(0, NU * 1024)]

    def _sl(apx, off, n):
        return bass.AP(apx.tensor, apx.offset + off, [[apx.ap[0][0], 128], [1, n]])

    psA = [PSUM(f"psA{i}", [128, 1024]) for i in range(2)]
    psS_all = PSUM("psSall", [128, 1536])
    psG = PSUM("psG", [128, 512])

    class _PsAlias:
        def __init__(self, base, coff):
            self.base = base
            self.coff = coff

        def __getitem__(self, idx):
            ps, cs = idx
            return self.base[ps, cs.start + self.coff:cs.stop + self.coff]

    psS = [_PsAlias(psS_all, 0), _PsAlias(psS_all, 512)]
    psW = [_PsAlias(psS_all, 1024), _PsAlias(psG, 0)]
    psS_allb = psS_all.bitcast(BF16)
    psGb = psG.bitcast(BF16)
    psWb = [_PsAlias(psS_allb, 2048), _PsAlias(psGb, 0)]
    psSb = [_PsAlias(psS_all, 0), _PsAlias(psS_all, 512), _PsAlias(psS_all, 1024)]

    sem = {n: Sem(n) for n in
           ["ld", "lg", "lsc", "lk", "lv0", "lv1", "lw", "lwq", "lw2", "lwq2",
            "lw3", "ccA", "ccB", "ccC", "tv", "vt", "sv", "pay", "st", "dbs"]}
    dbs = sem["dbs"]
    ld, lg, lsc, lk, lv0, lv1, lw, lwq = (sem[n] for n in
        ["ld", "lg", "lsc", "lk", "lv0", "lv1", "lw", "lwq"])
    lw2, lwq2 = sem["lw2"], sem["lwq2"]
    lw3 = sem["lw3"]
    lwq_sem = [lwq, lwq2]
    ccA, ccB, ccC, tv, vt, sv, pay, st = (sem[n] for n in
        ["ccA", "ccB", "ccC", "tv", "vt", "sv", "pay", "st"])

    def ph_a(sy):
        for t, nm in [(ident, "identc"), (ones1, "ones1c"), (onehotsb, "onehot"),
                      (zsel, "zselc"), (relsb, "rel"), (bvsb, "bv")]:
            sy.dma_start(out=t[:], in_=DIN[nm].ap()).then_inc(ld.h, 16)
        sy.dma_start(out=rev, in_=DIN["revc"].ap()).then_inc(ld.h, 16)
        sy.dma_start(out=bqk_t[:], in_=DIN["bqk"].ap()).then_inc(ld.h, 16)
        for slot, nm in [(G0c, "g0"), (B0c, "b0"), (G1c, "g1"), (B1c, "b1")]:
            sy.dma_start(out=cols[:, slot:slot + 1], in_=DIN[nm].ap()).then_inc(ld.h, 16)
        sy.dma_start(out=smalls[0:1, 8:9], in_=DIN["clipx"].ap()).then_inc(ld.h, 16)
        sy.dma_start(out=smalls[0:1, 9:10], in_=DIN["clipd"].ap()).then_inc(ld.h, 16)
    o.on("sync", ph_a)
    ld.inc(16 * 14)
    LD_CONST = ld.n

    ld_x = [ld.inc(16) for _ in range(NU)]

    def ph_a_x(sy):
        for t in range(NU):
            src = bass.AP(DIN["x_loc"], t * 128 * D, [[D, 128], [1, D]])
            sy.dma_start(out=acc[:, t * D:(t + 1) * D], in_=src).then_inc(ld.h, 16)
    o.on("sync", ph_a_x)

    wvstg = [A4(4096, D), A4(5120, D)]
    lg_wv = [lg.inc(16) for _ in range(NDC)]
    LG_QKSTG = lg.inc(16)
    VT_CAST0 = [0] * NDC

    def ph_a_g(gp):
        for dc in range(NDC):
            if dc >= 2:
                gp.wait_ge(vt.h, VT_CAST0[dc - 2])
            gp.dma_start(out=wvstg[dc % 2],
                         in_=bass.AP(DIN["wv"], dc * 128 * D, [[D, 128], [1, D]])).then_inc(lg.h, 16)
        gp.wait_ge(vt.h, VT_CAST0[NDC - 1] - 0)
        gp.dma_start(out=wvstg[1],
                     in_=bass.AP(DIN["wqk"], 0, [[E, 128], [128 * E, NDC], [1, E]])).then_inc(lg.h, 16)
    o.on("gpsimd", ph_a_g)



    TV_BUF = tv.inc()

    def b_t1(te):
        te.wait_ge(ld.h, LD_CONST)
        te.matmul(psS[0][0:1, 0:384], relsb[:, 0:1], onehotsb[:],
                  start=True, stop=True).then_inc(tv.h, 1)
    o.on("tensor", b_t1)

    VT_CPREP = vt.inc()
    VT_BUFC = vt.inc()

    def b_v1(v):
        v.memset(cols[:, 0:24], 0.0)
        v.memset(smalls[0:1, 0:16], 0.0)
        v.memset(agst[0:1, 0:32], 0.0)
        v.memset(cols[:, FLAG:FLAG + 1], 1.0)
        v.drain()
        v.tensor_scalar(cols[:, T0:T0 + 1], cols[:, FLAG:FLAG + 1],
                        float(np.float32(2.0 ** -30)), None, ALU.add)
        v.drain()
        v.tensor_scalar(cols[:, T1:T1 + 1], cols[:, T0:T0 + 1], 1.0, None, ALU.subtract)
        v.drain()
        v.tensor_scalar(cols[:, FLAG:FLAG + 1], cols[:, T1:T1 + 1], 0.0, None, ALU.is_gt)
        v.drain()
        v.tensor_scalar(cols[:, T0:T0 + 1], cols[:, FLAG:FLAG + 1], C52, None, ALU.mult)
        v.tensor_scalar(cols[:, T1:T1 + 1], cols[:, FLAG:FLAG + 1], -1.0, None, ALU.subtract)
        v.drain()
        v.tensor_scalar(cols[:, T1:T1 + 1], cols[:, T1:T1 + 1], -C32, None, ALU.mult)
        v.drain()
        v.tensor_tensor(cols[:, CP:CP + 1], cols[:, T0:T0 + 1], cols[:, T1:T1 + 1], ALU.add)
        v.drain()
        v.tensor_scalar(cols[:, NCP:NCP + 1], cols[:, CP:CP + 1], -1.0, None, ALU.mult)
        v.tensor_copy(cp_t[:], cols[:, CP:CP + 1])
        v.drain()
        v.tensor_copy(ncp_t[:], cols[:, NCP:NCP + 1]).then_inc(vt.h, 1)
        v.wait_ge(tv.h, TV_BUF)
        v.tensor_scalar(bufsb, psS[0][0:1, 0:384], -MU_A, None, ALU.add).then_inc(vt.h, 1)
    o.on("vector", b_v1)

    LD_B384 = ld.inc(16)
    LD_W01 = ld.inc(32)

    def b_s1(sy):
        sy.wait_ge(vt.h, VT_BUFC)
        sy.dma_start(out=buf384.ap(), in_=bufsb).then_inc(ld.h, 16)
        sy.wait_ge(ld.h, LD_B384)
        sy.dma_start(out=w0t, in_=bass.AP(buf384, 1, [[1, 128], [1, 128]])).then_inc(ld.h, 16)
        sy.dma_start(out=w1t, in_=bass.AP(buf384, 129, [[1, 128], [1, 128]])).then_inc(ld.h, 16)
    o.on("sync", b_s1)

    TV_B01 = tv.inc()

    def b_t2(te):
        te.wait_ge(ld.h, LD_W01)
        te.matmul(psW[0][:, 0:128], rev, w0t, start=True, stop=True)
        te.matmul(psW[0][:, 128:256], rev, w1t, start=True, stop=True)
        te.matmul(psW[0][:, 256:257], ones1[0:1, :], buf241,
                  start=True, stop=True).then_inc(tv.h, 1)
    o.on("tensor", b_t2)

    VT_ZB = vt.inc()

    def b_v2(v):
        v.wait_ge(tv.h, TV_B01)
        v.tensor_copy(b0t, psW[0][:, 0:128])
        v.tensor_copy(b1t, psW[0][:, 128:256])
        v.tensor_copy(cols[:, FARC:FARC + 1], psW[0][:, 256:257])
        v.tensor_copy(farc_t[:], psW[0][:, 256:257])
        v.drain()
        v.tensor_scalar(fart, ident[:], 0.0, cols[:, FARC:FARC + 1], ALU.mult, ALU.add)
        v.drain()
        for p in range(9):
            dstp = zbias[:, p * 128:(p + 1) * 128]
            v.tensor_scalar(dstp, fart, zsel[:, 3 * p:3 * p + 1], None, ALU.mult)
            v.drain()
            v.scalar_tensor_tensor(tmpA[:, 0:128], b1t, zsel[:, 3 * p + 1:3 * p + 2],
                                   dstp, ALU.mult, ALU.add)
            v.drain()
            inst = v.scalar_tensor_tensor(dstp, b0t, zsel[:, 3 * p + 2:3 * p + 3],
                                          tmpA[:, 0:128], ALU.mult, ALU.add)
            v.drain()
        inst.then_inc(vt.h, 1)
    o.on("vector", b_v2)

    VT_CAST0_DECL = None

    def ph_a_cast_v(v):
        v.wait_ge(tv.h, TV_B01)
        for dc in range(NDC):
            v.wait_ge(lg.h, lg_wv[dc])
            v.tensor_copy(_sl(wvb, dc * D, D), wvstg[dc % 2]).then_inc(vt.h, 1)
        v.wait_ge(lg.h, LG_QKSTG)
        v.tensor_copy(wqkb, wvstg[1]).then_inc(vt.h, 1)
    o.on("vector", ph_a_cast_v)
    for dc in range(NDC):
        VT_CAST0[dc] = vt.inc()
    VT_WQKB = vt.inc()

    sv_sq = []
    for _ in range(NU):
        sv_sq.append(sv.inc())
    vt_red = []
    for _ in range(NU):
        vt_red.append(vt.inc())
    VT_VAR = vt.inc()
    SV_SD = sv.inc()
    VT_RS = vt.inc()
    vt_xn, vt_ev = [], []
    for _ in range(NU):
        vt_xn.append(vt.inc())
        vt_ev.append([vt.inc(), vt.inc()])
    VT_AMXR = vt.inc()
    tv_tr = []
    for _ in range(NU):
        tv_tr.append([tv.inc(), tv.inc()])

    def b_ln_s(sc):
        for t in range(NU):
            sc.wait_ge(ld.h, ld_x[t])
            if t == 0:
                sc.wait_ge(vt.h, VT_ZB)
            if t >= 2:
                sc.wait_ge(vt.h, vt_red[t - 2])
            xt = acc[:, t * D:(t + 1) * D]
            sc.activation(tmpA[:, 0:1024] if t % 2 == 0 else tmpB[:, 0:1024],
                          xt, AF.Square, bias=0.0, scale=1.0).then_inc(sv.h, 1)
        sc.wait_ge(vt.h, VT_VAR)
        sc.activation(s_var[:], s_var[:], AF.Sqrt, bias=0.0, scale=1.0).then_inc(sv.h, 1)
    o.on("scalar", b_ln_s)

    def b_ln_v(v):
        for t in range(NU):
            xt = acc[:, t * D:(t + 1) * D]
            v.wait_ge(ld.h, ld_x[t])
            v.tensor_reduce(s_m[:, t:t + 1], xt, AX.X, ALU.add)
            v.wait_ge(sv.h, sv_sq[t])
            v.tensor_reduce(s_e2[:, t:t + 1], tmpA[:, 0:1024] if t % 2 == 0 else tmpB[:, 0:1024],
                            AX.X, ALU.add).then_inc(vt.h, 1)
        v.drain()
        v.tensor_scalar(s_m[:], s_m[:], 1.0 / D, None, ALU.mult)
        v.tensor_scalar(s_e2[:], s_e2[:], 1.0 / D, None, ALU.mult)
        v.drain()
        v.tensor_tensor(s_var[:], s_m[:], s_m[:], ALU.mult)
        v.drain()
        v.tensor_tensor(s_var[:], s_e2[:], s_var[:], ALU.subtract)
        v.drain()
        v.tensor_scalar(s_var[:], s_var[:], EPS, None, ALU.add)
        v.drain()
        v.nop().then_inc(vt.h, 1)
        v.wait_ge(sv.h, SV_SD)
        v.reciprocal(s_rs[:], s_var[:])
        v.drain()
        v.tensor_tensor(s_nb[:], s_m[:], s_rs[:], ALU.mult)
        v.drain()
        v.tensor_scalar(s_nb[:], s_nb[:], -1.0, None, ALU.mult)
        v.drain()
        v.nop().then_inc(vt.h, 1)
        for t in range(NU):
            xt = acc[:, t * D:(t + 1) * D]
            v.tensor_scalar(tmpB[:, 0:1024], xt, s_rs[:, t:t + 1], s_nb[:, t:t + 1],
                            ALU.mult, ALU.add)
            v.tensor_reduce(amx16[:, t:t + 1], tmpB[:, 0:1024], AX.X, ALU.max,
                            apply_absolute_value=True).then_inc(vt.h, 1)
            for grp in range(2):
                v.wait_ge(tv.h, tv_tr[t][grp])
                dst = bass.AP(xnT, (4 * grp) * ROWS + t * 128,
                              [[NDC * ROWS, 128], [ROWS, 4], [1, 128]])
                v.tensor_copy(dst, psW[grp][:, 0:512]).then_inc(vt.h, 1)
        v.drain()
        v.tensor_reduce(amx_t[:], amx16[:], AX.X, ALU.max).then_inc(vt.h, 1)
    o.on("vector", b_ln_v)

    def b_ln_t(te):
        for t in range(NU):
            te.wait_ge(vt.h, vt_xn[t])
            for grp in range(2):
                if t == 0 and grp == 0:
                    te.wait_ge(vt.h, VT_ZB)
                if t > 0:
                    te.wait_ge(vt.h, vt_ev[t - 1][grp])
                for dc in range(4):
                    dcg = grp * 4 + dc
                    inst = te.transpose(psW[grp][:, dc * 128:(dc + 1) * 128],
                                        tmpB[:, dcg * 128:(dcg + 1) * 128], ident[:])
                inst.then_inc(tv.h, 1)
    o.on("tensor", b_ln_t)

    TV_AMX = tv.inc()

    def b_amax_t(te):
        te.wait_ge(vt.h, VT_AMXR)
        te.transpose(psS[1][0:1, 0:128], amx_t[:], ident[:]).then_inc(tv.h, 1)
    o.on("tensor", b_amax_t)

    VT_AMX = vt.inc()

    def b_amax_v(v):
        v.wait_ge(tv.h, TV_AMX)
        v.tensor_copy(tmpA[0:1, 0:128], psS[1][0:1, 0:128])
        v.drain()
        v.tensor_reduce(agst[0:1, 0:1], tmpA[0:1, 0:128], AX.X, ALU.max,
                        apply_absolute_value=True).then_inc(vt.h, 1)
    o.on("vector", b_amax_v)

    LG_AGB = lg.inc(16)
    ccB.inc()

    def b_amax_g(gp):
        gp.wait_ge(vt.h, VT_AMX)
        gp.dma_start(out=bass.AP(agB_in, 0, [[16, 1], [1, 16]]),
                     in_=agst[0:1, 0:16]).then_inc(lg.h, 16)
        gp.wait_ge(lg.h, LG_AGB)
        gp.collective_compute("AllGather", ALU.bypass, replica_groups=[list(range(NCORE))],
                              ins=[agB_in.ap().opt()], outs=[agB_out.ap().opt()]).then_inc(ccB.h, 1)
    o.on("gpsimd", b_amax_g)

    vt_cast = [vt.inc() for _ in range(NDC)]

    def c_cast_v(v):
        v.wait_ge(vt.h, vt_ev[NU - 1][1])
        for dc in range(NDC):
            v.tensor_copy(xnTb[:, dc * ROWS:(dc + 1) * ROWS],
                          xnT[:, dc * ROWS:(dc + 1) * ROWS]).then_inc(vt.h, 1)
    o.on("vector", c_cast_v)

    tv_qk = [tv.inc() for _ in range(4)]
    sv_qk = [sv.inc() for _ in range(4)]
    vt_qk = [vt.inc() for _ in range(4)]

    def c_qk_t(te):
        te.wait_ge(vt.h, VT_WQKB)
        te.wait_ge(vt.h, vt_cast[NDC - 1])
        te.wait_ge(vt.h, VT_AMX)
        for sc4 in range(4):
            if sc4 >= 2:
                te.wait_ge(sv.h, sv_qk[sc4 - 2])
            ps = psS[sc4 % 2]
            for dc in range(NDC):
                inst = te.matmul(ps[:, 0:512], _sl(wqkb, dc * 128, 128),
                                 xnTb[:, dc * ROWS + sc4 * 512: dc * ROWS + sc4 * 512 + 512],
                                 start=(dc == 0), stop=(dc == NDC - 1))
            inst.then_inc(tv.h, 1)
    o.on("tensor", c_qk_t)

    def c_qk_s(sc):
        for sc4 in range(4):
            sc.wait_ge(tv.h, tv_qk[sc4])
            if sc4 >= 1:
                sc.wait_ge(vt.h, vt_qk[sc4 - 1])
            sc.activation(tmpB[:, 0:512], psS[sc4 % 2][:, 0:512], AF.Silu,
                          bias=bqk_t[:], scale=1.0).then_inc(sv.h, 1)
    o.on("scalar", c_qk_s)

    def c_qk_v(v):
        for sc4 in range(4):
            v.wait_ge(sv.h, sv_qk[sc4])
            v.tensor_scalar(q_bf[:, sc4 * 512:(sc4 + 1) * 512], tmpB[:, 0:512],
                            cols[:, G0c:G0c + 1], cols[:, B0c:B0c + 1], ALU.mult, ALU.add)
            v.tensor_scalar(_sl(krot, sc4 * 512, 512), tmpB[:, 0:512],
                            cols[:, G1c:G1c + 1], cols[:, B1c:B1c + 1],
                            ALU.mult, ALU.add).then_inc(vt.h, 1)
    o.on("vector", c_qk_v)

    PAY_K = pay.inc(16)

    def c_k_pay(sy):
        sy.wait_ge(vt.h, vt_qk[3])
        sy.dma_start(out=bass.AP(agAk_in, 0, [[ROWS, 128], [1, ROWS]]),
                     in_=krot).then_inc(pay.h, 16)
    o.on("sync", c_k_pay)

    tv_v = [tv.inc() for _ in range(NU)]
    sv_v = [sv.inc() for _ in range(NU)]
    pay_v = [pay.inc(16) for _ in range(NU)]
    PAY_ALL = pay.n

    def c_v_t(te):
        te.wait_ge(vt.h, VT_WQKB)
        for t in range(NU):
            if t >= 2:
                te.wait_ge(sv.h, sv_v[t - 2])
            ps = psA[t % 2]
            for dc in range(NDC):
                for half in range(2):
                    te.matmul(ps[:, half * 512:half * 512 + 512],
                              xnTb[:, dc * ROWS + t * 128: dc * ROWS + t * 128 + 128],
                              _sl(wvb, dc * D + half * 512, 512),
                              start=(dc == 0), stop=False)
            for half in range(2):
                inst = te.matmul(ps[:, half * 512:half * 512 + 512],
                                 ones1[half * 32:half * 32 + 1, :],
                                 bvsb[half * 32:half * 32 + 1, 0:512], start=False, stop=True)
            inst.then_inc(tv.h, 1)
    o.on("tensor", c_v_t)

    def c_v_s(sc):
        for t in range(NU):
            sc.wait_ge(tv.h, tv_v[t])
            if t >= 2:
                sc.wait_ge(pay.h, pay_v[t - 2])
            sc.activation(vrot[t % 2], psA[t % 2][:, 0:1024], AF.Silu,
                          bias=0.0, scale=1.0).then_inc(sv.h, 1)
    o.on("scalar", c_v_s)

    def c_v_pay(sy):
        for t in range(NU):
            sy.wait_ge(sv.h, sv_v[t])
            sy.dma_start(out=bass.AP(agAv_in[t // 4], (t % 4) * (128 * D),
                                     [[D, 128], [1, D]]),
                         in_=vrot[t % 2]).then_inc(pay.h, 16)
    o.on("sync", c_v_pay)

    ccA.inc(5)

    def c_agA(gp):
        gp.wait_ge(pay.h, PAY_K)
        gp.collective_compute("AllGather", ALU.bypass, replica_groups=[list(range(NCORE))],
                              ins=[agAk_in.ap().opt()],
                              outs=[agAk_out.ap().opt()]).then_inc(ccA.h, 1)
        for j in range(4):
            gp.wait_ge(pay.h, pay_v[4 * j + 3])
            gp.collective_compute("AllGather", ALU.bypass,
                                  replica_groups=[list(range(NCORE))],
                                  ins=[agAv_in[j].ap().opt()],
                                  outs=[agAv_out[j].ap().opt()]).then_inc(ccA.h, 1)

    LG_MAXX = lg.inc(16)
    LG_BCX = lg.inc(32)

    def d_mx_g(gp):
        gp.wait_ge(ccB.h, 1)
        gp.dma_start(out=mx8[0:1, 0:8],
                     in_=bass.AP(agB_out, 0, [[1, 1], [16, 8]]),
                     allow_slow_non_contiguous=True).then_inc(lg.h, 16)
        gp.wait_ge(vt.h, VT_RX)
        gp.dma_start(out=bass.AP(scr4, 0, [[4, 1], [1, 2]]),
                     in_=smalls[0:1, 2:4]).then_inc(lg.h, 16)
        gp.wait_ge(lg.h, LG_BCX - 16)
        gp.dma_start(out=bc_x[:], in_=bass.AP(scr4, 0, [[0, 128], [1, 2]])).then_inc(lg.h, 16)
    o.on("gpsimd", d_mx_g)

    LG_HM = lg.inc(16)

    def f_hm_g(gp):
        gp.wait_ge(tv.h, tv_v[NU - 1])
        gp.dma_start(out=hmask, in_=DIN["hmaskc"].ap()).then_inc(lg.h, 16)
    o.on("gpsimd", f_hm_g)
    o.on("gpsimd", c_agA)

    SV_SIG = sv.inc()

    def d_sig_s(sc):
        sc.wait_ge(ld.h, LD_CONST)
        sc.activation(smalls[0:1, 10:11], smalls[0:1, 8:9], AF.Sigmoid, bias=0.0, scale=1.0)
        sc.activation(smalls[0:1, 11:12], smalls[0:1, 9:10], AF.Sigmoid,
                      bias=0.0, scale=1.0).then_inc(sv.h, 1)
    o.on("scalar", d_sig_s)

    VT_RX = vt.inc()

    def d_rx_v(v):
        v.wait_ge(lg.h, LG_MAXX)
        v.wait_ge(sv.h, SV_SIG)
        v.tensor_reduce(smalls[0:1, 1:2], mx8[0:1, 0:8], AX.X, ALU.max)
        v.drain()
        v.tensor_tensor(smalls[0:1, 2:3], smalls[0:1, 10:11], smalls[0:1, 1:2], ALU.mult)
        v.drain()
        v.tensor_scalar(smalls[0:1, 2:3], smalls[0:1, 2:3], 1.0 / 127.0, None, ALU.mult)
        v.drain()
        v.reciprocal(smalls[0:1, 3:4], smalls[0:1, 2:3])
        v.drain()
        v.nop().then_inc(vt.h, 1)
    o.on("vector", d_rx_v)

    vt_xq = [vt.inc() for _ in range(NDC)]

    def d_xq_v(v):
        v.wait_ge(lg.h, LG_BCX)
        for dc in range(NDC):
            for hf in range(2):
                src_ = xnT[:, dc * ROWS + hf * 1024: dc * ROWS + (hf + 1) * 1024]
                dst = _sl(xintT, dc * ROWS + hf * 1024, 1024)
                v.tensor_scalar(tmpB[:, 0:1024], src_, bc_x[:, 1:2], 127.0,
                                ALU.mult, ALU.min)
                v.tensor_scalar(tmpB[:, 0:1024], tmpB[:, 0:1024], -128.0, cols[:, CP:CP + 1],
                                ALU.max, ALU.add)
                inst = v.tensor_scalar(dst, tmpB[:, 0:1024], cols[:, CP:CP + 1], None,
                                       ALU.subtract)
            inst.then_inc(vt.h, 1)
    o.on("vector", d_xq_v)

    tmpBb = tmpB.bitcast(BF16)
    wqb = [bass.AP(tmpBb, i * 1024, [[2048, 128], [1, 1024]]) for i in range(2)]
    identb = A4(11264, 128, BF16)
    seqw = [(wi, it) for it in range(NIC) for wi in range(2)]
    lw_t = {}
    lw_sem = {}
    for _i, _k in enumerate(seqw):
        _s = [lw, lw2, lw3][_i % 3]
        lw_t[_k] = _s.inc(16)
        lw_sem[_k] = _s
    vt_wdiag = {}
    for k in seqw:
        vt_wdiag[k] = vt.inc()
    tv_wq = {}
    for k in seqw:
        tv_wq[(k, 0)] = tv.inc()
        tv_wq[(k, 1)] = tv.inc()
    sv_wq = {}
    for k in seqw:
        sv_wq[(k, 0)] = sv.inc()
        sv_wq[(k, 1)] = sv.inc()
    lwq_st = {}
    for k in seqw:
        lwq_st[(k, 0)] = lwq.inc(16)
        lwq_st[(k, 1)] = lwq2.inc(16)

    def d_w_sync(sy):
        for idx, (wi, it) in enumerate(seqw):
            if idx == 0:
                sy.wait_ge(tv.h, tv_v[NU - 1])
            if idx >= 3:
                sy.wait_ge(tv.h, tv_wq[(seqw[idx - 3], 1)])
            win = DIN["wgate"] if wi == 0 else DIN["wup"]
            sy.dma_start(out=wstage[idx % 3],
                         in_=bass.AP(win, it * 128 * D,
                                     [[D, 128], [1, D]])).then_inc(lw_sem[(wi, it)].h, 16)
            if idx >= 1:
                pk = seqw[idx - 1]
                for grp in range(2):
                    sy.wait_ge(sv.h, sv_wq[(pk, grp)])
                    dst = bass.AP(wq_hbm, (pk[0] * NIC + pk[1]) * (NDC * 128 * 128) + grp * 4 * 16384,
                                  [[128, 128], [16384, 4], [1, 128]])
                    sy.dma_start(out=dst, in_=wqst[grp]).then_inc(lwq_sem[grp].h, 16)
        pk = seqw[-1]
        for grp in range(2):
            sy.wait_ge(sv.h, sv_wq[(pk, grp)])
            dst = bass.AP(wq_hbm, (pk[0] * NIC + pk[1]) * (NDC * 128 * 128) + grp * 4 * 16384,
                          [[128, 128], [16384, 4], [1, 128]])
            sy.dma_start(out=dst, in_=wqst[grp]).then_inc(lwq_sem[grp].h, 16)
    o.on("sync", d_w_sync)

    def d_w_v(v):
        for idx, k in enumerate(seqw):
            wi, it = k
            v.wait_ge(lw_sem[k].h, lw_t[k])
            if idx == 0:
                v.wait_ge(pay.h, PAY_ALL)
            rws = rws_g if wi == 0 else rws_u
            v.tensor_reduce(cols[:, T0:T0 + 1], wstage[idx % 3], AX.X, ALU.max,
                            apply_absolute_value=True)
            v.drain()
            v.tensor_scalar(rws[:, it:it + 1], cols[:, T0:T0 + 1], 1.0 / 127.0, None, ALU.mult)
            v.reciprocal(cols[:, T1:T1 + 1], cols[:, T0:T0 + 1])
            v.drain()
            v.tensor_scalar(cols[:, T1:T1 + 1], cols[:, T1:T1 + 1], 127.0, None, ALU.mult)
            v.drain()
            v.tensor_scalar(tmpB[:, 0:1024], wstage[idx % 3], cols[:, T1:T1 + 1],
                            127.0, ALU.mult, ALU.min)
            v.tensor_scalar(tmpB[:, 0:1024], tmpB[:, 0:1024], -128.0, cp_t[:],
                            ALU.max, ALU.add)
            v.tensor_scalar(wstage[idx % 3], tmpB[:, 0:1024], ncp_t[:], None,
                            ALU.add).then_inc(vt.h, 1)
    o.on("vector", d_w_v)

    def d_w_t(te):
        for idx, k in enumerate(seqw):
            te.wait_ge(vt.h, vt_wdiag[k])
            if idx == 0:
                te.wait_ge(vt.h, vt_ev[NU - 1][1])
            for grp in range(2):
                if idx >= 1:
                    te.wait_ge(sv.h, sv_wq[(seqw[idx - 1], grp)])
                for b in range(4):
                    hc = grp * 4 + b
                    inst = te.transpose(psW[grp][:, b * 128:(b + 1) * 128],
                                        _sl(wstage[idx % 3], hc * 128, 128),
                                        ident[:])
                inst.then_inc(tv.h, 1)
    o.on("tensor", d_w_t)

    def d_w_s(sc):
        for idx, k in enumerate(seqw):
            for grp in range(2):
                sc.wait_ge(tv.h, tv_wq[(k, grp)])
                if idx == 0 and grp == 0:
                    sc.wait_ge(pay.h, PAY_K)
                if idx >= 1:
                    sc.wait_ge(lwq_sem[grp].h, lwq_st[(seqw[idx - 1], grp)])
                sc.activation(wqst[grp], psW[grp][:, 0:512], AF.Copy, bias=0.0,
                              scale=1.0).then_inc(sv.h, 1)
    o.on("scalar", d_w_s)

    seqd = [(hd, gb) for hd in range(NDC) for gb in range(6)]
    lw_sem2 = [lw, lw2]
    lwd_t = {hd: lw_sem2[hd % 2].inc(16) for hd in range(NDC)}
    vt_dclip = {}
    for hd in range(NDC):
        for gb in range(6):
            vt_dclip[(hd, gb)] = vt.inc()
    tv_dq = {k: tv.inc() for k in seqd}
    sv_dq = {k: sv.inc() for k in seqd}
    lwd_st = {k: lwq_sem[k[1] % 2].inc(16) for k in seqd}
    LWQ_ALL0 = lwq.n
    LWQ_ALL1 = lwq2.n
    TV_DQL = tv_dq[(NDC - 1, 5)]
    SV_DCL = sv_dq[(NDC - 1, 5)]

    def d_wd_sync(sy):
        for h0 in range(2):
            sy.dma_start(out=wdstage[h0],
                         in_=bass.AP(DIN["wdown"], h0 * 128 * I,
                                     [[I, 128], [1, I]])).then_inc(lw_sem2[h0].h, 16)
        for hd in range(NDC):
            for gb in range(6):
                nb = 4 if gb < 5 else 2
                sy.wait_ge(sv.h, sv_dq[(hd, gb)])
                dst = bass.AP(wdT_hbm, hd * (NIC * 128 * 128) + gb * 4 * 16384,
                              [[128, 128], [16384, nb], [1, 128]])
                sy.dma_start(out=dst,
                             in_=_sl(wqst[gb % 2], 0, nb * 128)).then_inc(lwq_sem[gb % 2].h, 16)
            if hd < NDC - 2:
                sy.wait_ge(tv.h, tv_dq[(hd, 5)])
                sy.dma_start(out=wdstage[hd % 2],
                             in_=bass.AP(DIN["wdown"], (hd + 2) * 128 * I,
                                         [[I, 128], [1, I]])).then_inc(lw_sem2[hd % 2].h, 16)
    o.on("sync", d_wd_sync)

    def d_wd_v(v):
        for hd in range(NDC):
            v.wait_ge(lw_sem2[hd % 2].h, lwd_t[hd])
            v.tensor_reduce(cols[:, T0:T0 + 1], wdstage[hd % 2], AX.X, ALU.max,
                            apply_absolute_value=True)
            v.drain()
            v.tensor_scalar(rws_d[:, hd:hd + 1], cols[:, T0:T0 + 1], 1.0 / 127.0, None, ALU.mult)
            v.reciprocal(cols[:, T1:T1 + 1], cols[:, T0:T0 + 1])
            v.drain()
            v.tensor_scalar(cols[:, T1:T1 + 1], cols[:, T1:T1 + 1], 127.0, None, ALU.mult)
            v.drain()
            for gb in range(6):
                nb = 4 if gb < 5 else 2
                src_ = _sl(wdstage[hd % 2], gb * 512, nb * 128)
                v.tensor_scalar(tmpB[:, 0:nb * 128], src_, cols[:, T1:T1 + 1], 127.0,
                                ALU.mult, ALU.min)
                v.tensor_scalar(tmpB[:, 0:nb * 128], tmpB[:, 0:nb * 128], -128.0,
                                cp_t[:], ALU.max, ALU.add)
                v.tensor_scalar(src_, tmpB[:, 0:nb * 128], ncp_t[:], None,
                                ALU.add).then_inc(vt.h, 1)
    o.on("vector", d_wd_v)

    def d_wd_t(te):
        for hd in range(NDC):
            for gb in range(6):
                nb = 4 if gb < 5 else 2
                kk = hd * 6 + gb
                te.wait_ge(vt.h, vt_dclip[(hd, gb)])
                if kk < 2:
                    te.wait_ge(sv.h, sv_wq[(seqw[-1], kk % 2)])
                else:
                    te.wait_ge(sv.h, sv_dq[seqd[kk - 2]])
                for b in range(nb):
                    it = gb * 4 + b
                    inst = te.transpose(psW[gb % 2][:, b * 128:(b + 1) * 128],
                                        _sl(wdstage[hd % 2], it * 128, 128),
                                        ident[:])
                inst.then_inc(tv.h, 1)
    o.on("tensor", d_wd_t)

    def d_wd_s(sc):
        for hd in range(NDC):
            for gb in range(6):
                nb = 4 if gb < 5 else 2
                kk = hd * 6 + gb
                sc.wait_ge(tv.h, tv_dq[(hd, gb)])
                if kk < 2:
                    sc.wait_ge(lwq_sem[kk % 2].h, lwq_st[(seqw[-1], kk % 2)])
                else:
                    sc.wait_ge(lwq_sem[gb % 2].h, lwd_st[seqd[kk - 2]])
                sc.activation(_sl(wqst[gb % 2], 0, nb * 128), psW[gb % 2][:, 0:nb * 128],
                              AF.Copy, bias=0.0, scale=1.0).then_inc(sv.h, 1)
    o.on("scalar", d_wd_s)

    lwe_t = {}
    lv_ch = {}
    for it in range(NIC):
        for wi in range(2):
            lwe_t[(it, wi)] = lv0.inc(16)
        if it == 3:
            lv_ch[0] = lv0.inc(64)
    tv_gp = {}
    for it in range(NIC):
        for pr in range(2):
            tv_gp[(it, pr, 0)] = tv.inc()
            tv_gp[(it, pr, 1)] = tv.inc()
    sv_gu = {}
    for it in range(NIC):
        for sc4 in range(4):
            sv_gu[(it, sc4)] = sv.inc()
    VT_SCOL = vt.inc()
    vt_gu = {}
    for it in range(NIC):
        for sc4 in range(4):
            vt_gu[(it, sc4)] = vt.inc()
    lh_sp = {it: lsc.inc(16) for it in range(NIC)}

    def e_wq_ld(gp):
        for it in range(NIC):
            for wi in range(2):
                gp.wait_ge(lwq.h, lwq_st[((1, it), 0)])
                gp.wait_ge(lwq2.h, lwq_st[((1, it), 1)])
                if it >= 2 and wi == 0:
                    gp.wait_ge(vt.h, vt_gu[(it - 2, 3)])
                gp.dma_start(out=wqe[2 * (it % 2) + wi],
                             in_=bass.AP(wq_hbm, (wi * NIC + it) * (NDC * 128 * 128),
                                         [[NDC * 128, 128], [1, NDC * 128]])).then_inc(lv0.h, 16)
            if it == 3:
                gp.wait_ge(ccA.h, 2)
                gp.wait_ge(vt.h, vt_xq[NDC - 1])
                gp.wait_ge(tv.h, TV_DQL)
                for ubit in range(2):
                    gp.dma_start(
                        out=bass.AP(kbuf[0].tensor, kbuf[0].offset + ubit * 1024,
                                    [[12288, 128], [128, 8], [1, 128]]),
                        in_=bass.AP(agAk_out, ubit * 128,
                                    [[ROWS, 128], [KPAY, 8], [1, 128]]),
                    ).then_inc(lv0.h, 16)
                for ubit in range(2):
                    gp.dma_start(
                        out=bass.AP(vbuf[0].tensor, vbuf[0].offset + ubit * 8192,
                                    [[32768, 128], [1024, 8], [1, 1024]]),
                        in_=bass.AP(agAv_out[0], ubit * (128 * D),
                                    [[D, 128], [VQ, 8], [1, D]]),
                    ).then_inc(lv0.h, 16)
    o.on("gpsimd", e_wq_ld)

    _eset = [
        [psS[0][:, 0:512], psS[1][:, 0:512], psA[0][:, 0:512], psA[0][:, 512:1024]],
        [psW[0][:, 0:512], psW[1][:, 0:512], psA[1][:, 0:512], psA[1][:, 512:1024]],
    ]

    def e_gu_t(te):
        te.wait_ge(vt.h, vt_xq[NDC - 1])
        te.wait_ge(sv.h, sv_v[NU - 1])
        for it in range(NIC):
            te.wait_ge(lv0.h, lwe_t[(it, 1)])
            for pr in range(2):
                pi = it * 2 + pr
                bs = _eset[pi % 2]
                if pi == 1:
                    te.wait_ge(sv.h, SV_DCL)
                if pi >= 2:
                    pit, ppr = divmod(pi - 2, 2)
                    te.wait_ge(vt.h, vt_gu[(pit, 2 * ppr + 1)])
                for wi in range(2):
                    for dc in range(NDC):
                        te.matmul(bs[2 * wi + 0], _sl(wqe[2 * (it % 2) + wi], dc * 128, 128),
                                  _sl(xintT, dc * ROWS + (2 * pr) * 512, 512),
                                  start=(dc == 0), stop=(dc == NDC - 1))
                        inst = te.matmul(bs[2 * wi + 1],
                                         _sl(wqe[2 * (it % 2) + wi], dc * 128, 128),
                                         _sl(xintT, dc * ROWS + (2 * pr + 1) * 512, 512),
                                         start=(dc == 0), stop=(dc == NDC - 1))
                    inst.then_inc(tv.h, 1)
    o.on("tensor", e_gu_t)

    def e_scol_v(v):
        v.wait_ge(lg.h, LG_BCX)
        v.drain()
        v.tensor_scalar(scol_g[:], rws_g[:], bc_x[:, 0:1], None, ALU.mult)
        v.tensor_scalar(scol_u[:], rws_u[:], bc_x[:, 0:1], None, ALU.mult)
        v.drain()
        for it in range(NIC):
            v.tensor_copy(scg_t[it][:], scol_g[:, it:it + 1])
        v.drain()
        v.nop().then_inc(vt.h, 1)
    o.on("vector", e_scol_v)

    def e_gu_s(sc):
        sc.wait_ge(vt.h, VT_SCOL)
        for it in range(NIC):
            for sc4 in range(4):
                pr, j = divmod(sc4, 2)
                sc.wait_ge(tv.h, tv_gp[(it, pr, 0)])
                kk = it * 4 + sc4
                if kk >= 1:
                    pit, psc = divmod(kk - 1, 4)
                    sc.wait_ge(vt.h, vt_gu[(pit, psc)])
                sc.activation(tmpB[:, 512:1024], _eset[(it * 2 + pr) % 2][j], AF.Silu,
                              bias=0.0, scale=scg_t[it][:]).then_inc(sv.h, 1)
    o.on("scalar", e_gu_s)

    def e_gu_v(v):
        v.wait_ge(tv.h, TV_DQL)
        for it in range(NIC):
            for sc4 in range(4):
                pr, j = divmod(sc4, 2)
                v.wait_ge(sv.h, sv_gu[(it, sc4)])
                v.wait_ge(tv.h, tv_gp[(it, pr, 1)])
                if it >= 2 and sc4 == 0:
                    v.wait_ge(lsc.h, lh_sp[it - 2])
                dst = _sl(hacc[it % 2], sc4 * 512, 512)
                v.scalar_tensor_tensor(dst, _eset[(it * 2 + pr) % 2][2 + j],
                                       scol_u[:, it:it + 1], tmpB[:, 512:1024],
                                       ALU.mult, ALU.mult)
                v.tensor_reduce(amh88[:, it * 4 + sc4:it * 4 + sc4 + 1], dst, AX.X, ALU.max,
                                apply_absolute_value=True).then_inc(vt.h, 1)
    o.on("vector", e_gu_v)

    def e_h_sp(sy):
        for it in range(NIC):
            sy.wait_ge(vt.h, vt_gu[(it, 3)])
            dst = bass.AP(h_spill, it * 128 * ROWS, [[ROWS, 128], [1, ROWS]])
            sy.dma_start(out=dst, in_=hacc[it % 2]).then_inc(lsc.h, 16)
    o.on("sync", e_h_sp)

    VT_AMHR = vt.inc()

    def e_amh_v(v):
        v.drain()
        v.tensor_reduce(amh_t[:], amh88[:, 0:NIC * 4], AX.X, ALU.max).then_inc(vt.h, 1)
    o.on("vector", e_amh_v)

    TV_AMH = tv.inc()

    def e_hm_t(te):
        te.wait_ge(vt.h, VT_AMHR)
        te.transpose(psS[1][0:1, 0:128], amh_t[:], ident[:]).then_inc(tv.h, 1)
    o.on("tensor", e_hm_t)

    VT_AMH = vt.inc()

    def e_hm_v(v):
        v.wait_ge(tv.h, TV_AMH)
        v.tensor_copy(tmpA[0:1, 0:128], psS[1][0:1, 0:128])
        v.drain()
        v.tensor_reduce(agst[0:1, 16:17], tmpA[0:1, 0:128], AX.X, ALU.max,
                        apply_absolute_value=True).then_inc(vt.h, 1)
    o.on("vector", e_hm_v)

    LG_AGC = lg.inc(16)
    ccC.inc()

    def e_hm_g(gp):
        gp.wait_ge(vt.h, VT_AMH)
        gp.dma_start(out=bass.AP(agC_in, 0, [[16, 1], [1, 16]]),
                     in_=agst[0:1, 16:32]).then_inc(lg.h, 16)
        gp.wait_ge(lg.h, LG_AGC)
        gp.collective_compute("AllGather", ALU.bypass, replica_groups=[list(range(NCORE))],
                              ins=[agC_in.ap().opt()], outs=[agC_out.ap().opt()]).then_inc(ccC.h, 1)
    o.on("gpsimd", e_hm_g)

    sched = attn_schedule()

    groups = []
    pair_last_gi = {}
    _gi = 0
    for (g2, u, items) in sched:
        for it_ in items:
            kind, jt0, p0, n = it_
            groups.append(dict(kind=kind, jt0=jt0, p0=p0, n=n, g2=g2, u=u))
            _gi += 1
        pair_last_gi[(g2, u)] = _gi - 1
    NG = len(groups)

    NPAIR = len(sched)
    QBASE = [2 + 17 * sq for sq in range(4)]
    slot_te = [[] for _ in range(NPAIR)]
    slot_sc = [[] for _ in range(NPAIR)]
    slot_v = [[] for _ in range(NPAIR)]
    slot_sy = [[] for _ in range(NPAIR)]
    slot_v[1].append(("rh",))
    slot_v[2].append(("scd",))
    for sq in range(4):
        b = QBASE[sq]
        for it in range(NIC):
            slot_v[max(b + it // 3 - 1, 1)].append(("q", sq, it))
            slot_sy[max(b + it // 3 - 2, 1)].append(("hrl", sq, it))
        for hd in range(NDC):
            kk = sq * NDC + hd
            slot_te[b + 8 + hd].append(("dn", kk))
            slot_sc[b + 8 + hd].append(("scale", kk))
            slot_te[b + 9 + hd].append(("tr", kk))
            slot_v[b + 8 + hd].append(("add", kk))
            slot_sy[b + 7 + hd].append(("wd", kk))

    for i2, (g2, u, items) in enumerate(sched):
        for k, it_ in enumerate(items):
            groups[sum(len(s[2]) for s in sched[:i2]) + k]["i2"] = i2
            groups[sum(len(s[2]) for s in sched[:i2]) + k]["k"] = k
    T_SEQ = []
    S_SEQ = []
    V_SEQ = []
    pend_v = []
    _gi = 0
    backlog = []
    for i2, (g2, u, items) in enumerate(sched):
        for k, it_ in enumerate(items):
            S_SEQ.append(("erf", _gi))
            if it_[0] == "zone":
                V_SEQ.append(("ztmp", _gi))
            V_SEQ.append(("az", _gi))
            if k == 0:
                V_SEQ.extend(pend_v)
                pend_v = []
            T_SEQ.append(("sims", _gi))
            backlog.append(_gi)
            if len(backlog) > 2:
                T_SEQ.append(("pv", backlog.pop(0)))
            if k == 1:
                T_SEQ.extend(slot_te[i2])
            _gi += 1
        for op in slot_v[i2]:
            if op[0] == "add":
                V_SEQ.append(op)
        S_SEQ.extend(slot_sc[i2])
        pend_v.append(("acc", i2))
        pend_v.extend([op for op in slot_v[i2] if op[0] != "add"])
    while backlog:
        T_SEQ.append(("pv", backlog.pop(0)))
    V_SEQ.extend(pend_v)
    pair_of = {i2: (g2, u) for i2, (g2, u, _it) in enumerate(sched)}

    SV_F0 = sv.n
    VT_F0 = vt.n
    tv_sim = {}
    tv_av = {}
    tv_dn = {}
    tv_tr2 = {}
    sv_erf = {}
    sv_scale = {}
    vt_zb = {}
    vt_att = {}
    vt_acc = {}
    vt_add = {}
    vt_hq = {}
    VT_RH = VT_SCD = None
    for op in T_SEQ:
        if op[0] == "sims":
            tv_sim[op[1]] = tv.inc()
        elif op[0] == "pv":
            tv_av[op[1]] = tv.inc()
        elif op[0] == "dn":
            tv_dn[op[1]] = tv.inc()
        else:
            tv_tr2[op[1]] = tv.inc()
    for op in S_SEQ:
        if op[0] == "erf":
            sv_erf[op[1]] = sv.inc()
        else:
            sv_scale[op[1]] = sv.inc()
    for op in V_SEQ:
        if op[0] == "ztmp":
            vt_zb[op[1]] = vt.inc()
        elif op[0] == "az":
            vt_att[op[1]] = vt.inc()
        elif op[0] == "acc":
            vt_acc[pair_of[op[1]]] = vt.inc()
        elif op[0] == "rh":
            VT_RH = vt.inc()
        elif op[0] == "scd":
            VT_SCD = vt.inc()
        elif op[0] == "q":
            vt_hq[(op[1], op[2])] = vt.inc()
        else:
            vt_add[op[1]] = vt.inc()
    LK_MX8 = lk.inc(16)
    LK_SCR4 = lk.inc(16)
    LK_BCH = lk.inc(16)
    lhr = {}
    lk_wd = {}
    for i2 in range(NPAIR):
        for op in slot_sy[i2]:
            if op[0] == "hrl":
                lhr[(op[1], op[2])] = (lw if op[2] % 2 == 0 else lw2).inc(16)
            else:
                lk_wd[op[1]] = lk.inc(16)

    ps_free = {}
    for gidx in range(NG):
        gj = gidx - 3
        if gj < 0:
            ps_free[gidx] = ("sv", SV_F0)
        elif groups[gj]["kind"] == "far":
            ps_free[gidx] = ("sv", sv_erf[gj])
        else:
            ps_free[gidx] = ("vt", vt_zb[gj])
    pair_list = [(g2, u) for (g2, u, _) in sched]
    pa_free = {}
    for i2, pk in enumerate(pair_list):
        same = [p for p in pair_list[:i2] if p[1] % 2 == pk[1] % 2]
        pa_free[pk] = vt_acc[same[-1]] if same else VT_F0

    for g2 in range(1, 8):
        semv = lv1 if (g2 % 2) else lv0
        lv_ch[g2] = semv.inc(64)

    def f_v_g(gp):
        for g2 in range(1, 8):
            semv = lv1 if (g2 % 2) else lv0
            if g2 == 1:
                gp.wait_ge(tv.h, tv_gp[(NIC - 1, 1, 1)])
            if g2 >= 2:
                gp.wait_ge(vt.h, vt_acc[(g2 - 2, 15)])
            gp.wait_ge(ccA.h, 2 + g2 // 2)
            for ubit in range(2):
                gp.dma_start(
                    out=bass.AP(kbuf[g2 % 2].tensor, kbuf[g2 % 2].offset + ubit * 1024,
                                [[12288, 128], [128, 8], [1, 128]]),
                    in_=bass.AP(agAk_out, (2 * g2 + ubit) * 128,
                                [[ROWS, 128], [KPAY, 8], [1, 128]]),
                ).then_inc(semv.h, 16)
            for ubit in range(2):
                t = 2 * g2 + ubit
                gp.dma_start(
                    out=bass.AP(vbuf[g2 % 2].tensor, vbuf[g2 % 2].offset + ubit * 8192,
                                [[32768, 128], [1024, 8], [1, 1024]]),
                    in_=bass.AP(agAv_out[t // 4], (t % 4) * (128 * D),
                                [[D, 128], [VQ, 8], [1, D]]),
                ).then_inc(semv.h, 16)
    o.on("gpsimd", f_v_g)

    def fg_t(te):
        for op in T_SEQ:
            if op[0] == "sims":
                gidx = op[1]
                g = groups[gidx]
                g2, u, jt0, n = g["g2"], g["u"], g["jt0"], g["n"]
                if g["k"] == 0:
                    semv = lv1 if (g2 % 2) else lv0
                    te.wait_ge(semv.h, lv_ch[g2])
                w, val = ps_free[gidx]
                te.wait_ge((sv if w == "sv" else vt).h, val)
                for qq in range(n):
                    jt = jt0 + qq
                    inst = te.matmul(psSb[gidx % 3][:, qq * 128:(qq + 1) * 128],
                                     _sl(kbuf[g2 % 2], jt * 128, 128),
                                     q_bf[:, u * 128:(u + 1) * 128], start=True, stop=True)
                inst.then_inc(tv.h, 1)
            elif op[0] == "pv":
                gidx = op[1]
                g = groups[gidx]
                g2, u, jt0, n = g["g2"], g["u"], g["jt0"], g["n"]
                if g["k"] == 0:
                    te.wait_ge(vt.h, pa_free[(g2, u)])
                te.wait_ge(vt.h, vt_att[gidx])
                first_mm = (g["k"] == 0)
                for qq in range(n):
                    jt = jt0 + qq
                    for half in range(2):
                        inst = te.matmul(psA[u % 2][:, half * 512:half * 512 + 512],
                                         _sl(az[gidx % 3], qq * 128, 128),
                                         _sl(vbuf[g2 % 2], jt * 1024 + half * 512, 512),
                                         start=first_mm, stop=False, skip_group_check=True)
                    first_mm = False
                inst.then_inc(tv.h, 1)
            elif op[0] == "dn":
                kk = op[1]
                sq, hd = divmod(kk, NDC)
                te.wait_ge(lk.h, lk_wd[kk])
                te.wait_ge(vt.h, vt_hq[(sq, NIC - 1)])
                if kk == 0:
                    te.wait_ge(sv.h, SV_F0)
                else:
                    te.wait_ge(vt.h, vt_add[kk - 1])
                for it in range(NIC):
                    inst = te.matmul(psG[:, 0:512], _sl(wdb_s, it * 128, 128),
                                     _sl(hq, it * 512, 512),
                                     start=(it == 0), stop=(it == NIC - 1),
                                     skip_group_check=True)
                inst.then_inc(tv.h, 1)
            else:
                kk = op[1]
                te.wait_ge(sv.h, sv_scale[kk])
                for b in range(4):
                    inst = te.transpose(psG[:, b * 128:(b + 1) * 128],
                                        bass.AP(tmpA, b * 128, [[D, 128], [1, 128]]),
                                        ident[:])
                inst.then_inc(tv.h, 1)
    o.on("tensor", fg_t)

    def fg_s(sc):
        for op in S_SEQ:
            if op[0] == "erf":
                gidx = op[1]
                g = groups[gidx]
                n = g["n"]
                if gidx >= 3:
                    sc.wait_ge(vt.h, vt_att[gidx - 3])
                if g["kind"] == "zone":
                    sc.wait_ge(vt.h, vt_zb[gidx])
                    sc.activation(_sl(ez[gidx % 3], 0, n * 128), _sl(ztmp[gidx % 3], 0, n * 128),
                                  AF.Erf, bias=0.0, scale=1.0).then_inc(sv.h, 1)
                else:
                    sc.wait_ge(tv.h, tv_sim[gidx])
                    sc.activation(_sl(ez[gidx % 3], 0, n * 128), psSb[gidx % 3][:, 0:n * 128],
                                  AF.Erf, bias=farc_t[:],
                                  scale=S_IMM).then_inc(sv.h, 1)
            else:
                kk = op[1]
                hd = kk % NDC
                sc.wait_ge(tv.h, tv_dn[kk])
                sc.activation(foT, psG[:, 0:512], AF.Copy, bias=0.0,
                              scale=scd_t[hd][:]).then_inc(sv.h, 1)
    o.on("scalar", fg_s)

    def fg_v(v):
        v.wait_ge(lg.h, LG_HM)
        v.wait_ge(tv.h, tv_gp[(NIC - 1, 1, 1)])
        v.wait_ge(sv.h, SV_F0)
        for op in V_SEQ:
            if op[0] == "ztmp":
                gidx = op[1]
                g = groups[gidx]
                n, p0 = g["n"], g["p0"]
                v.wait_ge(tv.h, tv_sim[gidx])
                if gidx >= 3:
                    v.wait_ge(sv.h, sv_erf[gidx - 3])
                v.scalar_tensor_tensor(_sl(ztmp[gidx % 3], 0, n * 128),
                                       psSb[gidx % 3][:, 0:n * 128], S_IMM,
                                       zbias[:, p0 * 128:(p0 + n) * 128],
                                       ALU.mult, ALU.add).then_inc(vt.h, 1)
            elif op[0] == "az":
                gidx = op[1]
                g = groups[gidx]
                n, p0 = g["n"], g["p0"]
                v.wait_ge(sv.h, sv_erf[gidx])
                if gidx >= 3:
                    v.wait_ge(tv.h, tv_av[gidx - 3])
                if g["kind"] == "zone":
                    v.scalar_tensor_tensor(_sl(az[gidx % 3], 0, n * 128),
                                           _sl(ez[gidx % 3], 0, n * 128), 1.0,
                                           _sl(hmask, p0 * 128, n * 128),
                                           ALU.add, ALU.mult).then_inc(vt.h, 1)
                else:
                    v.tensor_scalar(_sl(az[gidx % 3], 0, n * 128), _sl(ez[gidx % 3], 0, n * 128),
                                    0.5, 0.5, ALU.mult, ALU.add).then_inc(vt.h, 1)
            elif op[0] == "acc":
                g2, u = pair_of[op[1]]
                v.wait_ge(tv.h, tv_av[pair_last_gi[(g2, u)]])
                v.tensor_tensor(acc[:, u * D:(u + 1) * D], acc[:, u * D:(u + 1) * D],
                                psA[u % 2][:, 0:1024], ALU.add).then_inc(vt.h, 1)
            elif op[0] == "rh":
                v.wait_ge(lk.h, LK_MX8)
                v.tensor_reduce(smalls[0:1, 5:6], mx8[0:1, 8:16], AX.X, ALU.max)
                v.drain()
                v.tensor_tensor(smalls[0:1, 6:7], smalls[0:1, 11:12],
                                smalls[0:1, 5:6], ALU.mult)
                v.drain()
                v.tensor_scalar(smalls[0:1, 6:7], smalls[0:1, 6:7],
                                1.0 / 127.0, None, ALU.mult)
                v.drain()
                v.reciprocal(smalls[0:1, 7:8], smalls[0:1, 6:7])
                v.drain()
                v.nop().then_inc(vt.h, 1)
            elif op[0] == "scd":
                v.wait_ge(lk.h, LK_BCH)
                v.drain()
                v.tensor_scalar(scol_d[:], rws_d[:], bc_h[:, 0:1], None, ALU.mult)
                v.drain()
                for hd in range(NDC):
                    v.tensor_copy(scd_t[hd][:], scol_d[:, hd:hd + 1])
                v.drain()
                v.nop().then_inc(vt.h, 1)
            elif op[0] == "q":
                sq, it = op[1], op[2]
                if sq == 0 and it == 0:
                    v.wait_ge(tv.h, TV_DQL)
                    v.wait_ge(lsc.h, lh_sp[NIC - 1])
                if sq > 0 and it == 0:
                    v.wait_ge(tv.h, tv_dn[sq * NDC - 1])
                v.wait_ge((lw if it % 2 == 0 else lw2).h, lhr[(sq, it)])
                v.tensor_scalar(qtmp, _sl(hrl[it % 2], 0, 512), bc_h[:, 1:2],
                                127.0, ALU.mult, ALU.min)
                v.tensor_scalar(qtmp, qtmp, -128.0,
                                cols[:, CP:CP + 1], ALU.max, ALU.add)
                v.tensor_scalar(_sl(hq, it * 512, 512), qtmp,
                                cols[:, CP:CP + 1], None, ALU.subtract).then_inc(vt.h, 1)
            else:
                kk = op[1]
                sq, hd = divmod(kk, NDC)
                v.wait_ge(tv.h, tv_tr2[kk])
                base = sq * 4 * D + hd * 128
                dst = bass.AP(acc, base, [[NU * D, 128], [D, 4], [1, 128]])
                v.tensor_tensor(dst, dst, psG[:, 0:512], ALU.add).then_inc(vt.h, 1)
    o.on("vector", fg_v)

    hrl_order = [(op[1], op[2]) for i2 in range(NPAIR)
                 for op in slot_sy[i2] if op[0] == "hrl"]

    def g_sync(sy):
        sy.wait_ge(lwq.h, LWQ_ALL0)
        sy.wait_ge(lwq2.h, LWQ_ALL1)
        sy.wait_ge(ccC.h, 1)
        sy.dma_start(out=mx8[0:1, 8:16],
                     in_=bass.AP(agC_out, 0, [[1, 1], [16, 8]]),
                     allow_slow_non_contiguous=True).then_inc(lk.h, 16)
        sy.wait_ge(vt.h, VT_RH)
        sy.dma_start(out=bass.AP(scr4, 4, [[4, 1], [1, 2]]),
                     in_=smalls[0:1, 6:8]).then_inc(lk.h, 16)
        sy.wait_ge(lk.h, LK_SCR4)
        sy.dma_start(out=bc_h[:], in_=bass.AP(scr4, 4, [[0, 128], [1, 2]])).then_inc(lk.h, 16)
        nhrl = 0
        for i2 in range(NPAIR):
            for op in slot_sy[i2]:
                if op[0] == "hrl":
                    sq, it = op[1], op[2]
                    if nhrl >= 2:
                        sy.wait_ge(vt.h, vt_hq[hrl_order[nhrl - 2]])
                    sy.wait_ge(lsc.h, lh_sp[it])
                    src_ = bass.AP(h_spill, it * 128 * ROWS + sq * 512,
                                   [[ROWS, 128], [1, 512]])
                    sy.dma_start(out=_sl(hrl[it % 2], 0, 512),
                                 in_=src_).then_inc((lw if it % 2 == 0 else lw2).h, 16)
                    nhrl += 1
                else:
                    kk = op[1]
                    sq, hd = divmod(kk, NDC)
                    if kk > 0:
                        sy.wait_ge(tv.h, tv_dn[kk - 1])
                    sy.dma_start(out=wdb_s,
                                 in_=bass.AP(wdT_hbm, hd * (NIC * 128 * 128),
                                             [[NIC * 128, 128], [1, NIC * 128]])
                                 ).then_inc(lk.h, 16)
    o.on("sync", g_sync)

    ST_ALL = st.inc(64 + (32 if DBG else 0))
    DBS_ALL = dbs.inc(112) if DBG else 0

    def g_out(sy):
        for sq in range(4):
            tgt = vt_add[sq * NDC + NDC - 1]
            for u in range(sq * 4, sq * 4 + 4):
                tgt = max(tgt, vt_acc[(u // 2, u)])
            sy.wait_ge(vt.h, tgt)
            src_ = bass.AP(acc, sq * 4 * D, [[NU * D, 128], [D, 4], [1, D]])
            dst = bass.AP(out_ext, sq * 4 * 128 * D, [[D, 128], [128 * D, 4], [1, D]])
            sy.dma_start(out=dst, in_=src_).then_inc(st.h, 16)
        sy.wait_ge(st.h, ST_ALL)
        if DBG:
            sy.wait_ge(dbs.h, DBS_ALL)
    o.on("sync", g_out)

    blk = nc.Block()
    with blk:
        o.emit(blk)
    return nc


_NC_CACHE = {}


def _get_nc(stage="all"):
    if stage not in _NC_CACHE:
        _NC_CACHE[stage] = build_program(stage)
    return _NC_CACHE[stage]


def _bv33(bv):
    b = np.zeros((33, 512), dtype=np.float32)
    v = np.asarray(bv, np.float32).reshape(1024)
    b[0] = v[:512]
    b[32] = v[512:]
    return b


def make_in_maps(inputs):
    x = np.ascontiguousarray(np.asarray(inputs["L0_attn_x"], np.float32).reshape(S, D))
    consts = host_constants()
    gm = np.asarray(inputs["L0_attn_gamma"], np.float32)
    bt = np.asarray(inputs["L0_attn_beta"], np.float32)
    shared = {
        "wqk": np.ascontiguousarray(np.asarray(inputs["L0_attn_Wqk"], np.float32)),
        "bqk": np.ascontiguousarray(np.asarray(inputs["L0_attn_bqk"], np.float32).reshape(E, 1)),
        "g0": np.ascontiguousarray(gm[0].reshape(E, 1)),
        "b0": np.ascontiguousarray(bt[0].reshape(E, 1)),
        "g1": np.ascontiguousarray(gm[1].reshape(E, 1)),
        "b1": np.ascontiguousarray(bt[1].reshape(E, 1)),
        "wv": np.ascontiguousarray(np.asarray(inputs["L0_attn_Wv"], np.float32)),
        "bv": _bv33(inputs["L0_attn_bv"]),
        "rel": np.ascontiguousarray(np.asarray(inputs["L0_attn_rel_emb"], np.float32).reshape(32, 1)),
        "clipx": np.ascontiguousarray(np.asarray(inputs["L0_ffn_clip_x"], np.float32).reshape(1, 1)),
        "clipd": np.ascontiguousarray(np.asarray(inputs["L0_ffn_clip_down"], np.float32).reshape(1, 1)),
        "wgate": np.ascontiguousarray(np.asarray(inputs["L0_ffn_w_gate"], np.float32)),
        "wup": np.ascontiguousarray(np.asarray(inputs["L0_ffn_w_up"], np.float32)),
        "wdown": np.ascontiguousarray(np.asarray(inputs["L0_ffn_w_down"], np.float32)),
    }
    shared.update(consts)
    in_maps = []
    for c in range(NCORE):
        tiles = [x[t * 128:(t + 1) * 128] for t in range(128) if t % 8 == c]
        zs, hm = percore_tables(c)
        m = dict(shared)
        m["x_loc"] = np.ascontiguousarray(np.concatenate(tiles, axis=0))
        m["zselc"] = zs
        m["hmaskc"] = hm.astype(ml_dtypes.bfloat16)
        in_maps.append(m)
    return in_maps


def kernel(**inputs):
    nc = _get_nc("all")
    in_maps = make_in_maps(inputs)
    res = run_bass_kernel_spmd(nc, in_maps, core_ids=list(range(NCORE)))
    out = np.empty((S, D), dtype=np.float32)
    for c in range(NCORE):
        ol = np.asarray(res.results[c]["out_loc"])
        for u in range(NU):
            out[(8 * u + c) * 128:(8 * u + c + 1) * 128] = ol[u * 128:(u + 1) * 128]
    return out.reshape(1, S, D)


if __name__ == "__main__":
    nc = _get_nc("all")
    try:
        print("built ok; instructions:", nc.n_instructions())
    except Exception:
        print("built ok")
